# version 02
# baseline (speedup 1.0000x reference)
import sys
sys.path.insert(0, "/opt/trn_rl_repo")
import numpy as np
from contextlib import ExitStack

import concourse.bass as bass
import concourse.mybir as mybir


"""Mini dependency-tracking emitter for raw Bass programs.

Records ops (closures) with buffer-level read/write sets, derives all
RAW/WAW/WAR semaphore edges (same-engine included -- the simulator's race
detector requires them), assigns semaphore increments only to ops that are
actually depended on, then emits everything inside one nc.Block().

Semaphores: one per compute engine; DMA-issuing engines use N_DMA_SLOTS
rotating semaphores per engine (the race detector allows only one
outstanding DMA per semaphore, so slot s's next DMA first waits for slot
s's previous DMA -> up to N_DMA_SLOTS DMAs in flight per queue).
"""

ENGINES = ("tensor", "vector", "scalar", "gpsimd", "sync")
N_DMA_SLOTS = 8


class Buf:
    __slots__ = ("name", "writer", "readers")

    def __init__(self, name=""):
        self.name = name
        self.writer = None
        self.readers = []

    def __repr__(self):
        return f"Buf({self.name})"


class Op:
    __slots__ = ("engine", "emit_fn", "deps", "inc", "cum", "needs_inc", "tag",
                 "gidx", "sem_key")

    def __repr__(self):
        return f"Op({self.engine}:{self.tag}@{self.gidx})"


class Prog:
    def __init__(self):
        self.ops = {e: [] for e in ENGINES}
        self.n = 0
        self.final_waits = []
        self._dma_count = {e: 0 for e in ENGINES}
        self._last_dma_on_slot = {}

    def buf(self, name=""):
        return Buf(name)

    def op(self, engine, emit_fn, reads=(), writes=(), dma=False, deps=(), tag=""):
        assert engine in ENGINES, engine
        o = Op()
        o.engine = engine
        o.emit_fn = emit_fn
        o.inc = 16 if dma else 1
        o.tag = tag
        o.gidx = self.n
        o.needs_inc = False
        self.n += 1
        if dma:
            slot = self._dma_count[engine] % N_DMA_SLOTS
            self._dma_count[engine] += 1
            o.sem_key = (engine, slot)
        else:
            o.sem_key = engine
        dep = set(deps)
        for b in reads:
            if b.writer is not None:
                dep.add(b.writer)
        for b in writes:
            if b.writer is not None:
                dep.add(b.writer)
            dep.update(b.readers)
        if dma:
            prev = self._last_dma_on_slot.get(o.sem_key)
            if prev is not None:
                dep.add(prev)
            self._last_dma_on_slot[o.sem_key] = o
        dep.discard(o)
        o.deps = dep
        for b in reads:
            b.readers.append(o)
        for b in writes:
            b.writer = o
            b.readers = []
        self.ops[engine].append(o)
        return o

    def wait(self, engine, op):
        self.final_waits.append((engine, op))

    def emit(self, nc):
        keys = set()
        for e in ENGINES:
            for o in self.ops[e]:
                keys.add(o.sem_key)
        sems = {}
        ctxs = []
        for k in sorted(keys, key=str):
            name = f"sem_{k}" if isinstance(k, str) else f"sem_{k[0]}_{k[1]}"
            ctx = nc.semaphore(name)
            sems[k] = ctx.__enter__()
            ctxs.append(ctx)
        for e in ENGINES:
            for o in self.ops[e]:
                for d in o.deps:
                    d.needs_inc = True
        for fe, fo in self.final_waits:
            fo.needs_inc = True
        cum = {k: 0 for k in keys}
        for e in ENGINES:
            for o in self.ops[e]:
                if o.needs_inc:
                    cum[o.sem_key] += o.inc
                o.cum = cum[o.sem_key]
        stats = {e: {"ops": len(self.ops[e]),
                     "incs": sum(1 for o in self.ops[e] if o.needs_inc),
                     "waits": 0} for e in ENGINES}
        block_ctx = nc.Block()
        block = block_ctx.__enter__()
        engines_used = [e for e in ENGINES
                        if self.ops[e] or any(fe == e for fe, _ in self.final_waits)]
        for e in engines_used:
            def make_body(e):
                def body(eng):
                    waited = {k: 0 for k in keys}
                    for o in self.ops[e]:
                        need = {}
                        for d in o.deps:
                            if d.cum > need.get(d.sem_key, 0):
                                need[d.sem_key] = d.cum
                        for k in sorted(need, key=str):
                            v = need[k]
                            if v > waited[k]:
                                eng.wait_ge(sems[k], v)
                                waited[k] = v
                                stats[e]["waits"] += 1
                        inst = o.emit_fn(eng)
                        if o.needs_inc:
                            inst.then_inc(sems[o.sem_key], o.inc)
                    for fe, fo in self.final_waits:
                        if fe == e and fo.cum > waited[fo.sem_key]:
                            eng.wait_ge(sems[fo.sem_key], fo.cum)
                            waited[fo.sem_key] = fo.cum
                return body

            getattr(block, e)(make_body(e))
        block_ctx.__exit__(None, None, None)
        for ctx in reversed(ctxs):
            ctx.__exit__(None, None, None)
        return stats




T, D, H, HD, INTER = 8192, 4096, 4, 1024, 11008
NCORE = 8
CH = 256
NCH = T // CH
DCH = D // 128
MK = 128
PAD_SPAN = [16, 32, 48, 64]
NSLOT = sum(PAD_SPAN)
EPS = 1e-5
R_RUN = 4
TOK = CH * 4


def sigma_of_core(i):
    return (i, 7 - i, i, 7 - i)


def chunks_of_core(i):
    s = sigma_of_core(i)
    return [8 * q + s[q] for q in range(4)]


def slot_sched():
    return [[(q, m) for m in range(PAD_SPAN[q])] for q in range(4)]


def perm_cols(nk):
    ks = []
    nrow = nk // 8
    for R in range(0, nrow, 128):
        rows = min(128, nrow - R)
        for j in range(8):
            for r in range(rows):
                ks.append(8 * (R + r) + j)
    return np.array(ks, dtype=np.int64)


def perm_chunk_list(nk):
    out = []
    pos = 0
    nrow = nk // 8
    for R in range(0, nrow, 128):
        rows = min(128, nrow - R)
        for j in range(8):
            out.append((R, rows, j, pos))
            pos += rows
    return out


def s_rep_host(scales, nk):
    nrow = nk // 8
    idx = (8 * np.arange(nrow)) // 128
    return np.asarray(scales, dtype=np.float32)[idx]


def host_core_inputs(i, inputs):
    import ml_dtypes
    x = np.ascontiguousarray(np.asarray(inputs["L0_attn_x"], dtype=np.float32).reshape(T, D))
    chunks = chunks_of_core(i)
    x_own = np.concatenate([x[c * CH:(c + 1) * CH] for c in chunks], axis=0)
    masks = np.zeros((NSLOT, 128, CH), dtype=np.float32)
    k = 0
    for q in range(4):
        c = chunks[q]
        q_glob = c * CH + np.arange(CH)[None, :]
        for m in range(PAD_SPAN[q]):
            s_glob = m * MK + np.arange(MK)[:, None]
            masks[k] = (s_glob <= q_glob)
            k += 1
    masks = masks.astype(ml_dtypes.bfloat16)

    pd = perm_cols(INTER)
    m_ = {
        "x_full": x,
        "x_own": x_own,
        "masks": masks,
        "wmod": np.asarray(inputs["L0_wmod"], dtype=np.float32),
        "cvec": np.asarray(inputs["L0_c"], dtype=np.float32),
        "bv_q": np.asarray(inputs["L0_attn_bv_q"], dtype=np.float32),
        "bv_k": np.asarray(inputs["L0_attn_bv_k"], dtype=np.float32),
        "bv_v": np.asarray(inputs["L0_attn_bv_v"], dtype=np.float32),
        "qw_g": np.asarray(inputs["L0_ffn_gate_qweight"], dtype=np.int32),
        "qw_u": np.asarray(inputs["L0_ffn_up_qweight"], dtype=np.int32),
        "qw_d": np.asarray(inputs["L0_ffn_down_qweight"], dtype=np.int32),
        "qz_g": np.asarray(inputs["L0_ffn_gate_qzeros"], dtype=np.int32),
        "qz_u": np.asarray(inputs["L0_ffn_up_qzeros"], dtype=np.int32),
        "qz_d": np.asarray(inputs["L0_ffn_down_qzeros"], dtype=np.int32),
        "srep_g": s_rep_host(inputs["L0_ffn_gate_scales"], D)[:, pd],
        "srep_u": s_rep_host(inputs["L0_ffn_up_scales"], D)[:, pd],
        "srep_d": s_rep_host(inputs["L0_ffn_down_scales"], INTER),
        "sc_g": np.asarray(inputs["L0_ffn_gate_scales"], dtype=np.float32)[:, pd],
        "sc_u": np.asarray(inputs["L0_ffn_up_scales"], dtype=np.float32)[:, pd],
        "sc_d": np.asarray(inputs["L0_ffn_down_scales"], dtype=np.float32),
    }
    return {k2: np.ascontiguousarray(v) for k2, v in m_.items()}


def out_assemble(results, dtype=np.float32):
    full = np.empty((T, D), dtype=dtype)
    for i in range(NCORE):
        chunks = chunks_of_core(i)
        r = np.asarray(results[i]).reshape(TOK, D)
        for q, c in enumerate(chunks):
            full[c * CH:(c + 1) * CH] = r[q * CH:(q + 1) * CH]
    return full


"""Program builder for the adaLN megakernel (v1, simple-modulation design).

Arena map (f32 units per partition, arena [128, 36864]):
  P1:  XT0 0:4096 | XT1 4096:8192 | NH 8192:10240 (bf16 view) | DQ 10400:12192
       FM 12288:14336 | WMB 14336:16384 | MROW 16384:20480
  P2:  XT0 | QPR 4096:8192 | NH 8192:10240 | AMASK 10240:10368 | DQ (tail)
       KT 12288:16384 (2 slots) | VT 16384:26624 (5 slots) | X2S 26624:34816
       AT 34816:36864
  P3:  XT0 | HT 12288:28672 | WG 28672:30720 | WU 30720:32768
       SIL 32768:33792 | VST 33792:34304 | SZT 34304:34560 | VCH 34560:34816
       (down phase: DST 28672:30720, FRG 30720:31232, X2C 31232:31744,
        WDT 33792:34304)
All cross-phase reuse is protected by phase-master barrier deps.
"""

F32 = mybir.dt.float32
BF16 = mybir.dt.bfloat16
I32 = mybir.dt.int32
AF = mybir.ActivationFunctionType
OP = mybir.AluOpType
AX = mybir.AxisListType

NMODV = 6
VG = {"sa": 0, "ca": 1, "ga": 2, "sb": 3, "cb": 4, "gb": 5,
      "u": 6, "u2": 7, "qkb": 8, "vb": 9, "gavb": 10, "tmp": 11}
ARENA_F = 36864


class Ctx:
    pass


def vcol(c, name, k=0, n=None):
    base = VG[name] * DCH
    if n is None:
        n = DCH - k if k == 0 else 1
    return c.vecs[:, base + k:base + k + n]


def ar(c, off, cols, dt=F32):
    if dt == F32 or dt == I32:
        v = c.arena[:, off:off + cols]
        return v if dt == F32 else v.bitcast(I32)
    return c.arena[:, off:off + (cols + 1) // 2].bitcast(BF16)


def declare(nc, st):
    import ml_dtypes
    c = Ctx()
    c.nc = nc
    c.p = Prog()
    e = st.enter_context
    dt_ = nc.dram_tensor

    c.x_full = dt_("x_full", [T, D], F32, kind="ExternalInput").ap()
    c.x_own = dt_("x_own", [TOK, D], F32, kind="ExternalInput").ap()
    c.masks = dt_("masks", [NSLOT, MK, CH], BF16, kind="ExternalInput").ap()
    c.wmod = dt_("wmod", [D, NMODV * D], F32, kind="ExternalInput").ap()
    c.cvec = dt_("cvec", [1, D], F32, kind="ExternalInput").ap()
    c.bv = {k: dt_("bv_" + k, [H, HD], F32, kind="ExternalInput").ap() for k in "qkv"}
    c.qw = {k: dt_("qw_" + k, [{"g": D, "u": D, "d": INTER}[k] // 8,
                               {"g": INTER, "u": INTER, "d": D}[k]], I32,
                   kind="ExternalInput").ap() for k in "gud"}
    c.qz = {k: dt_("qz_" + k, [{"g": 32, "u": 32, "d": 86}[k],
                               {"g": INTER // 8, "u": INTER // 8, "d": D // 8}[k]], I32,
                   kind="ExternalInput").ap() for k in "gud"}
    c.srep = {k: dt_("srep_" + k, [{"g": D // 8, "u": D // 8, "d": INTER // 8}[k],
                                   {"g": INTER, "u": INTER, "d": D}[k]], F32,
                     kind="ExternalInput").ap() for k in "gud"}
    c.scp = {k: dt_("sc_" + k, [{"g": 32, "u": 32, "d": 86}[k],
                                {"g": INTER, "u": INTER, "d": D}[k]], F32,
                    kind="ExternalInput").ap() for k in "gud"}
    c.out = dt_("out", [TOK, D], F32, kind="ExternalOutput").ap()

    c.ha_fm = dt_("ha_fm", [T // MK, D, MK], BF16).ap()
    c.ha_tok = dt_("ha_tok", [T, D], BF16).ap()
    c.x2_dram = dt_("x2_dram", [TOK, D], F32).ap()
    c.wdq = {k: dt_("wdq_" + k, [{"g": D, "u": D, "d": INTER}[k],
                                 {"g": INTER, "u": INTER, "d": D}[k]], BF16).ap()
             for k in "gud"}
    c.v_dram = dt_("v_dram", [INTER, TOK], BF16).ap()
    c.sz_dram = {k: dt_("szd_" + k, [{"g": 32, "u": 32, "d": 86}[k],
                                     {"g": INTER, "u": INTER, "d": D}[k]], BF16).ap()
                 for k in "gud"}

    c.identf = nc.inline_tensor(np.eye(128, dtype=np.float32), "identf").ap()
    c.identb = nc.inline_tensor(np.eye(128, dtype=np.float32).astype(ml_dtypes.bfloat16), "identb").ap()
    c.ones11 = nc.inline_tensor(np.ones((1, 1), dtype=np.float32), "ones11").ap()
    c.onesf_row = nc.inline_tensor(np.ones((1, 128), dtype=np.float32), "onesf_row").ap()
    indh = np.zeros((4, 128, 32), dtype=np.float32)
    for R in range(4):
        indh[R, np.arange(128), 8 * R + np.arange(128) // 16] = 1.0
    c.indh = nc.inline_tensor(indh.astype(ml_dtypes.bfloat16).reshape(4 * 128, 32), "indh").ap()
    indv = np.zeros((11, 128, 86), dtype=np.float32)
    for R in range(11):
        rows = 128 if R < 10 else 96
        indv[R, np.arange(rows), 8 * R + np.arange(rows) // 16] = 1.0
    c.indv = nc.inline_tensor(indv.astype(ml_dtypes.bfloat16).reshape(11 * 128, 86), "indv").ap()

    c.arena = e(nc.sbuf_tensor("arena", [128, ARENA_F], F32))
    c.cidf = e(nc.sbuf_tensor("cidf", [128, 128], F32))
    c.cidb = e(nc.sbuf_tensor("cidb", [128, 128], BF16))
    c.cones = e(nc.sbuf_tensor("cones", [1, 132], F32))
    c.cindh = e(nc.sbuf_tensor("cindh", [128, 4 * 32], BF16))
    c.cindv = e(nc.sbuf_tensor("cindv", [128, 11 * 86], BF16))
    c.vecs = e(nc.sbuf_tensor("vecs", [128, 12 * DCH], F32))
    c.bcast = e(nc.sbuf_tensor("bcast", [128, 3 * D], BF16))
    c.lnst = e(nc.sbuf_tensor("lnst", [128, 128], F32))
    c.tiny = e(nc.sbuf_tensor("tiny", [128, 64], F32))
    c.ccols = e(nc.sbuf_tensor("ccols", [128, DCH], F32))
    c.bvs = e(nc.sbuf_tensor("bvs", [128, 3 * DCH], F32))
    c.hsumh = e(nc.sbuf_tensor("hsumh", [32, 1024], BF16))
    c.hsuml = e(nc.sbuf_tensor("hsuml", [32, 1024], BF16))
    c.hsumv = e(nc.sbuf_tensor("hsumv", [86, 1024], BF16))

    c.ps_a = e(nc.psum_tensor("ps_a", [128, 1024], F32))
    c.ps_s = e(nc.psum_tensor("ps_s", [128, 1024], F32))
    c.ps_t = e(nc.psum_tensor("ps_t", [128, 512], F32))
    c.ps_u = e(nc.psum_tensor("ps_u", [128, 512], F32))
    c.ps_x = e(nc.psum_tensor("ps_x", [128, 1024], BF16))

    c.B = {}
    c.pdeps = []
    return c


def buf(c, name):
    if name not in c.B:
        c.B[name] = c.p.buf(name)
    return c.B[name]


def phase_barrier(c, name):
    old = [b for n, b in c.B.items() if n.startswith("ar_")]
    mb = c.p.op("gpsimd", lambda g: g.memset(c.tiny[0:1, 63:64], 0.0),
                reads=old, writes=[buf(c, "pb_" + name)], tag="pb_" + name)
    c.pdeps = [mb]
    return mb


def stage_consts(c):
    p = c.p
    for fn, name in [
        (lambda g: g.dma_start(out=c.cidf[:], in_=c.identf), "cidf"),
        (lambda g: g.dma_start(out=c.cidb[:], in_=c.identb), "cidb"),
        (lambda g: g.dma_start(out=c.cones[0:1, 0:128], in_=c.onesf_row), "cones"),
        (lambda g: g.dma_start(out=c.cones[0:1, 128:129], in_=c.ones11), "cone1"),
        (lambda g: g.dma_start(out=c.cindh[:].rearrange("p (r g) -> p r g", r=4),
                               in_=c.indh.rearrange("(r p) g -> p r g", p=128)), "cindh"),
        (lambda g: g.dma_start(out=c.cindv[:].rearrange("p (r g) -> p r g", r=11),
                               in_=c.indv.rearrange("(r p) g -> p r g", p=128)), "cindv"),
    ]:
        p.op("gpsimd", fn, writes=[buf(c, name)], dma=True, tag="ld_" + name)


def stage_mod(c):
    p = c.p
    b_c = buf(c, "c_row")
    b_pst = buf(c, "ps_t")
    b_ccols = buf(c, "ccols")
    b_vecs = buf(c, "vecs")
    b_tiny = buf(c, "tiny")
    b_bvs = buf(c, "bvs")
    b_one = buf(c, "cone1")
    b_idf = buf(c, "cidf")
    b_onesr = buf(c, "cones")
    ones1 = c.cones[0:1, 128:129]
    xt1 = ar(c, 4096, D)

    b_xt1 = buf(c, "ar_xt1")
    p.op("sync", lambda g: g.dma_start(out=xt1[0:1, 0:D], in_=c.cvec),
         writes=[b_xt1], dma=True, tag="ld_c")
    for k in range(DCH):
        p.op("tensor", lambda t, k=k: t.matmul(
            c.ps_t[:, k:k + 1], xt1[0:1, 128 * k:128 * (k + 1)], ones1,
            start=True, stop=True),
            reads=[b_xt1, b_one], writes=[b_pst], tag=f"ccol{k}")
    p.op("scalar", lambda s: s.activation(out=c.ccols[:, 0:DCH], in_=c.ps_t[:, 0:DCH], func=AF.Silu),
         reads=[b_pst], writes=[b_ccols], tag="silu_c")

    b_wm = [buf(c, "ar_wmb0"), buf(c, "ar_wmb1")]
    b_mrow = buf(c, "ar_mrow")
    wmb2 = [ar(c, 16000, 2048), ar(c, 18048, 2048)]
    mrow = ar(c, 20096, 4096)
    c.mod_done = {}
    psacc = [(c.ps_a, 0), (c.ps_a, 512), (c.ps_s, 0), (c.ps_s, 512)]
    mv_ps = [buf(c, f"mv_ps{i}") for i in range(4)]
    for v in range(NMODV):
        name = {0: "sa", 1: "ca", 2: "ga", 3: "sb", 4: "cb", 5: "gb"}[v]
        for blk in range(2):
            col0 = v * D + blk * 2048
            for k in range(DCH):
                hb = (v * 2 * DCH + blk * DCH + k) % 2
                p.op("sync", lambda g, k=k, col0=col0, hb=hb: g.dma_start(
                    out=wmb2[hb], in_=c.wmod[128 * k:128 * (k + 1), col0:col0 + 2048]),
                    writes=[b_wm[hb]], dma=True, tag=f"wm{v}_{blk}_{k}")
                for s4 in range(4):
                    ps, off = psacc[s4]
                    p.op("tensor", lambda t, k=k, s4=s4, ps=ps, off=off, hb=hb: t.matmul(
                        ps[0:1, off:off + 512], c.ccols[:, k:k + 1],
                        wmb2[hb][:, s4 * 512:(s4 + 1) * 512],
                        start=(k == 0), stop=(k == DCH - 1)),
                        reads=[b_wm[hb], b_ccols], writes=[mv_ps[s4]], tag=f"mv{v}_{blk}_{k}_{s4}")
            for s4 in range(4):
                ps, off = psacc[s4]
                p.op("vector", lambda vv, s4=s4, ps=ps, off=off, blk=blk: vv.tensor_copy(
                    out=mrow[0:1, blk * 2048 + s4 * 512:blk * 2048 + (s4 + 1) * 512],
                    in_=ps[0:1, off:off + 512]),
                    reads=[mv_ps[s4]], writes=[b_mrow], tag=f"mvev{v}_{blk}_{s4}")
        for half16 in range(2):
            for i in range(16):
                k = half16 * 16 + i
                p.op("tensor", lambda t, k=k, i=i: t.matmul(
                    c.ps_t[:, 32 + i:33 + i], mrow[0:1, 128 * k:128 * (k + 1)], ones1,
                    start=True, stop=True),
                    reads=[b_mrow, b_one], writes=[b_pst], tag=f"mvc{v}_{k}")
            last = p.op("vector", lambda vv, name=name, half16=half16: vv.tensor_copy(
                out=c.vecs[:, VG[name] * DCH + half16 * 16:VG[name] * DCH + half16 * 16 + 16],
                in_=c.ps_t[:, 32:48]),
                reads=[b_pst], writes=[b_vecs], tag=f"mvout{v}_{half16}")
        c.mod_done[name] = last

    for bi, kk in enumerate("qkv"):
        for h in range(H):
            p.op("gpsimd", lambda g, kk=kk, h=h: g.dma_start(
                out=xt1[0:1, 0:HD], in_=c.bv[kk][h:h + 1, :]),
                writes=[b_xt1], dma=True, tag=f"ldbv{kk}{h}")
            p.op("vector", lambda vv, bi=bi, h=h: vv.reduce_sum(
                out=c.tiny[0:1, 8 + bi * 4 + h:9 + bi * 4 + h], in_=xt1[0:1, 0:HD],
                axis=AX.X, apply_absolute_value=True),
                reads=[b_xt1], writes=[b_tiny], tag=f"alpha{kk}{h}")
            p.op("vector", lambda vv: vv.tensor_scalar(
                out=xt1[0:1, 0:HD], in0=xt1[0:1, 0:HD],
                scalar1=0.0, scalar2=2.0, op0=OP.is_ge, op1=OP.mult),
                reads=[b_xt1], writes=[b_xt1], tag=f"sge{kk}{h}")
            p.op("vector", lambda vv, bi=bi, h=h: vv.tensor_scalar(
                out=xt1[0:1, 0:HD], in0=xt1[0:1, 0:HD],
                scalar1=-1.0, scalar2=c.tiny[0:1, 8 + bi * 4 + h:9 + bi * 4 + h],
                op0=OP.add, op1=OP.mult),
                reads=[b_xt1, b_tiny], writes=[b_xt1], tag=f"sgn{kk}{h}")
            p.op("vector", lambda vv: vv.tensor_scalar(
                out=xt1[0:1, 0:HD], in0=xt1[0:1, 0:HD],
                scalar1=1.0 / HD, scalar2=None, op0=OP.mult),
                reads=[b_xt1], writes=[b_xt1], tag=f"bsc{kk}{h}")
            for sl in range(8):
                p.op("tensor", lambda t, sl=sl: t.matmul(
                    c.ps_t[:, 48 + sl:49 + sl],
                    xt1[0:1, 128 * sl:128 * (sl + 1)], ones1,
                    start=True, stop=True),
                    reads=[b_xt1, b_one], writes=[b_pst], tag=f"bvc{kk}{h}{sl}")
            p.op("vector", lambda vv, bi=bi, h=h: vv.tensor_copy(
                out=c.bvs[:, bi * DCH + h * 8:bi * DCH + h * 8 + 8],
                in_=c.ps_t[:, 48:56]),
                reads=[b_pst], writes=[b_bvs], tag=f"bvce{kk}{h}")

    def vts(out, in0, s1, s2, o0, o1, deps, tag):
        if s2 is None:
            return p.op("vector", lambda vv: vv.tensor_scalar(
                out=out, in0=in0, scalar1=s1, scalar2=None, op0=o0),
                reads=[b_vecs, b_bvs], writes=[b_vecs], deps=deps, tag=tag)
        return p.op("vector", lambda vv: vv.tensor_scalar(
            out=out, in0=in0, scalar1=s1, scalar2=s2, op0=o0, op1=o1),
            reads=[b_vecs, b_bvs], writes=[b_vecs], deps=deps, tag=tag)

    def vtt(out, in0, in1, op, deps, tag):
        return p.op("vector", lambda vv: vv.tensor_tensor(out=out, in0=in0, in1=in1, op=op),
                    reads=[b_vecs, b_bvs], writes=[b_vecs], deps=deps, tag=tag)

    u_op = vts(vcol(c, "u"), vcol(c, "ca"), 1.0, None, OP.add, OP.bypass, [c.mod_done["ca"]], "mk_u")
    u2_op = vts(vcol(c, "u2"), vcol(c, "cb"), 1.0, None, OP.add, OP.bypass, [c.mod_done["cb"]], "mk_u2")
    for gn in ("ga", "gb"):
        dep = [c.mod_done[gn]]
        vts(vcol(c, "tmp"), vcol(c, gn), -1.0, None, OP.mult, None, dep, f"cl1a{gn}")
        vtt(vcol(c, "tmp"), vcol(c, "tmp"), vcol(c, gn), OP.max, [], f"cl1b{gn}")
        vts(vcol(c, "tmp"), vcol(c, "tmp"), 0.25, None, OP.max, None, [], f"cl1c{gn}")
        vts(vcol(c, gn), vcol(c, gn), 0.0, 2.0, OP.is_ge, OP.mult, dep, f"cl2{gn}")
        vts(vcol(c, gn), vcol(c, gn), -1.0, None, OP.add, OP.bypass, [], f"cl3{gn}")
        vtt(vcol(c, gn), vcol(c, gn), vcol(c, "tmp"), OP.mult, [], f"cl4{gn}")
    vtt(vcol(c, "qkb"), c.bvs[:, 0:DCH], c.bvs[:, DCH:2 * DCH], OP.mult, [], "mk_qkb")
    p.op("vector", lambda vv: vv.tensor_copy(out=vcol(c, "vb"), in_=c.bvs[:, 2 * DCH:3 * DCH]),
         reads=[b_bvs], writes=[b_vecs], tag="mk_vb")
    gavb_op = vtt(vcol(c, "gavb"), vcol(c, "ga"), vcol(c, "vb"), OP.mult, [], "mk_gavb")

    b_bc = buf(c, "bcast")
    c.bc_done = {}
    for gi, (gn, dep) in enumerate((("u", [u_op]), ("sa", [c.mod_done["sa"]]),
                                    ("gavb", [gavb_op]))):
        for k in range(DCH):
            p.op("tensor", lambda t, gn=gn, k=k: t.matmul(
                c.ps_t[0:1, 0:128], vcol(c, gn, k, 1), c.cidf[:],
                start=True, stop=True),
                reads=[b_vecs, b_idf], writes=[b_pst], deps=dep, tag=f"r{gn}{k}")
            p.op("vector", lambda vv, k=k: vv.tensor_copy(
                out=mrow[0:1, 128 * k:128 * (k + 1)], in_=c.ps_t[0:1, 0:128]),
                reads=[b_pst], writes=[b_mrow], tag=f"re{gn}{k}")
        lastb = None
        for s in range(8):
            p.op("tensor", lambda t, s=s: t.matmul(
                c.ps_t[:, 0:512], c.cones[0:1, 0:128], mrow[0:1, 512 * s:512 * (s + 1)],
                start=True, stop=True),
                reads=[b_mrow, b_onesr], writes=[b_pst], tag=f"b{gn}{s}")
            lastb = p.op("vector", lambda vv, gi=gi, s=s: vv.tensor_copy(
                out=c.bcast[:, gi * D + 512 * s:gi * D + 512 * (s + 1)], in_=c.ps_t[:, 0:512]),
                reads=[b_pst], writes=[b_bc], tag=f"be{gn}{s}")
        c.bc_done[gn] = lastb
    return c


def ln_tile(c, xsrc, xt, b_xt, nh, b_nh, stcol, tag, uoff, soff, extra_deps=()):
    p = c.p
    b_st = buf(c, "lnst")
    b_bc = buf(c, "bcast")
    ld = p.op("scalar", lambda s: s.dma_start(out=xt, in_=xsrc),
              writes=[b_xt], deps=list(extra_deps) + c.pdeps, dma=True, tag=f"{tag}_ld")
    for i in range(8):
        p.op("vector", lambda vv, i=i: vv.bn_stats(
            out=c.lnst[:, stcol + 6 * i:stcol + 6 * (i + 1)],
            in_=xt[:, 512 * i:512 * (i + 1)]),
            reads=[b_xt], writes=[b_st], tag=f"{tag}_bn{i}")
    p.op("vector", lambda vv: vv.bn_aggr(
        out=c.lnst[:, stcol + 48:stcol + 50], in_=c.lnst[:, stcol:stcol + 48]),
        reads=[b_st], writes=[b_st], tag=f"{tag}_ag")
    p.op("vector", lambda vv: vv.tensor_scalar(
        out=c.lnst[:, stcol + 53:stcol + 54], in0=c.lnst[:, stcol + 49:stcol + 50],
        scalar1=EPS, scalar2=None, op0=OP.add),
        reads=[b_st], writes=[b_st], tag=f"{tag}_ep")
    p.op("scalar", lambda s: s.activation(
        out=c.lnst[:, stcol + 50:stcol + 51], in_=c.lnst[:, stcol + 53:stcol + 54],
        func=AF.Sqrt),
        reads=[b_st], writes=[b_st], tag=f"{tag}_sq")
    p.op("vector", lambda vv: vv.reciprocal(
        out=c.lnst[:, stcol + 51:stcol + 52], in_=c.lnst[:, stcol + 50:stcol + 51]),
        reads=[b_st], writes=[b_st], tag=f"{tag}_rc")
    p.op("vector", lambda vv: vv.tensor_scalar(
        out=c.lnst[:, stcol + 52:stcol + 53], in0=c.lnst[:, stcol + 48:stcol + 49],
        scalar1=c.lnst[:, stcol + 51:stcol + 52], scalar2=-1.0, op0=OP.mult, op1=OP.mult),
        reads=[b_st], writes=[b_st], tag=f"{tag}_nb")
    p.op("vector", lambda vv: vv.tensor_scalar(
        out=nh, in0=xt, scalar1=c.lnst[:, stcol + 51:stcol + 52],
        scalar2=c.lnst[:, stcol + 52:stcol + 53], op0=OP.mult, op1=OP.add),
        reads=[b_xt, b_st], writes=[b_nh], tag=f"{tag}_nm")
    p.op("vector", lambda vv: vv.tensor_tensor(
        out=nh, in0=nh, in1=c.bcast[:, uoff:uoff + D], op=OP.mult),
        reads=[b_nh, b_bc], writes=[b_nh], tag=f"{tag}_mu")
    p.op("vector", lambda vv: vv.tensor_tensor(
        out=nh, in0=nh, in1=c.bcast[:, soff:soff + D], op=OP.add),
        reads=[b_nh, b_bc], writes=[b_nh], tag=f"{tag}_ms")
    return ld


def stage_lnspan(c):
    p = c.p
    b_idb = buf(c, "cidb")
    b_psx = buf(c, "ps_x")
    b_fm = buf(c, "ar_fm")
    fm = ar(c, 13952, 4096, BF16)
    xts = [ar(c, 0, D), ar(c, 4096, D)]
    b_xts = [buf(c, "ar_xt0"), buf(c, "ar_xt1")]
    nh = ar(c, 8192, 4096, BF16)
    b_nh = buf(c, "ar_nh")
    c.ha_done = []
    for ti in range(T // MK):
        xt, b_xt = xts[ti % 2], b_xts[ti % 2]
        dep = [c.bc_done["u"], c.bc_done["sa"]] if ti == 0 else []
        ln_tile(c, c.x_full[MK * ti:MK * (ti + 1), :], xt, b_xt, nh, b_nh,
                (ti % 2) * 56, f"ls{ti}", 0, D, extra_deps=dep)
        tok = p.op("scalar", lambda s, ti=ti: s.dma_start(
            out=c.ha_tok[MK * ti:MK * (ti + 1), :], in_=nh),
            reads=[b_nh], writes=[buf(c, "d_tok")], dma=True, tag=f"ls{ti}_tok")
        for bt in range(4):
            for k8 in range(8):
                k = bt * 8 + k8
                p.op("tensor", lambda t, k=k, k8=k8: t.transpose(
                    c.ps_x[:, 128 * k8:128 * (k8 + 1)], nh[:, 128 * k:128 * (k + 1)],
                    c.cidb[:]),
                    reads=[b_nh, b_idb], writes=[b_psx], tag=f"ls{ti}_tr{k}")
            p.op("vector", lambda vv, bt=bt: vv.tensor_copy(
                out=fm[:, 1024 * bt:1024 * (bt + 1)], in_=c.ps_x[:]),
                reads=[b_psx], writes=[b_fm], tag=f"ls{ti}_te{bt}")
        fmw = p.op("scalar", lambda s, ti=ti: s.dma_start(
            out=c.ha_fm[ti].rearrange("(cd p) t -> p cd t", p=128),
            in_=fm.rearrange("p (cd t) -> p cd t", t=128)),
            reads=[b_fm], writes=[buf(c, "d_fm")], dma=True, tag=f"ls{ti}_fw")
        c.ha_done.append((tok, fmw))


def dequant_units(c):
    p = c.p
    sr_f = ar(c, 10368, 1024)
    pk_i = ar(c, 11392, 1024, I32)
    nb_i = ar(c, 12416, 512, I32)
    nb_f = ar(c, 12928, 512)
    ou_b = ar(c, 13440, 1024, BF16)
    b_sr, b_pk, b_ni, b_nf, b_ou = (buf(c, "ar_dqsr"), buf(c, "ar_dqpk"),
                                    buf(c, "ar_dqni"), buf(c, "ar_dqnf"),
                                    buf(c, "ar_dqou"))
    units = []
    for mat in "gud":
        nk = D if mat in "gu" else INTER
        nn = INTER if mat in "gu" else D
        nrow = nk // 8
        rtiles = [(R, min(128, nrow - R)) for R in range(0, nrow, 128)]
        for Ri, (R0, rows) in enumerate(rtiles):
            for blk in range((nn + 1023) // 1024):
                ncols = min(1024, nn - 1024 * blk)
                units.append((mat, Ri, R0, rows, blk, ncols))

    def emit_unit(u, eng):
        mat, Ri, R0, rows, blk, ncols = u
        tg = f"dq{mat}{Ri}_{blk}"
        p.op("sync", lambda g: g.dma_start(
            out=pk_i[0:rows, 0:ncols], in_=c.qw[mat][R0:R0 + rows, 1024 * blk:1024 * blk + ncols]),
            writes=[b_pk], deps=c.pdeps, dma=True, tag=tg + "pk")
        p.op("sync", lambda g: g.dma_start(
            out=sr_f[0:rows, 0:ncols], in_=c.srep[mat][R0:R0 + rows, 1024 * blk:1024 * blk + ncols]),
            writes=[b_sr], deps=c.pdeps, dma=True, tag=tg + "sr")
        nh2 = (ncols + 511) // 512
        for j in range(8):
            for hf in range(nh2):
                hc = min(512, ncols - 512 * hf)
                p.op("vector", lambda E, j=j, hf=hf, hc=hc: E.tensor_scalar(
                    out=nb_i[0:rows, 0:hc], in0=pk_i[0:rows, 512 * hf:512 * hf + hc],
                    scalar1=4 * j, scalar2=15, op0=OP.logical_shift_right,
                    op1=OP.bitwise_and),
                    reads=[b_pk], writes=[b_ni], tag=f"{tg}_{j}{hf}x")
                p.op("vector", lambda E, hc=hc: E.tensor_copy(
                    out=nb_f[0:rows, 0:hc], in_=nb_i[0:rows, 0:hc]),
                    reads=[b_ni], writes=[b_nf], tag=f"{tg}_{j}{hf}c")
                if mat in "gu":
                    rw = ncols // 8
                    r0h = 64 * hf if ncols == 1024 else 64 * hf
                    rcnt = hc // 8
                    p.op(eng, lambda E, hf=hf, rw=rw, r0h=r0h, rcnt=rcnt, ncols=ncols: E.tensor_tensor(
                        out=ou_b[0:rows, 0:ncols].rearrange("p (j2 r2) -> p r2 j2", j2=8
                                                      )[:, r0h:r0h + rcnt, :],
                        in0=nb_f[0:rows, 0:rcnt * 8].rearrange("p (r2 j2) -> p r2 j2", j2=8),
                        in1=sr_f[0:rows, 0:ncols].rearrange("p (j2 r2) -> p r2 j2", j2=8
                                                      )[:, r0h:r0h + rcnt, :],
                        op=OP.mult),
                        reads=[b_nf, b_sr], writes=[b_ou], tag=f"{tg}_{j}{hf}m")
                else:
                    p.op(eng, lambda E, hf=hf, hc=hc: E.tensor_tensor(
                        out=ou_b[0:rows, 512 * hf:512 * hf + hc],
                        in0=nb_f[0:rows, 0:hc],
                        in1=sr_f[0:rows, 512 * hf:512 * hf + hc], op=OP.mult),
                        reads=[b_nf, b_sr], writes=[b_ou], tag=f"{tg}_{j}{hf}m")
            p.op("sync", lambda g, j=j: g.dma_start(
                out=c.wdq[mat][1024 * Ri + rows * j:1024 * Ri + rows * (j + 1),
                               1024 * blk:1024 * blk + ncols],
                in_=ou_b[0:rows, 0:ncols]),
                reads=[b_ou], dma=True, tag=f"{tg}_{j}w")
    return units, emit_unit


def stage_attention(c):
    p = c.p
    b_idb = buf(c, "cidb")
    b_psx = buf(c, "ps_x")
    b_psa = buf(c, "ps_aA")
    b_pss = [buf(c, "ps_sA"), buf(c, "ps_sB")]
    b_vecs = buf(c, "vecs")
    b_bc = buf(c, "bcast")
    xt0 = ar(c, 0, D)
    b_xt0 = buf(c, "ar_xt0")
    nh = ar(c, 8192, 4096, BF16)
    b_nh = buf(c, "ar_nh")
    qpr = ar(c, 4096, 8192, BF16)
    b_qpr = buf(c, "ar_qpr")
    kts = [ar(c, 13952 + 2048 * i, 4096, BF16) for i in range(2)]
    b_kt = [buf(c, f"ar_kt{i}") for i in range(2)]
    vts = [ar(c, 18048 + 2048 * i, 4096, BF16) for i in range(4)]
    b_vt = [buf(c, f"ar_vt{i}") for i in range(4)]
    x2s = ar(c, 26240, 8192)
    b_x2s = buf(c, "ar_x2s")
    ats = [ar(c, 34432 + 128 * i, 256, BF16) for i in range(16)]
    b_at = [buf(c, f"ar_at{i}") for i in range(16)]
    msks = [ar(c, 36480 + 128 * i, 256, BF16) for i in range(2)]
    b_msk = [buf(c, f"ar_msk{i}") for i in range(2)]

    c.x2_done = []
    for q in range(4):
        for half in range(2):
            r0 = q * CH + half * 128
            ln_tile(c, c.x_own[r0:r0 + 128, :], xt0, b_xt0, nh, b_nh,
                    56 * (half % 2), f"qp{q}_{half}", 0, D,
                    extra_deps=([c.bc_done["u"], c.bc_done["sa"]] if q == 0 else []))
            for bt in range(4):
                for k8 in range(8):
                    k = bt * 8 + k8
                    p.op("tensor", lambda t, k=k, k8=k8: t.transpose(
                        c.ps_x[:, 128 * k8:128 * (k8 + 1)], nh[:, 128 * k:128 * (k + 1)],
                        c.cidb[:]),
                        reads=[b_nh, b_idb], writes=[b_psx], tag=f"qp{q}{half}tr{k}")
                for k8 in range(8):
                    k = bt * 8 + k8
                    p.op("vector", lambda vv, k=k, k8=k8, half=half: vv.tensor_scalar(
                        out=qpr[:, 256 * k + 128 * half:256 * k + 128 * half + 128],
                        in0=c.ps_x[:, 128 * k8:128 * (k8 + 1)],
                        scalar1=vcol(c, "qkb", k, 1), scalar2=None, op0=OP.mult),
                        reads=[b_psx, b_vecs], writes=[b_qpr], tag=f"qp{q}{half}e{k}")
        span = PAD_SPAN[q]
        first_flash = True
        for r0m in range(0, span, R_RUN):
            run = list(range(r0m, min(r0m + R_RUN, span)))
            for m in run:
                ks, bk = kts[m % 2], b_kt[m % 2]
                vs, bv = vts[m % 4], b_vt[m % 4]
                p.op("scalar", lambda s, m=m, ks=ks: s.dma_start(
                    out=ks.rearrange("p (cd t) -> p cd t", t=128),
                    in_=c.ha_fm[m].rearrange("(cd p) t -> p cd t", p=128)),
                    reads=[buf(c, "d_fm")], writes=[bk], deps=c.pdeps, tag=f"at{q}k{m}", dma=True)
                p.op("scalar", lambda s, m=m, vs=vs: s.dma_start(
                    out=vs, in_=c.ha_tok[MK * m:MK * (m + 1), :]),
                    reads=[buf(c, "d_tok")], writes=[bv], deps=c.pdeps, tag=f"at{q}v{m}", dma=True)
                mi = m % 2
                si = sum(PAD_SPAN[:q]) + m
                p.op("gpsimd", lambda g, si=si, mi=mi: g.dma_start(
                    out=msks[mi], in_=c.masks[si]),
                    writes=[b_msk[mi]], deps=c.pdeps, dma=True, tag=f"at{q}m{m}")
                for h in range(H):
                    sb = (h % 2)
                    pso = 512 * sb
                    ati = (m % 4) * 4 + h
                    for dk in range(8):
                        kc = h * 8 + dk
                        p.op("tensor", lambda t, kc=kc, pso=pso, ks=ks: t.matmul(
                            c.ps_s[:, pso:pso + 256], ks[:, 128 * kc:128 * (kc + 1)],
                            qpr[:, 256 * kc:256 * (kc + 1)],
                            start=(kc % 8 == 0), stop=(kc % 8 == 7)),
                            reads=[bk, b_qpr], writes=[b_pss[sb]], tag=f"s{q}{m}{h}{dk}")
                    p.op("scalar", lambda s, pso=pso, ati=ati: s.activation(
                        out=ats[ati], in_=c.ps_s[:, pso:pso + 256],
                        func=AF.Sigmoid, scale=0.125),
                        reads=[b_pss[sb]], writes=[b_at[ati]], tag=f"sg{q}{m}{h}")
                    p.op("vector", lambda vv, ati=ati, mi=mi: vv.tensor_tensor(
                        out=ats[ati], in0=ats[ati], in1=msks[mi], op=OP.mult),
                        reads=[b_at[ati], b_msk[mi]], writes=[b_at[ati]], tag=f"mk{q}{m}{h}")
            for h in range(H):
                for qs in range(2):
                    for m in run:
                        ati = (m % 4) * 4 + h
                        vs, bv = vts[m % 4], b_vt[m % 4]
                        for dh in range(2):
                            p.op("tensor", lambda t, ati=ati, qs=qs, dh=dh, h=h, vs=vs,
                                 st=(m == run[0]), sp=(m == run[-1]): t.matmul(
                                c.ps_a[:, 512 * dh:512 * (dh + 1)],
                                ats[ati][:, 128 * qs:128 * (qs + 1)],
                                vs[:, 1024 * h + 512 * dh:1024 * h + 512 * (dh + 1)],
                                start=st, stop=sp),
                                reads=[b_at[ati], bv], writes=[b_psa], tag=f"av{q}{m}{h}{qs}{dh}")
                    if first_flash:
                        p.op("vector", lambda vv, qs=qs, h=h: vv.tensor_copy(
                            out=x2s[:, 4096 * qs + 1024 * h:4096 * qs + 1024 * (h + 1)],
                            in_=c.ps_a[:, 0:1024]),
                            reads=[b_psa], writes=[b_x2s], deps=c.pdeps, tag=f"fl{q}{r0m}{h}{qs}")
                    else:
                        p.op("vector", lambda vv, qs=qs, h=h: vv.tensor_tensor(
                            out=x2s[:, 4096 * qs + 1024 * h:4096 * qs + 1024 * (h + 1)],
                            in0=x2s[:, 4096 * qs + 1024 * h:4096 * qs + 1024 * (h + 1)],
                            in1=c.ps_a[:, 0:1024], op=OP.add),
                            reads=[b_psa, b_x2s], writes=[b_x2s], tag=f"fl{q}{r0m}{h}{qs}")
            first_flash = False
        for qs in range(2):
            r0 = q * CH + qs * 128
            p.op("scalar", lambda s, r0=r0: s.dma_start(
                out=xt0, in_=c.x_own[r0:r0 + 128, :]),
                writes=[b_xt0], dma=True, tag=f"ev{q}{qs}ld")
            p.op("vector", lambda vv, qs=qs: vv.tensor_tensor(
                out=x2s[:, 4096 * qs:4096 * (qs + 1)],
                in0=x2s[:, 4096 * qs:4096 * (qs + 1)],
                in1=c.bcast[:, 2 * D:3 * D], op=OP.mult),
                reads=[b_x2s, b_bc], writes=[b_x2s], tag=f"ev{q}{qs}g")
            p.op("vector", lambda vv, qs=qs: vv.tensor_tensor(
                out=x2s[:, 4096 * qs:4096 * (qs + 1)],
                in0=x2s[:, 4096 * qs:4096 * (qs + 1)],
                in1=xt0, op=OP.add),
                reads=[b_x2s, b_xt0], writes=[b_x2s], tag=f"ev{q}{qs}a")
            w = p.op("gpsimd", lambda g, r0=r0, qs=qs: g.dma_start(
                out=c.x2_dram[r0:r0 + 128, :], in_=x2s[:, 4096 * qs:4096 * (qs + 1)]),
                reads=[b_x2s], writes=[buf(c, "d_x2")], dma=True, tag=f"ev{q}{qs}w")
            c.x2_done.append(w)


def stage_lnb_ffn(c):
    p = c.p
    if getattr(c, "want_debug", False):
        c.dbg_dst_ap = c.nc.dram_tensor("dbg_dst", [128, 2048], F32, kind="ExternalOutput").ap()
        c.dbg_frg_ap = c.nc.dram_tensor("dbg_frg", [128, 512], F32, kind="ExternalOutput").ap()
        c.dbg_x2c_ap = c.nc.dram_tensor("dbg_x2c", [128, 512], F32, kind="ExternalOutput").ap()
    b_idb = buf(c, "cidb")
    b_psx = buf(c, "ps_x")
    b_vecs = buf(c, "vecs")
    b_bc = buf(c, "bcast")
    b_pst = buf(c, "ps_t")
    b_one = buf(c, "cone1")
    b_idf = buf(c, "cidf")
    b_onesr = buf(c, "cones")
    ones1 = c.cones[0:1, 128:129]

    mrow = ar(c, 0, 4096)
    b_mrow = buf(c, "ar_xt0")
    for gi, gn in enumerate(("u2", "sb", "gb")):
        for k in range(DCH):
            p.op("tensor", lambda t, gn=gn, k=k: t.matmul(
                c.ps_t[0:1, 0:128], vcol(c, gn, k, 1), c.cidf[:],
                start=True, stop=True),
                reads=[b_vecs, b_idf], writes=[b_pst], deps=c.pdeps, tag=f"r3{gn}{k}")
            p.op("vector", lambda vv, k=k: vv.tensor_copy(
                out=mrow[0:1, 128 * k:128 * (k + 1)], in_=c.ps_t[0:1, 0:128]),
                reads=[b_pst], writes=[b_mrow], deps=c.pdeps, tag=f"re3{gn}{k}")
        for s in range(8):
            p.op("tensor", lambda t, s=s: t.matmul(
                c.ps_t[:, 0:512], c.cones[0:1, 0:128], mrow[0:1, 512 * s:512 * (s + 1)],
                start=True, stop=True),
                reads=[b_mrow, b_onesr], writes=[b_pst], tag=f"b3{gn}{s}")
            p.op("vector", lambda vv, gi=gi, s=s: vv.tensor_copy(
                out=c.bcast[:, gi * D + 512 * s:gi * D + 512 * (s + 1)], in_=c.ps_t[:, 0:512]),
                reads=[b_pst], writes=[b_bc], tag=f"be3{gn}{s}")

    ht = ar(c, 10368, 32768, BF16)
    b_ht = buf(c, "ar_ht")
    xt0 = ar(c, 0, D)
    b_xt0 = buf(c, "ar_xt0")
    nh = ar(c, 8192, 4096, BF16)
    b_nh = buf(c, "ar_nh")
    pch = perm_chunk_list(D)
    for ti in range(8):
        ln_tile(c, c.x2_dram[MK * ti:MK * (ti + 1), :], xt0, b_xt0, nh, b_nh,
                56 * (ti % 2), f"lb{ti}", 0, D, extra_deps=c.x2_done)
        for bt in range(4):
            for k8 in range(8):
                kc = bt * 8 + k8
                R0, rows, j, pos = pch[kc]
                p.op("tensor", lambda t, k8=k8, R0=R0, j=j: t.transpose(
                    c.ps_x[:, 128 * k8:128 * (k8 + 1)],
                    nh[:, 8 * R0 + j:8 * R0 + j + 1017:8],
                    c.cidb[:]),
                    reads=[b_nh, b_idb], writes=[b_psx], tag=f"lb{ti}tr{kc}")
            p.op("vector", lambda vv, bt=bt, ti=ti: vv.tensor_copy(
                out=ht.rearrange("p (kc t) -> p kc t", t=1024)[
                    :, bt * 8:(bt + 1) * 8, ti * 128:(ti + 1) * 128],
                in_=c.ps_x[:].rearrange("p (k8 t) -> p k8 t", t=128)),
                reads=[b_psx], writes=[b_ht], tag=f"lb{ti}te{bt}")

    b_psa = buf(c, "ps_aA")
    b_hsh = buf(c, "hsumh")
    b_indh = buf(c, "cindh")
    for kc in range(DCH):
        Ri = kc // 8
        for th in range(2):
            p.op("tensor", lambda t, kc=kc, Ri=Ri, th=th: t.matmul(
                c.ps_a[0:32, 512 * th:512 * (th + 1)],
                c.cindh[:, 32 * Ri:32 * (Ri + 1)],
                ht.rearrange("p (kc t) -> p kc t", t=1024)[:, kc, 512 * th:512 * (th + 1)],
                start=(kc == 0), stop=(kc == DCH - 1)),
                reads=[b_ht, b_indh], writes=[b_psa], tag=f"hsh{kc}{th}")
    p.op("vector", lambda vv: vv.tensor_copy(out=c.hsumh[:], in_=c.ps_a[0:32, 0:1024]),
         reads=[b_psa], writes=[b_hsh], tag="hsh_ev")
    b_hsl = buf(c, "hsuml")
    p.op("vector", lambda vv: vv.tensor_tensor(
        out=c.hsuml[:], in0=c.ps_a[0:32, 0:1024], in1=c.hsumh[:], op=OP.subtract),
        reads=[b_psa, b_hsh], writes=[b_hsl], tag="hsl_ev")

    wg = ar(c, 26752, 4096, BF16)
    wu = ar(c, 28800, 4096, BF16)
    b_wg, b_wu = buf(c, "ar_wg"), buf(c, "ar_wu")
    sil = ar(c, 30848, 1024)
    b_sil = buf(c, "ar_sil")
    vst = ar(c, 31872, 1024, BF16)
    b_vst = buf(c, "ar_vst")
    szt = ar(c, 32384, 512, BF16)
    b_szt = buf(c, "ar_szt")
    b_pss = buf(c, "ps_sA")
    b_pstu = [buf(c, "ps_t"), buf(c, "ps_u")]
    b_hsv = buf(c, "hsumv")
    b_indv = buf(c, "cindv")
    pchI = perm_chunk_list(INTER)
    for ci, (R0, rows, j, pos) in enumerate(pchI):
        p.op("sync", lambda g, pos=pos, rows=rows: g.dma_start(
            out=wg[:, 0:32 * rows].rearrange("p (kc r) -> p kc r", kc=32),
            in_=c.wdq["g"].rearrange("(kc p) n -> p kc n", p=128)[:, :, pos:pos + rows]),
            writes=[b_wg], deps=c.pdeps, dma=True, tag=f"gw{ci}")
        p.op("sync", lambda g, pos=pos, rows=rows: g.dma_start(
            out=wu[:, 0:32 * rows].rearrange("p (kc r) -> p kc r", kc=32),
            in_=c.wdq["u"].rearrange("(kc p) n -> p kc n", p=128)[:, :, pos:pos + rows]),
            writes=[b_wu], deps=c.pdeps, dma=True, tag=f"uw{ci}")
        p.op("gpsimd", lambda g, pos=pos, rows=rows: g.dma_start(
            out=szt[0:32, 0:rows], in_=c.sz_dram["g"][0:32, pos:pos + rows]),
            writes=[b_szt], deps=c.pdeps + [c.sz_done], dma=True, tag=f"szg{ci}")
        p.op("gpsimd", lambda g, pos=pos, rows=rows: g.dma_start(
            out=szt[0:32, rows:2 * rows], in_=c.sz_dram["u"][0:32, pos:pos + rows]),
            reads=[b_szt], writes=[b_szt], dma=True, tag=f"szu{ci}")
        for th in range(2):
            for kc in range(DCH):
                p.op("tensor", lambda t, kc=kc, th=th, rows=rows: t.matmul(
                    c.ps_a[0:rows, 512 * th:512 * th + 512],
                    wg[:, kc * rows:(kc + 1) * rows],
                    ht.rearrange("p (kc t) -> p kc t", t=1024)[:, kc, 512 * th:512 * th + 512],
                    start=(kc == 0), stop=False),
                    reads=[b_wg, b_ht], writes=[b_psa], tag=f"g{ci}{th}{kc}")
                p.op("tensor", lambda t, kc=kc, th=th, rows=rows: t.matmul(
                    c.ps_s[0:rows, 512 * th:512 * th + 512],
                    wu[:, kc * rows:(kc + 1) * rows],
                    ht.rearrange("p (kc t) -> p kc t", t=1024)[:, kc, 512 * th:512 * th + 512],
                    start=(kc == 0), stop=False),
                    reads=[b_wu, b_ht], writes=[b_pss], tag=f"u{ci}{th}{kc}")
            p.op("tensor", lambda t, th=th, rows=rows: t.matmul(
                c.ps_a[0:rows, 512 * th:512 * th + 512], szt[0:32, 0:rows],
                c.hsumh[:, 512 * th:512 * th + 512], start=False, stop=False),
                reads=[b_szt, b_hsh], writes=[b_psa], tag=f"gz{ci}{th}")
            p.op("tensor", lambda t, th=th, rows=rows: t.matmul(
                c.ps_a[0:rows, 512 * th:512 * th + 512], szt[0:32, 0:rows],
                c.hsuml[:, 512 * th:512 * th + 512], start=False, stop=True),
                reads=[b_szt, b_hsl], writes=[b_psa], tag=f"gzl{ci}{th}")
            p.op("tensor", lambda t, th=th, rows=rows: t.matmul(
                c.ps_s[0:rows, 512 * th:512 * th + 512], szt[0:32, rows:2 * rows],
                c.hsumh[:, 512 * th:512 * th + 512], start=False, stop=False),
                reads=[b_szt, b_hsh], writes=[b_pss], tag=f"uz{ci}{th}")
            p.op("tensor", lambda t, th=th, rows=rows: t.matmul(
                c.ps_s[0:rows, 512 * th:512 * th + 512], szt[0:32, rows:2 * rows],
                c.hsuml[:, 512 * th:512 * th + 512], start=False, stop=True),
                reads=[b_szt, b_hsl], writes=[b_pss], tag=f"uzl{ci}{th}")
            p.op("scalar", lambda s, th=th, rows=rows: s.activation(
                out=sil[0:rows, 512 * th:512 * th + 512],
                in_=c.ps_a[0:rows, 512 * th:512 * th + 512], func=AF.Silu),
                reads=[b_psa], writes=[b_sil], tag=f"sl{ci}{th}")
            p.op("vector", lambda vv, th=th, rows=rows: vv.tensor_tensor(
                out=vst[0:rows, 512 * th:512 * th + 512],
                in0=sil[0:rows, 512 * th:512 * th + 512],
                in1=c.ps_s[0:rows, 512 * th:512 * th + 512], op=OP.mult),
                reads=[b_sil, b_pss], writes=[b_vst], tag=f"vm{ci}{th}")
            p.op("tensor", lambda t, th=th, rows=rows, Ri=R0 // 128,
                 st=(ci == 0), sp=(ci == len(pchI) - 1): t.matmul(
                (c.ps_t if th == 0 else c.ps_u)[0:86, 0:512],
                c.cindv[:, 86 * Ri:86 * (Ri + 1)][0:rows, :],
                vst[0:rows, 512 * th:512 * th + 512],
                start=st, stop=sp),
                reads=[b_vst, b_indv], writes=[b_pstu[th]], tag=f"hv{ci}{th}")
        p.op("gpsimd", lambda g, pos=pos, rows=rows: g.dma_start(
            out=c.v_dram[pos:pos + rows, :], in_=vst[0:rows, :]),
            reads=[b_vst], writes=[buf(c, "d_v")], dma=True, tag=f"vw{ci}")
    c.ffn_gu_done = []
    for th in range(2):
        ev = p.op("vector", lambda vv, th=th: vv.tensor_copy(
            out=c.hsumv[:, 512 * th:512 * th + 512],
            in_=(c.ps_t if th == 0 else c.ps_u)[0:86, 0:512]),
            reads=[b_pstu[th]], writes=[b_hsv], tag=f"hv_ev{th}")
        c.ffn_gu_done.append(ev)

    if getattr(c, "want_debug", False):
        c.dbg_gbb_ap = c.nc.dram_tensor("dbg_gbb", [128, D], BF16, kind="ExternalOutput").ap()
        p.op("gpsimd", lambda g: g.dma_start(out=c.dbg_gbb_ap, in_=c.bcast[:, 2 * D:3 * D]),
             reads=[b_bc], writes=[buf(c, "d_dbggbb")], dma=True, tag="dbg_gbb")
    dst_ = ar(c, 26752, 2048)
    b_dst = buf(c, "ar_dst")
    frg = ar(c, 28800, 512)
    b_frg = buf(c, "ar_frg")
    x2c = ar(c, 29312, 512)
    b_x2c = buf(c, "ar_x2c")
    wdt = ar(c, 29824, 1024, BF16)
    b_wdt = buf(c, "ar_wdt")
    vch = ar(c, 30848, 1024, BF16)
    b_vch = buf(c, "ar_vch")
    szd = ar(c, 31872, 1024, BF16)
    b_szd = buf(c, "ar_szd")
    psq = [(c.ps_a, 0), (c.ps_a, 512), (c.ps_s, 0), (c.ps_s, 512)]
    b_psq = [buf(c, "ps_aA"), buf(c, "ps_aB"), buf(c, "ps_sA"), buf(c, "ps_sB2")]
    b_out = buf(c, "d_out")
    c.out_done = []
    for blkn in range(8):
        for th in range(2):
            p.op("gpsimd", lambda g, blkn=blkn: g.dma_start(
                out=szd[0:86, 0:512], in_=c.sz_dram["d"][0:86, 512 * blkn:512 * (blkn + 1)]),
                writes=[b_szd], deps=c.pdeps + [c.sz_done] + c.ffn_gu_done, dma=True, tag=f"dz{blkn}{th}")
            for ci, (R0, rows, j, pos) in enumerate(pchI):
                p.op("sync", lambda g, pos=pos, rows=rows, blkn=blkn: g.dma_start(
                    out=wdt[0:rows, 0:512],
                    in_=c.wdq["d"][pos:pos + rows, 512 * blkn:512 * (blkn + 1)]),
                    writes=[b_wdt], deps=c.pdeps + c.ffn_gu_done, dma=True, tag=f"dw{blkn}{th}{ci}")
                p.op("scalar", lambda s, pos=pos, rows=rows, th=th: s.dma_start(
                    out=vch[0:rows, 0:512],
                    in_=c.v_dram[pos:pos + rows, 512 * th:512 * (th + 1)]),
                    reads=[buf(c, "d_v")], writes=[b_vch], deps=c.pdeps, dma=True,
                    tag=f"dv{blkn}{th}{ci}")
                for sub in range(4):
                    ps, off = psq[sub]
                    p.op("tensor", lambda t, sub=sub, ps=ps, off=off, rows=rows,
                         st=(ci == 0), ci=ci: t.matmul(
                        ps[:, off:off + 512], wdt[0:rows, 128 * sub:128 * (sub + 1)],
                        vch[0:rows, 0:512], start=st, stop=False),
                        reads=[b_wdt, b_vch], writes=[b_psq[sub]], tag=f"dm{blkn}{th}{ci}{sub}")
            for sub in range(4):
                ps, off = psq[sub]
                p.op("tensor", lambda t, sub=sub, ps=ps, off=off, th=th: t.matmul(
                    ps[:, off:off + 512], szd[0:86, 128 * sub:128 * (sub + 1)],
                    c.hsumv[:, 512 * th:512 * (th + 1)], start=False, stop=True),
                    reads=[b_szd, buf(c, "hsumv")], writes=[b_psq[sub]], tag=f"dzm{blkn}{th}{sub}")
                p.op("vector", lambda vv, sub=sub, ps=ps, off=off: vv.tensor_copy(
                    out=dst_[:, 512 * sub:512 * (sub + 1)], in_=ps[:, off:off + 512]),
                    reads=[b_psq[sub]], writes=[b_dst], tag=f"de{blkn}{th}{sub}")
            if blkn == 0 and th == 0 and hasattr(c, "dbg_dst_ap"):
                p.op("gpsimd", lambda g: g.dma_start(out=c.dbg_dst_ap, in_=dst_[:, 0:2048]),
                     reads=[b_dst], writes=[buf(c, "d_dbgdst")], dma=True, tag="dbg_dst")
            dstb = ar(c, 30848, 4096, BF16)
            b_dstb = buf(c, "ar_dstb")
            for sub in range(4):
                p.op("vector", lambda vv, sub=sub: vv.tensor_copy(
                    out=dstb[:, 512 * sub:512 * (sub + 1)],
                    in_=dst_[:, 512 * sub:512 * (sub + 1)]),
                    reads=[b_dst], writes=[b_dstb], tag=f"db{blkn}{th}{sub}")
            for ts_ in range(4):
                for sub in range(4):
                    p.op("tensor", lambda t, ts_=ts_, sub=sub: t.transpose(
                        c.ps_x[:, 128 * sub:128 * (sub + 1)],
                        dstb[:, 512 * sub + 128 * ts_:512 * sub + 128 * (ts_ + 1)],
                        c.cidb[:]),
                        reads=[b_dstb, buf(c, "cidb")], writes=[buf(c, "ps_x")],
                        tag=f"dt{blkn}{th}{ts_}{sub}")
                p.op("scalar", lambda s, blkn=blkn, th=th, ts_=ts_: s.dma_start(
                    out=x2c[:, 0:512],
                    in_=c.x2_dram[512 * th + 128 * ts_:512 * th + 128 * (ts_ + 1),
                                  512 * blkn:512 * (blkn + 1)]),
                    reads=[buf(c, "d_x2")], writes=[b_x2c], deps=c.ffn_gu_done, dma=True, tag=f"dx{blkn}{th}{ts_}")
                p.op("vector", lambda vv, blkn=blkn: vv.tensor_tensor(
                    out=frg[:, 0:512], in0=c.ps_x[:, 0:512],
                    in1=c.bcast[:, 2 * D + 512 * blkn:2 * D + 512 * (blkn + 1)], op=OP.mult),
                    reads=[buf(c, "ps_x"), b_bc], writes=[b_frg], tag=f"df{blkn}{th}{ts_}"),
                p.op("vector", lambda vv: vv.tensor_tensor(
                    out=frg[:, 0:512], in0=frg[:, 0:512], in1=x2c[:, 0:512], op=OP.add),
                    reads=[b_frg, b_x2c], writes=[b_frg], tag=f"dg{blkn}{th}{ts_}")

                w = p.op("gpsimd", lambda g, blkn=blkn, th=th, ts_=ts_: g.dma_start(
                    out=c.out[512 * th + 128 * ts_:512 * th + 128 * (ts_ + 1),
                              512 * blkn:512 * (blkn + 1)],
                    in_=frg[:, 0:512]),
                    reads=[b_frg], writes=[b_out], dma=True, tag=f"do{blkn}{th}{ts_}")
                c.out_done.append(w)


def stage_sz(c):
    p = c.p
    zld = ar(c, 24192, 1536, I32)
    b_zld = buf(c, "ar_zld")
    zpl = ar(c, 25728, 1536)
    b_zpl = buf(c, "ar_zpl")
    scb = ar(c, 27264, 1024)
    b_scb = buf(c, "ar_scb")
    szb = ar(c, 28288, 1024, BF16)
    b_szb = buf(c, "ar_szb")
    last = None
    for mat in "gud":
        G = {"g": 32, "u": 32, "d": 86}[mat]
        nn = INTER if mat in "gu" else D
        ncpk = nn // 8
        p.op("gpsimd", lambda g, mat=mat, G=G, ncpk=ncpk: g.dma_start(
            out=zld[0:G, 0:ncpk], in_=c.qz[mat]),
            writes=[b_zld], dma=True, tag=f"sz_{mat}_ld")
        nblocks = (nn + 1023) // 1024
        for blk in range(nblocks):
            ncols = min(1024, nn - 1024 * blk)
            p.op("gpsimd", lambda g, mat=mat, G=G, blk=blk, ncols=ncols: g.dma_start(
                out=scb[0:G, 0:ncols], in_=c.scp[mat][0:G, 1024 * blk:1024 * blk + ncols]),
                writes=[b_scb], dma=True, tag=f"sz_{mat}{blk}_sc")
            for pn in range(8):
                cw = ncols // 8
                cc0 = 128 * blk
                p.op("vector", lambda vv, G=G, pn=pn, cw=cw, cc0=cc0: vv.tensor_scalar(
                    out=zpl[0:G, 0:cw].bitcast(I32), in0=zld[0:G, cc0:cc0 + cw],
                    scalar1=4 * pn, scalar2=15, op0=OP.logical_shift_right,
                    op1=OP.bitwise_and),
                    reads=[b_zld], writes=[b_zpl], tag=f"sz_{mat}{blk}{pn}x")
                p.op("vector", lambda vv, G=G, cw=cw: vv.tensor_scalar(
                    out=zpl[0:G, 512:512 + cw], in0=zpl[0:G, 0:cw].bitcast(I32),
                    scalar1=1.0, scalar2=-1.0, op0=OP.add, op1=OP.mult),
                    reads=[b_zpl], writes=[b_zpl], tag=f"sz_{mat}{blk}{pn}n")
                if mat in "gu":
                    p.op("vector", lambda vv, G=G, pn=pn, cw=cw: vv.tensor_tensor(
                        out=szb[0:G, pn * cw:(pn + 1) * cw], in0=zpl[0:G, 512:512 + cw],
                        in1=scb[0:G, pn * cw:(pn + 1) * cw], op=OP.mult),
                        reads=[b_zpl, b_scb], writes=[b_szb], tag=f"sz_{mat}{blk}{pn}m")
                else:
                    p.op("vector", lambda vv, G=G, pn=pn, cw=cw: vv.tensor_tensor(
                        out=szb[0:G, pn:pn + 8 * (cw - 1) + 1:8],
                        in0=zpl[0:G, 512:512 + cw],
                        in1=scb[0:G, pn:pn + 8 * (cw - 1) + 1:8], op=OP.mult),
                        reads=[b_zpl, b_scb], writes=[b_szb], tag=f"sz_{mat}{blk}{pn}m")
            last = p.op("gpsimd", lambda g, mat=mat, G=G, blk=blk, ncols=ncols: g.dma_start(
                out=c.sz_dram[mat][0:G, 1024 * blk:1024 * blk + ncols],
                in_=szb[0:G, 0:ncols]),
                reads=[b_szb], writes=[buf(c, f"d_sz{mat}")], dma=True, tag=f"sz_{mat}{blk}w2")
    c.sz_done = last


def stage_debug(c):
    p = c.p
    nc = c.nc
    dbg = {}
    dbg["vecs"] = nc.dram_tensor("dbg_vecs", [128, 12 * DCH], F32, kind="ExternalOutput").ap()
    dbg["bcast"] = nc.dram_tensor("dbg_bcast", [128, 3 * D], BF16, kind="ExternalOutput").ap()
    dbg["ha"] = nc.dram_tensor("dbg_ha", [256, D], BF16, kind="ExternalOutput").ap()
    dbg["fm"] = nc.dram_tensor("dbg_fm", [D, MK], BF16, kind="ExternalOutput").ap()
    dbg["x2"] = nc.dram_tensor("dbg_x2", [TOK, D], F32, kind="ExternalOutput").ap()
    dbg["wg"] = nc.dram_tensor("dbg_wg", [256, 2048], BF16, kind="ExternalOutput").ap()
    dbg["szg"] = nc.dram_tensor("dbg_szg", [32, INTER], BF16, kind="ExternalOutput").ap()
    dbg["v"] = nc.dram_tensor("dbg_v", [256, TOK], BF16, kind="ExternalOutput").ap()
    dbg["vlast"] = nc.dram_tensor("dbg_vlast", [256, TOK], BF16, kind="ExternalOutput").ap()
    dbg["wd"] = nc.dram_tensor("dbg_wd", [256, 2048], BF16, kind="ExternalOutput").ap()
    dbg["hsv"] = nc.dram_tensor("dbg_hsv", [86, 1024], BF16, kind="ExternalOutput").ap()
    dbg["ht"] = nc.dram_tensor("dbg_ht", [128, 4096], BF16, kind="ExternalOutput").ap()
    c.dbg_outs = list(dbg.keys())
    ops = [
        ("vecs", lambda g: g.dma_start(out=dbg["vecs"], in_=c.vecs[:]), ["vecs"]),
        ("bcast", lambda g: g.dma_start(out=dbg["bcast"], in_=c.bcast[:]), ["bcast"]),
        ("ha", lambda g: g.dma_start(out=dbg["ha"], in_=c.ha_tok[0:256, :]), ["d_tok"]),
        ("fm", lambda g: g.dma_start(out=dbg["fm"], in_=c.ha_fm[3]), ["d_fm"]),
        ("x2", lambda g: g.dma_start(out=dbg["x2"], in_=c.x2_dram), ["d_x2"]),
        ("wg", lambda g: g.dma_start(out=dbg["wg"], in_=c.wdq["g"][0:256, 0:2048]), []),
        ("szg", lambda g: g.dma_start(out=dbg["szg"], in_=c.sz_dram["g"]), ["d_szg"]),
        ("v", lambda g: g.dma_start(out=dbg["v"], in_=c.v_dram[0:256, :]), ["d_v"]),
        ("vlast", lambda g: g.dma_start(out=dbg["vlast"], in_=c.v_dram[10752:11008, :]), ["d_v"]),
        ("wd", lambda g: g.dma_start(out=dbg["wd"], in_=c.wdq["d"][0:256, 0:2048]), []),
        ("hsv", lambda g: g.dma_start(out=dbg["hsv"], in_=c.hsumv[:]), ["hsumv"]),
        ("ht", lambda g: g.dma_start(out=dbg["ht"], in_=ar(c, 10368, 8192, BF16)[:, 0:4096]), ["ar_ht"]),
    ]
    for name, fn, rd in ops:
        w = p.op("gpsimd", fn, reads=[buf(c, n) for n in rd],
                 deps=c.out_done[-1:], dma=True, tag="dbg_" + name)
        p.wait("gpsimd", w)


class _SelfModule:
    def __getattr__(self, n):
        return globals()[n]


kb = K = _SelfModule()

def build_program(dq_engines=("vector", "gpsimd"), debug=False):
    nc = bass.Bass("TRN2", target_bir_lowering=False)
    st = ExitStack()
    with st:
        c = K.declare(nc, st)
        K.stage_consts(c)
        K.stage_mod(c)
        K.stage_sz(c)
        units, emit_unit = K.dequant_units(c)
        ui = 0
        p = c.p

        b_idb = K.buf(c, "cidb")
        b_psx = K.buf(c, "ps_x")
        b_fm = K.buf(c, "ar_fm")
        fm = K.ar(c, 13952, 4096, K.BF16)
        xts = [K.ar(c, 0, K.D), K.ar(c, 4096, K.D)]
        b_xts = [K.buf(c, "ar_xt0"), K.buf(c, "ar_xt1")]
        nh = K.ar(c, 8192, 4096, K.BF16)
        b_nh = K.buf(c, "ar_nh")
        for ti in range(kb.T // kb.MK):
            xt, b_xt = xts[ti % 2], b_xts[ti % 2]
            dep = [c.bc_done["u"], c.bc_done["sa"]]
            K.ln_tile(c, c.x_full[kb.MK * ti:kb.MK * (ti + 1), :], xt, b_xt, nh, b_nh,
                      (ti % 2) * 56, f"ls{ti}", 0, kb.D, extra_deps=dep)
            p.op("scalar", lambda s, ti=ti: s.dma_start(
                out=c.ha_tok[kb.MK * ti:kb.MK * (ti + 1), :], in_=nh),
                reads=[b_nh], writes=[K.buf(c, "d_tok")], dma=True, tag=f"ls{ti}_tok")
            for bt in range(4):
                for k8 in range(8):
                    k = bt * 8 + k8
                    p.op("tensor", lambda t, k=k, k8=k8: t.transpose(
                        c.ps_x[:, 128 * k8:128 * (k8 + 1)], nh[:, 128 * k:128 * (k + 1)],
                        c.cidb[:]),
                        reads=[b_nh, b_idb], writes=[b_psx], tag=f"ls{ti}_tr{k}")
                p.op("vector", lambda vv, bt=bt: vv.tensor_copy(
                    out=fm[:, 1024 * bt:1024 * (bt + 1)], in_=c.ps_x[:]),
                    reads=[b_psx], writes=[b_fm], tag=f"ls{ti}_te{bt}")
            p.op("scalar", lambda s, ti=ti: s.dma_start(
                out=c.ha_fm[ti].rearrange("(cd p) t -> p cd t", p=128),
                in_=fm.rearrange("p (cd t) -> p cd t", t=128)),
                reads=[b_fm], writes=[K.buf(c, "d_fm")], dma=True, tag=f"ls{ti}_fw")
            for _ in range(3):
                if ui < len(units):
                    emit_unit(units[ui], dq_engines[ui % len(dq_engines)])
                    ui += 1
        while ui < len(units):
            emit_unit(units[ui], dq_engines[ui % len(dq_engines)])
            ui += 1

        K.phase_barrier(c, "p2")
        K.stage_attention(c)
        K.phase_barrier(c, "p3")
        c.want_debug = debug
        K.stage_lnb_ffn(c)
        for w in c.out_done:
            p.wait("gpsimd", w)
        if debug:
            K.stage_debug(c)
        stats = p.emit(nc)
    return nc, stats


_NC_CACHE = {}


def kernel(**inputs):
    from concourse.bass_utils import run_bass_kernel_spmd
    if "nc" not in _NC_CACHE:
        _NC_CACHE["nc"] = build_program()[0]
    nc = _NC_CACHE["nc"]
    in_maps = [kb.host_core_inputs(i, inputs) for i in range(kb.NCORE)]
    res = run_bass_kernel_spmd(nc, in_maps, core_ids=list(range(kb.NCORE)))
    outs = [res.results[i]["out"] for i in range(kb.NCORE)]
    return kb.out_assemble(outs).reshape(1, kb.T, kb.D)




# version 03
# speedup vs baseline: 1.1579x; 1.1579x over previous
import sys
sys.path.insert(0, "/opt/trn_rl_repo")
import numpy as np
from contextlib import ExitStack

import concourse.bass as bass
import concourse.mybir as mybir


"""Mini dependency-tracking emitter for raw Bass programs.

Records ops (closures) with buffer-level read/write sets, derives all
RAW/WAW/WAR semaphore edges (same-engine included -- the simulator's race
detector requires them), assigns semaphore increments only to ops that are
actually depended on, then emits everything inside one nc.Block().

Semaphores: one per compute engine; DMA-issuing engines use N_DMA_SLOTS
rotating semaphores per engine (the race detector allows only one
outstanding DMA per semaphore, so slot s's next DMA first waits for slot
s's previous DMA -> up to N_DMA_SLOTS DMAs in flight per queue).
"""

import os

ENGINES = ("tensor", "vector", "scalar", "gpsimd", "sync")
N_DMA_SLOTS = 8

SIM_STRICT = os.environ.get("BASS_SIM_FP64") == "1"


def _skip_wait(d, o):
    return (not SIM_STRICT) and d.sem_key == "tensor" and o.sem_key == "tensor"


class Buf:
    __slots__ = ("name", "writer", "readers")

    def __init__(self, name=""):
        self.name = name
        self.writer = None
        self.readers = []

    def __repr__(self):
        return f"Buf({self.name})"


class Op:
    __slots__ = ("engine", "emit_fn", "deps", "inc", "cum", "needs_inc", "tag",
                 "gidx", "sem_key")

    def __repr__(self):
        return f"Op({self.engine}:{self.tag}@{self.gidx})"


class Prog:
    def __init__(self):
        self.ops = {e: [] for e in ENGINES}
        self.n = 0
        self.final_waits = []
        self._dma_count = {e: 0 for e in ENGINES}
        self._last_dma_on_slot = {}

    def buf(self, name=""):
        return Buf(name)

    def op(self, engine, emit_fn, reads=(), writes=(), dma=False, deps=(), tag=""):
        assert engine in ENGINES, engine
        o = Op()
        o.engine = engine
        o.emit_fn = emit_fn
        o.inc = 16 if dma else 1
        o.tag = tag
        o.gidx = self.n
        o.needs_inc = False
        self.n += 1
        if dma:
            slot = self._dma_count[engine] % N_DMA_SLOTS
            self._dma_count[engine] += 1
            o.sem_key = (engine, slot)
        else:
            o.sem_key = engine
        dep = set(deps)
        for b in reads:
            if b.writer is not None:
                dep.add(b.writer)
        for b in writes:
            if b.writer is not None:
                dep.add(b.writer)
            dep.update(b.readers)
        if dma:
            prev = self._last_dma_on_slot.get(o.sem_key)
            if prev is not None:
                dep.add(prev)
            self._last_dma_on_slot[o.sem_key] = o
        dep.discard(o)
        o.deps = dep
        for b in reads:
            b.readers.append(o)
        for b in writes:
            b.writer = o
            b.readers = []
        self.ops[engine].append(o)
        return o

    def wait(self, engine, op):
        self.final_waits.append((engine, op))

    def emit(self, nc):
        keys = set()
        for e in ENGINES:
            for o in self.ops[e]:
                keys.add(o.sem_key)
        sems = {}
        ctxs = []
        for k in sorted(keys, key=str):
            name = f"sem_{k}" if isinstance(k, str) else f"sem_{k[0]}_{k[1]}"
            ctx = nc.semaphore(name)
            sems[k] = ctx.__enter__()
            ctxs.append(ctx)
        for e in ENGINES:
            for o in self.ops[e]:
                for d in o.deps:
                    if not _skip_wait(d, o):
                        d.needs_inc = True
        for fe, fo in self.final_waits:
            fo.needs_inc = True
        cum = {k: 0 for k in keys}
        for e in ENGINES:
            for o in self.ops[e]:
                if o.needs_inc:
                    cum[o.sem_key] += o.inc
                o.cum = cum[o.sem_key]
        stats = {e: {"ops": len(self.ops[e]),
                     "incs": sum(1 for o in self.ops[e] if o.needs_inc),
                     "waits": 0} for e in ENGINES}
        block_ctx = nc.Block()
        block = block_ctx.__enter__()
        engines_used = [e for e in ENGINES
                        if self.ops[e] or any(fe == e for fe, _ in self.final_waits)]
        for e in engines_used:
            def make_body(e):
                def body(eng):
                    waited = {k: 0 for k in keys}
                    for o in self.ops[e]:
                        need = {}
                        for d in o.deps:
                            if _skip_wait(d, o):
                                continue
                            if d.cum > need.get(d.sem_key, 0):
                                need[d.sem_key] = d.cum
                        for k in sorted(need, key=str):
                            v = need[k]
                            if v > waited[k]:
                                eng.wait_ge(sems[k], v)
                                waited[k] = v
                                stats[e]["waits"] += 1
                        inst = o.emit_fn(eng)
                        if o.needs_inc:
                            inst.then_inc(sems[o.sem_key], o.inc)
                    for fe, fo in self.final_waits:
                        if fe == e and fo.cum > waited[fo.sem_key]:
                            eng.wait_ge(sems[fo.sem_key], fo.cum)
                            waited[fo.sem_key] = fo.cum
                return body

            getattr(block, e)(make_body(e))
        block_ctx.__exit__(None, None, None)
        for ctx in reversed(ctxs):
            ctx.__exit__(None, None, None)
        return stats




T, D, H, HD, INTER = 8192, 4096, 4, 1024, 11008
NCORE = 8
CH = 256
NCH = T // CH
DCH = D // 128
MK = 128
PAD_SPAN = [16, 32, 48, 64]
NSLOT = sum(PAD_SPAN)
EPS = 1e-5
R_RUN = 4
TOK = CH * 4


def sigma_of_core(i):
    return (i, 7 - i, i, 7 - i)


def chunks_of_core(i):
    s = sigma_of_core(i)
    return [8 * q + s[q] for q in range(4)]


def slot_sched():
    return [[(q, m) for m in range(PAD_SPAN[q])] for q in range(4)]


def perm_cols(nk):
    ks = []
    nrow = nk // 8
    for R in range(0, nrow, 128):
        rows = min(128, nrow - R)
        for j in range(8):
            for r in range(rows):
                ks.append(8 * (R + r) + j)
    return np.array(ks, dtype=np.int64)


def perm_chunk_list(nk):
    out = []
    pos = 0
    nrow = nk // 8
    for R in range(0, nrow, 128):
        rows = min(128, nrow - R)
        for j in range(8):
            out.append((R, rows, j, pos))
            pos += rows
    return out


def s_rep_host(scales, nk):
    nrow = nk // 8
    idx = (8 * np.arange(nrow)) // 128
    return np.asarray(scales, dtype=np.float32)[idx]


def host_core_inputs(i, inputs):
    import ml_dtypes
    x = np.ascontiguousarray(np.asarray(inputs["L0_attn_x"], dtype=np.float32).reshape(T, D))
    chunks = chunks_of_core(i)
    x_own = np.concatenate([x[c * CH:(c + 1) * CH] for c in chunks], axis=0)
    masks = np.zeros((NSLOT, 128, CH), dtype=np.float32)
    k = 0
    for q in range(4):
        c = chunks[q]
        q_glob = c * CH + np.arange(CH)[None, :]
        for m in range(PAD_SPAN[q]):
            s_glob = m * MK + np.arange(MK)[:, None]
            masks[k] = (s_glob <= q_glob)
            k += 1
    masks = masks.astype(ml_dtypes.bfloat16)

    pd = perm_cols(INTER)
    m_ = {
        "x_full": x,
        "x_own": x_own,
        "masks": masks,
        "wmod": np.asarray(inputs["L0_wmod"], dtype=np.float32),
        "cvec": np.asarray(inputs["L0_c"], dtype=np.float32),
        "bv_q": np.asarray(inputs["L0_attn_bv_q"], dtype=np.float32),
        "bv_k": np.asarray(inputs["L0_attn_bv_k"], dtype=np.float32),
        "bv_v": np.asarray(inputs["L0_attn_bv_v"], dtype=np.float32),
        "qw_g": np.asarray(inputs["L0_ffn_gate_qweight"], dtype=np.int32),
        "qw_u": np.asarray(inputs["L0_ffn_up_qweight"], dtype=np.int32),
        "qw_d": np.asarray(inputs["L0_ffn_down_qweight"], dtype=np.int32),
        "qz_g": np.asarray(inputs["L0_ffn_gate_qzeros"], dtype=np.int32),
        "qz_u": np.asarray(inputs["L0_ffn_up_qzeros"], dtype=np.int32),
        "qz_d": np.asarray(inputs["L0_ffn_down_qzeros"], dtype=np.int32),
        "srep_g": s_rep_host(inputs["L0_ffn_gate_scales"], D)[:, pd],
        "srep_u": s_rep_host(inputs["L0_ffn_up_scales"], D)[:, pd],
        "srep_d": s_rep_host(inputs["L0_ffn_down_scales"], INTER),
        "sc_g": np.asarray(inputs["L0_ffn_gate_scales"], dtype=np.float32)[:, pd],
        "sc_u": np.asarray(inputs["L0_ffn_up_scales"], dtype=np.float32)[:, pd],
        "sc_d": np.asarray(inputs["L0_ffn_down_scales"], dtype=np.float32),
    }
    return {k2: np.ascontiguousarray(v) for k2, v in m_.items()}


def out_assemble(results, dtype=np.float32):
    full = np.empty((T, D), dtype=dtype)
    for i in range(NCORE):
        chunks = chunks_of_core(i)
        r = np.asarray(results[i]).reshape(TOK, D)
        for q, c in enumerate(chunks):
            full[c * CH:(c + 1) * CH] = r[q * CH:(q + 1) * CH]
    return full


"""Program builder for the adaLN megakernel (v1, simple-modulation design).

Arena map (f32 units per partition, arena [128, 36864]):
  P1:  XT0 0:4096 | XT1 4096:8192 | NH 8192:10240 (bf16 view) | DQ 10400:12192
       FM 12288:14336 | WMB 14336:16384 | MROW 16384:20480
  P2:  XT0 | QPR 4096:8192 | NH 8192:10240 | AMASK 10240:10368 | DQ (tail)
       KT 12288:16384 (2 slots) | VT 16384:26624 (5 slots) | X2S 26624:34816
       AT 34816:36864
  P3:  XT0 | HT 12288:28672 | WG 28672:30720 | WU 30720:32768
       SIL 32768:33792 | VST 33792:34304 | SZT 34304:34560 | VCH 34560:34816
       (down phase: DST 28672:30720, FRG 30720:31232, X2C 31232:31744,
        WDT 33792:34304)
All cross-phase reuse is protected by phase-master barrier deps.
"""

F32 = mybir.dt.float32
BF16 = mybir.dt.bfloat16
I32 = mybir.dt.int32
AF = mybir.ActivationFunctionType
OP = mybir.AluOpType
AX = mybir.AxisListType

NMODV = 6
VG = {"sa": 0, "ca": 1, "ga": 2, "sb": 3, "cb": 4, "gb": 5,
      "u": 6, "u2": 7, "qkb": 8, "vb": 9, "gavb": 10, "tmp": 11}
ARENA_F = 36864


class Ctx:
    pass


def vcol(c, name, k=0, n=None):
    base = VG[name] * DCH
    if n is None:
        n = DCH - k if k == 0 else 1
    return c.vecs[:, base + k:base + k + n]


def ar(c, off, cols, dt=F32):
    if dt == F32 or dt == I32:
        v = c.arena[:, off:off + cols]
        return v if dt == F32 else v.bitcast(I32)
    return c.arena[:, off:off + (cols + 1) // 2].bitcast(BF16)


def declare(nc, st):
    import ml_dtypes
    c = Ctx()
    c.nc = nc
    c.p = Prog()
    e = st.enter_context
    dt_ = nc.dram_tensor

    c.x_full = dt_("x_full", [T, D], F32, kind="ExternalInput").ap()
    c.x_own = dt_("x_own", [TOK, D], F32, kind="ExternalInput").ap()
    c.masks = dt_("masks", [NSLOT, MK, CH], BF16, kind="ExternalInput").ap()
    c.wmod = dt_("wmod", [D, NMODV * D], F32, kind="ExternalInput").ap()
    c.cvec = dt_("cvec", [1, D], F32, kind="ExternalInput").ap()
    c.bv = {k: dt_("bv_" + k, [H, HD], F32, kind="ExternalInput").ap() for k in "qkv"}
    c.qw = {k: dt_("qw_" + k, [{"g": D, "u": D, "d": INTER}[k] // 8,
                               {"g": INTER, "u": INTER, "d": D}[k]], I32,
                   kind="ExternalInput").ap() for k in "gud"}
    c.qz = {k: dt_("qz_" + k, [{"g": 32, "u": 32, "d": 86}[k],
                               {"g": INTER // 8, "u": INTER // 8, "d": D // 8}[k]], I32,
                   kind="ExternalInput").ap() for k in "gud"}
    c.srep = {k: dt_("srep_" + k, [{"g": D // 8, "u": D // 8, "d": INTER // 8}[k],
                                   {"g": INTER, "u": INTER, "d": D}[k]], F32,
                     kind="ExternalInput").ap() for k in "gud"}
    c.scp = {k: dt_("sc_" + k, [{"g": 32, "u": 32, "d": 86}[k],
                                {"g": INTER, "u": INTER, "d": D}[k]], F32,
                    kind="ExternalInput").ap() for k in "gud"}
    c.out = dt_("out", [TOK, D], F32, kind="ExternalOutput").ap()

    c.ha_fm = dt_("ha_fm", [T // MK, D, MK], BF16).ap()
    c.ha_tok = dt_("ha_tok", [T, D], BF16).ap()
    c.x2_dram = dt_("x2_dram", [TOK, D], F32).ap()
    c.wdq = {k: dt_("wdq_" + k, [{"g": D, "u": D, "d": INTER}[k],
                                 {"g": INTER, "u": INTER, "d": D}[k]], BF16).ap()
             for k in "gud"}
    c.v_dram = dt_("v_dram", [INTER, TOK], BF16).ap()
    c.sz_dram = {k: dt_("szd_" + k, [{"g": 32, "u": 32, "d": 86}[k],
                                     {"g": INTER, "u": INTER, "d": D}[k]], BF16).ap()
                 for k in "gud"}

    c.identf = nc.inline_tensor(np.eye(128, dtype=np.float32), "identf").ap()
    c.identb = nc.inline_tensor(np.eye(128, dtype=np.float32).astype(ml_dtypes.bfloat16), "identb").ap()
    c.ones11 = nc.inline_tensor(np.ones((1, 1), dtype=np.float32), "ones11").ap()
    c.onesf_row = nc.inline_tensor(np.ones((1, 128), dtype=np.float32), "onesf_row").ap()
    indh = np.zeros((4, 128, 32), dtype=np.float32)
    for R in range(4):
        indh[R, np.arange(128), 8 * R + np.arange(128) // 16] = 1.0
    c.indh = nc.inline_tensor(indh.astype(ml_dtypes.bfloat16).reshape(4 * 128, 32), "indh").ap()
    indv = np.zeros((11, 128, 86), dtype=np.float32)
    for R in range(11):
        rows = 128 if R < 10 else 96
        indv[R, np.arange(rows), 8 * R + np.arange(rows) // 16] = 1.0
    c.indv = nc.inline_tensor(indv.astype(ml_dtypes.bfloat16).reshape(11 * 128, 86), "indv").ap()

    c.arena = e(nc.sbuf_tensor("arena", [128, ARENA_F], F32))
    c.cidf = e(nc.sbuf_tensor("cidf", [128, 128], F32))
    c.cidb = e(nc.sbuf_tensor("cidb", [128, 128], BF16))
    c.cones = e(nc.sbuf_tensor("cones", [1, 132], F32))
    c.cindh = e(nc.sbuf_tensor("cindh", [128, 4 * 32], BF16))
    c.cindv = e(nc.sbuf_tensor("cindv", [128, 11 * 86], BF16))
    c.vecs = e(nc.sbuf_tensor("vecs", [128, 12 * DCH], F32))
    c.bcast = e(nc.sbuf_tensor("bcast", [128, 3 * D], BF16))
    c.lnst = e(nc.sbuf_tensor("lnst", [128, 128], F32))
    c.tiny = e(nc.sbuf_tensor("tiny", [128, 64], F32))
    c.ccols = e(nc.sbuf_tensor("ccols", [128, DCH], F32))
    c.bvs = e(nc.sbuf_tensor("bvs", [128, 3 * DCH], F32))
    c.hsumh = e(nc.sbuf_tensor("hsumh", [32, 1024], BF16))
    c.hsuml = e(nc.sbuf_tensor("hsuml", [32, 1024], BF16))
    c.hsumv = e(nc.sbuf_tensor("hsumv", [86, 1024], BF16))

    c.ps_a = e(nc.psum_tensor("ps_a", [128, 1024], F32))
    c.ps_s = e(nc.psum_tensor("ps_s", [128, 1024], F32))
    c.ps_t = e(nc.psum_tensor("ps_t", [128, 512], F32))
    c.ps_u = e(nc.psum_tensor("ps_u", [128, 512], F32))
    c.ps_x = e(nc.psum_tensor("ps_x", [128, 1024], BF16))

    c.B = {}
    c.pdeps = []
    return c


def buf(c, name):
    if name not in c.B:
        c.B[name] = c.p.buf(name)
    return c.B[name]


def phase_barrier(c, name):
    old = [b for n, b in c.B.items() if n.startswith("ar_")]
    mb = c.p.op("gpsimd", lambda g: g.memset(c.tiny[0:1, 63:64], 0.0),
                reads=old, writes=[buf(c, "pb_" + name)], tag="pb_" + name)
    c.pdeps = [mb]
    return mb


def stage_consts(c):
    p = c.p
    for fn, name in [
        (lambda g: g.dma_start(out=c.cidf[:], in_=c.identf), "cidf"),
        (lambda g: g.dma_start(out=c.cidb[:], in_=c.identb), "cidb"),
        (lambda g: g.dma_start(out=c.cones[0:1, 0:128], in_=c.onesf_row), "cones"),
        (lambda g: g.dma_start(out=c.cones[0:1, 128:129], in_=c.ones11), "cone1"),
        (lambda g: g.dma_start(out=c.cindh[:].rearrange("p (r g) -> p r g", r=4),
                               in_=c.indh.rearrange("(r p) g -> p r g", p=128)), "cindh"),
        (lambda g: g.dma_start(out=c.cindv[:].rearrange("p (r g) -> p r g", r=11),
                               in_=c.indv.rearrange("(r p) g -> p r g", p=128)), "cindv"),
    ]:
        p.op("gpsimd", fn, writes=[buf(c, name)], dma=True, tag="ld_" + name)


def stage_mod(c):
    p = c.p
    b_c = buf(c, "c_row")
    b_pst = buf(c, "ps_t")
    b_ccols = buf(c, "ccols")
    b_vecs = buf(c, "vecs")
    b_tiny = buf(c, "tiny")
    b_bvs = buf(c, "bvs")
    b_one = buf(c, "cone1")
    b_idf = buf(c, "cidf")
    b_onesr = buf(c, "cones")
    ones1 = c.cones[0:1, 128:129]
    xt1 = ar(c, 4096, D)

    b_xt1 = buf(c, "ar_xt1")
    p.op("sync", lambda g: g.dma_start(out=xt1[0:1, 0:D], in_=c.cvec),
         writes=[b_xt1], dma=True, tag="ld_c")
    for k in range(DCH):
        p.op("tensor", lambda t, k=k: t.matmul(
            c.ps_t[:, k:k + 1], xt1[0:1, 128 * k:128 * (k + 1)], ones1,
            start=True, stop=True),
            reads=[b_xt1, b_one], writes=[b_pst], tag=f"ccol{k}")
    p.op("scalar", lambda s: s.activation(out=c.ccols[:, 0:DCH], in_=c.ps_t[:, 0:DCH], func=AF.Silu),
         reads=[b_pst], writes=[b_ccols], tag="silu_c")

    b_wm = [buf(c, "ar_wmb0"), buf(c, "ar_wmb1")]
    b_mrow = buf(c, "ar_mrow")
    wmb2 = [ar(c, 16000, 2048), ar(c, 18048, 2048)]
    mrow = ar(c, 20096, 4096)
    c.mod_done = {}
    psacc = [(c.ps_a, 0), (c.ps_a, 512), (c.ps_s, 0), (c.ps_s, 512)]
    mv_ps = [buf(c, f"mv_ps{i}") for i in range(4)]
    for v in range(NMODV):
        name = {0: "sa", 1: "ca", 2: "ga", 3: "sb", 4: "cb", 5: "gb"}[v]
        for blk in range(2):
            col0 = v * D + blk * 2048
            for k in range(DCH):
                hb = (v * 2 * DCH + blk * DCH + k) % 2
                p.op("sync", lambda g, k=k, col0=col0, hb=hb: g.dma_start(
                    out=wmb2[hb], in_=c.wmod[128 * k:128 * (k + 1), col0:col0 + 2048]),
                    writes=[b_wm[hb]], dma=True, tag=f"wm{v}_{blk}_{k}")
                for s4 in range(4):
                    ps, off = psacc[s4]
                    p.op("tensor", lambda t, k=k, s4=s4, ps=ps, off=off, hb=hb: t.matmul(
                        ps[0:1, off:off + 512], c.ccols[:, k:k + 1],
                        wmb2[hb][:, s4 * 512:(s4 + 1) * 512],
                        start=(k == 0), stop=(k == DCH - 1)),
                        reads=[b_wm[hb], b_ccols], writes=[mv_ps[s4]], tag=f"mv{v}_{blk}_{k}_{s4}")
            for s4 in range(4):
                ps, off = psacc[s4]
                p.op("vector", lambda vv, s4=s4, ps=ps, off=off, blk=blk: vv.tensor_copy(
                    out=mrow[0:1, blk * 2048 + s4 * 512:blk * 2048 + (s4 + 1) * 512],
                    in_=ps[0:1, off:off + 512]),
                    reads=[mv_ps[s4]], writes=[b_mrow], tag=f"mvev{v}_{blk}_{s4}")
        for half16 in range(2):
            for i in range(16):
                k = half16 * 16 + i
                p.op("tensor", lambda t, k=k, i=i: t.matmul(
                    c.ps_t[:, 32 + i:33 + i], mrow[0:1, 128 * k:128 * (k + 1)], ones1,
                    start=True, stop=True),
                    reads=[b_mrow, b_one], writes=[b_pst], tag=f"mvc{v}_{k}")
            last = p.op("vector", lambda vv, name=name, half16=half16: vv.tensor_copy(
                out=c.vecs[:, VG[name] * DCH + half16 * 16:VG[name] * DCH + half16 * 16 + 16],
                in_=c.ps_t[:, 32:48]),
                reads=[b_pst], writes=[b_vecs], tag=f"mvout{v}_{half16}")
        c.mod_done[name] = last

    for bi, kk in enumerate("qkv"):
        for h in range(H):
            p.op("gpsimd", lambda g, kk=kk, h=h: g.dma_start(
                out=xt1[0:1, 0:HD], in_=c.bv[kk][h:h + 1, :]),
                writes=[b_xt1], dma=True, tag=f"ldbv{kk}{h}")
            p.op("vector", lambda vv, bi=bi, h=h: vv.reduce_sum(
                out=c.tiny[0:1, 8 + bi * 4 + h:9 + bi * 4 + h], in_=xt1[0:1, 0:HD],
                axis=AX.X, apply_absolute_value=True),
                reads=[b_xt1], writes=[b_tiny], tag=f"alpha{kk}{h}")
            p.op("vector", lambda vv: vv.tensor_scalar(
                out=xt1[0:1, 0:HD], in0=xt1[0:1, 0:HD],
                scalar1=0.0, scalar2=2.0, op0=OP.is_ge, op1=OP.mult),
                reads=[b_xt1], writes=[b_xt1], tag=f"sge{kk}{h}")
            p.op("vector", lambda vv, bi=bi, h=h: vv.tensor_scalar(
                out=xt1[0:1, 0:HD], in0=xt1[0:1, 0:HD],
                scalar1=-1.0, scalar2=c.tiny[0:1, 8 + bi * 4 + h:9 + bi * 4 + h],
                op0=OP.add, op1=OP.mult),
                reads=[b_xt1, b_tiny], writes=[b_xt1], tag=f"sgn{kk}{h}")
            p.op("vector", lambda vv: vv.tensor_scalar(
                out=xt1[0:1, 0:HD], in0=xt1[0:1, 0:HD],
                scalar1=1.0 / HD, scalar2=None, op0=OP.mult),
                reads=[b_xt1], writes=[b_xt1], tag=f"bsc{kk}{h}")
            for sl in range(8):
                p.op("tensor", lambda t, sl=sl: t.matmul(
                    c.ps_t[:, 48 + sl:49 + sl],
                    xt1[0:1, 128 * sl:128 * (sl + 1)], ones1,
                    start=True, stop=True),
                    reads=[b_xt1, b_one], writes=[b_pst], tag=f"bvc{kk}{h}{sl}")
            p.op("vector", lambda vv, bi=bi, h=h: vv.tensor_copy(
                out=c.bvs[:, bi * DCH + h * 8:bi * DCH + h * 8 + 8],
                in_=c.ps_t[:, 48:56]),
                reads=[b_pst], writes=[b_bvs], tag=f"bvce{kk}{h}")

    def vts(out, in0, s1, s2, o0, o1, deps, tag):
        if s2 is None:
            return p.op("vector", lambda vv: vv.tensor_scalar(
                out=out, in0=in0, scalar1=s1, scalar2=None, op0=o0),
                reads=[b_vecs, b_bvs], writes=[b_vecs], deps=deps, tag=tag)
        return p.op("vector", lambda vv: vv.tensor_scalar(
            out=out, in0=in0, scalar1=s1, scalar2=s2, op0=o0, op1=o1),
            reads=[b_vecs, b_bvs], writes=[b_vecs], deps=deps, tag=tag)

    def vtt(out, in0, in1, op, deps, tag):
        return p.op("vector", lambda vv: vv.tensor_tensor(out=out, in0=in0, in1=in1, op=op),
                    reads=[b_vecs, b_bvs], writes=[b_vecs], deps=deps, tag=tag)

    u_op = vts(vcol(c, "u"), vcol(c, "ca"), 1.0, None, OP.add, OP.bypass, [c.mod_done["ca"]], "mk_u")
    u2_op = vts(vcol(c, "u2"), vcol(c, "cb"), 1.0, None, OP.add, OP.bypass, [c.mod_done["cb"]], "mk_u2")
    for gn in ("ga", "gb"):
        dep = [c.mod_done[gn]]
        vts(vcol(c, "tmp"), vcol(c, gn), -1.0, None, OP.mult, None, dep, f"cl1a{gn}")
        vtt(vcol(c, "tmp"), vcol(c, "tmp"), vcol(c, gn), OP.max, [], f"cl1b{gn}")
        vts(vcol(c, "tmp"), vcol(c, "tmp"), 0.25, None, OP.max, None, [], f"cl1c{gn}")
        vts(vcol(c, gn), vcol(c, gn), 0.0, 2.0, OP.is_ge, OP.mult, dep, f"cl2{gn}")
        vts(vcol(c, gn), vcol(c, gn), -1.0, None, OP.add, OP.bypass, [], f"cl3{gn}")
        vtt(vcol(c, gn), vcol(c, gn), vcol(c, "tmp"), OP.mult, [], f"cl4{gn}")
    vtt(vcol(c, "qkb"), c.bvs[:, 0:DCH], c.bvs[:, DCH:2 * DCH], OP.mult, [], "mk_qkb")
    p.op("vector", lambda vv: vv.tensor_copy(out=vcol(c, "vb"), in_=c.bvs[:, 2 * DCH:3 * DCH]),
         reads=[b_bvs], writes=[b_vecs], tag="mk_vb")
    gavb_op = vtt(vcol(c, "gavb"), vcol(c, "ga"), vcol(c, "vb"), OP.mult, [], "mk_gavb")

    b_bc = buf(c, "bcast")
    c.bc_done = {}
    for gi, (gn, dep) in enumerate((("u", [u_op]), ("sa", [c.mod_done["sa"]]),
                                    ("gavb", [gavb_op]))):
        for k in range(DCH):
            p.op("tensor", lambda t, gn=gn, k=k: t.matmul(
                c.ps_t[0:1, 0:128], vcol(c, gn, k, 1), c.cidf[:],
                start=True, stop=True),
                reads=[b_vecs, b_idf], writes=[b_pst], deps=dep, tag=f"r{gn}{k}")
            p.op("vector", lambda vv, k=k: vv.tensor_copy(
                out=mrow[0:1, 128 * k:128 * (k + 1)], in_=c.ps_t[0:1, 0:128]),
                reads=[b_pst], writes=[b_mrow], tag=f"re{gn}{k}")
        lastb = None
        for s in range(8):
            p.op("tensor", lambda t, s=s: t.matmul(
                c.ps_t[:, 0:512], c.cones[0:1, 0:128], mrow[0:1, 512 * s:512 * (s + 1)],
                start=True, stop=True),
                reads=[b_mrow, b_onesr], writes=[b_pst], tag=f"b{gn}{s}")
            lastb = p.op("vector", lambda vv, gi=gi, s=s: vv.tensor_copy(
                out=c.bcast[:, gi * D + 512 * s:gi * D + 512 * (s + 1)], in_=c.ps_t[:, 0:512]),
                reads=[b_pst], writes=[b_bc], tag=f"be{gn}{s}")
        c.bc_done[gn] = lastb
    return c


def ln_tile(c, xsrc, xt, b_xt, nh, b_nh, stcol, tag, uoff, soff, extra_deps=()):
    p = c.p
    b_st = buf(c, "lnst")
    b_bc = buf(c, "bcast")
    ld = p.op("scalar", lambda s: s.dma_start(out=xt, in_=xsrc),
              writes=[b_xt], deps=list(extra_deps) + c.pdeps, dma=True, tag=f"{tag}_ld")
    for i in range(8):
        p.op("vector", lambda vv, i=i: vv.bn_stats(
            out=c.lnst[:, stcol + 6 * i:stcol + 6 * (i + 1)],
            in_=xt[:, 512 * i:512 * (i + 1)]),
            reads=[b_xt], writes=[b_st], tag=f"{tag}_bn{i}")
    p.op("vector", lambda vv: vv.bn_aggr(
        out=c.lnst[:, stcol + 48:stcol + 50], in_=c.lnst[:, stcol:stcol + 48]),
        reads=[b_st], writes=[b_st], tag=f"{tag}_ag")
    p.op("vector", lambda vv: vv.tensor_scalar(
        out=c.lnst[:, stcol + 53:stcol + 54], in0=c.lnst[:, stcol + 49:stcol + 50],
        scalar1=EPS, scalar2=None, op0=OP.add),
        reads=[b_st], writes=[b_st], tag=f"{tag}_ep")
    p.op("scalar", lambda s: s.activation(
        out=c.lnst[:, stcol + 50:stcol + 51], in_=c.lnst[:, stcol + 53:stcol + 54],
        func=AF.Sqrt),
        reads=[b_st], writes=[b_st], tag=f"{tag}_sq")
    p.op("vector", lambda vv: vv.reciprocal(
        out=c.lnst[:, stcol + 51:stcol + 52], in_=c.lnst[:, stcol + 50:stcol + 51]),
        reads=[b_st], writes=[b_st], tag=f"{tag}_rc")
    p.op("vector", lambda vv: vv.tensor_scalar(
        out=c.lnst[:, stcol + 52:stcol + 53], in0=c.lnst[:, stcol + 48:stcol + 49],
        scalar1=c.lnst[:, stcol + 51:stcol + 52], scalar2=-1.0, op0=OP.mult, op1=OP.mult),
        reads=[b_st], writes=[b_st], tag=f"{tag}_nb")
    p.op("vector", lambda vv: vv.tensor_scalar(
        out=nh, in0=xt, scalar1=c.lnst[:, stcol + 51:stcol + 52],
        scalar2=c.lnst[:, stcol + 52:stcol + 53], op0=OP.mult, op1=OP.add),
        reads=[b_xt, b_st], writes=[b_nh], tag=f"{tag}_nm")
    p.op("vector", lambda vv: vv.tensor_tensor(
        out=nh, in0=nh, in1=c.bcast[:, uoff:uoff + D], op=OP.mult),
        reads=[b_nh, b_bc], writes=[b_nh], tag=f"{tag}_mu")
    p.op("vector", lambda vv: vv.tensor_tensor(
        out=nh, in0=nh, in1=c.bcast[:, soff:soff + D], op=OP.add),
        reads=[b_nh, b_bc], writes=[b_nh], tag=f"{tag}_ms")
    return ld


def stage_lnspan(c):
    p = c.p
    b_idb = buf(c, "cidb")
    b_psx = buf(c, "ps_x")
    b_fm = buf(c, "ar_fm")
    fm = ar(c, 13952, 4096, BF16)
    xts = [ar(c, 0, D), ar(c, 4096, D)]
    b_xts = [buf(c, "ar_xt0"), buf(c, "ar_xt1")]
    nh = ar(c, 8192, 4096, BF16)
    b_nh = buf(c, "ar_nh")
    c.ha_done = []
    for ti in range(T // MK):
        xt, b_xt = xts[ti % 2], b_xts[ti % 2]
        dep = [c.bc_done["u"], c.bc_done["sa"]] if ti == 0 else []
        ln_tile(c, c.x_full[MK * ti:MK * (ti + 1), :], xt, b_xt, nh, b_nh,
                (ti % 2) * 56, f"ls{ti}", 0, D, extra_deps=dep)
        tok = p.op("scalar", lambda s, ti=ti: s.dma_start(
            out=c.ha_tok[MK * ti:MK * (ti + 1), :], in_=nh),
            reads=[b_nh], writes=[buf(c, "d_tok")], dma=True, tag=f"ls{ti}_tok")
        for bt in range(4):
            for k8 in range(8):
                k = bt * 8 + k8
                p.op("tensor", lambda t, k=k, k8=k8: t.transpose(
                    c.ps_x[:, 128 * k8:128 * (k8 + 1)], nh[:, 128 * k:128 * (k + 1)],
                    c.cidb[:]),
                    reads=[b_nh, b_idb], writes=[b_psx], tag=f"ls{ti}_tr{k}")
            p.op("vector", lambda vv, bt=bt: vv.tensor_copy(
                out=fm[:, 1024 * bt:1024 * (bt + 1)], in_=c.ps_x[:]),
                reads=[b_psx], writes=[b_fm], tag=f"ls{ti}_te{bt}")
        fmw = p.op("scalar", lambda s, ti=ti: s.dma_start(
            out=c.ha_fm[ti].rearrange("(cd p) t -> p cd t", p=128),
            in_=fm.rearrange("p (cd t) -> p cd t", t=128)),
            reads=[b_fm], writes=[buf(c, "d_fm")], dma=True, tag=f"ls{ti}_fw")
        c.ha_done.append((tok, fmw))


def dequant_units(c):
    p = c.p
    sr_f = ar(c, 10368, 1024)
    pk_i = ar(c, 11392, 1024, I32)
    nb_i = ar(c, 12416, 512, I32)
    nb_f = ar(c, 12928, 512)
    ou_b = ar(c, 13440, 1024, BF16)
    b_sr, b_pk, b_ni, b_nf, b_ou = (buf(c, "ar_dqsr"), buf(c, "ar_dqpk"),
                                    buf(c, "ar_dqni"), buf(c, "ar_dqnf"),
                                    buf(c, "ar_dqou"))
    units = []
    for mat in "gud":
        nk = D if mat in "gu" else INTER
        nn = INTER if mat in "gu" else D
        nrow = nk // 8
        rtiles = [(R, min(128, nrow - R)) for R in range(0, nrow, 128)]
        for Ri, (R0, rows) in enumerate(rtiles):
            for blk in range((nn + 1023) // 1024):
                ncols = min(1024, nn - 1024 * blk)
                units.append((mat, Ri, R0, rows, blk, ncols))

    def emit_unit(u, eng):
        mat, Ri, R0, rows, blk, ncols = u
        tg = f"dq{mat}{Ri}_{blk}"
        p.op("sync", lambda g: g.dma_start(
            out=pk_i[0:rows, 0:ncols], in_=c.qw[mat][R0:R0 + rows, 1024 * blk:1024 * blk + ncols]),
            writes=[b_pk], deps=c.pdeps, dma=True, tag=tg + "pk")
        p.op("sync", lambda g: g.dma_start(
            out=sr_f[0:rows, 0:ncols], in_=c.srep[mat][R0:R0 + rows, 1024 * blk:1024 * blk + ncols]),
            writes=[b_sr], deps=c.pdeps, dma=True, tag=tg + "sr")
        nh2 = (ncols + 511) // 512
        for j in range(8):
            for hf in range(nh2):
                hc = min(512, ncols - 512 * hf)
                p.op("vector", lambda E, j=j, hf=hf, hc=hc: E.tensor_scalar(
                    out=nb_i[0:rows, 0:hc], in0=pk_i[0:rows, 512 * hf:512 * hf + hc],
                    scalar1=4 * j, scalar2=15, op0=OP.logical_shift_right,
                    op1=OP.bitwise_and),
                    reads=[b_pk], writes=[b_ni], tag=f"{tg}_{j}{hf}x")
                p.op("vector", lambda E, hc=hc: E.tensor_copy(
                    out=nb_f[0:rows, 0:hc], in_=nb_i[0:rows, 0:hc]),
                    reads=[b_ni], writes=[b_nf], tag=f"{tg}_{j}{hf}c")
                if mat in "gu":
                    rw = ncols // 8
                    r0h = 64 * hf if ncols == 1024 else 64 * hf
                    rcnt = hc // 8
                    p.op(eng, lambda E, hf=hf, rw=rw, r0h=r0h, rcnt=rcnt, ncols=ncols: E.tensor_tensor(
                        out=ou_b[0:rows, 0:ncols].rearrange("p (j2 r2) -> p r2 j2", j2=8
                                                      )[:, r0h:r0h + rcnt, :],
                        in0=nb_f[0:rows, 0:rcnt * 8].rearrange("p (r2 j2) -> p r2 j2", j2=8),
                        in1=sr_f[0:rows, 0:ncols].rearrange("p (j2 r2) -> p r2 j2", j2=8
                                                      )[:, r0h:r0h + rcnt, :],
                        op=OP.mult),
                        reads=[b_nf, b_sr], writes=[b_ou], tag=f"{tg}_{j}{hf}m")
                else:
                    p.op(eng, lambda E, hf=hf, hc=hc: E.tensor_tensor(
                        out=ou_b[0:rows, 512 * hf:512 * hf + hc],
                        in0=nb_f[0:rows, 0:hc],
                        in1=sr_f[0:rows, 512 * hf:512 * hf + hc], op=OP.mult),
                        reads=[b_nf, b_sr], writes=[b_ou], tag=f"{tg}_{j}{hf}m")
            p.op("sync", lambda g, j=j: g.dma_start(
                out=c.wdq[mat][1024 * Ri + rows * j:1024 * Ri + rows * (j + 1),
                               1024 * blk:1024 * blk + ncols],
                in_=ou_b[0:rows, 0:ncols]),
                reads=[b_ou], dma=True, tag=f"{tg}_{j}w")
    return units, emit_unit


def stage_attention(c):
    p = c.p
    b_idb = buf(c, "cidb")
    b_psx = buf(c, "ps_x")
    b_psa = buf(c, "ps_aA")
    b_pss = [buf(c, "ps_sA"), buf(c, "ps_sB")]
    b_vecs = buf(c, "vecs")
    b_bc = buf(c, "bcast")
    xt0 = ar(c, 0, D)
    b_xt0 = buf(c, "ar_xt0")
    nh = ar(c, 8192, 4096, BF16)
    b_nh = buf(c, "ar_nh")
    qpr = ar(c, 4096, 8192, BF16)
    b_qpr = buf(c, "ar_qpr")
    kts = [ar(c, 13952 + 2048 * i, 4096, BF16) for i in range(2)]
    b_kt = [buf(c, f"ar_kt{i}") for i in range(2)]
    vts = [ar(c, 18048 + 2048 * i, 4096, BF16) for i in range(4)]
    b_vt = [buf(c, f"ar_vt{i}") for i in range(4)]
    x2s = ar(c, 26240, 8192)
    b_x2s = buf(c, "ar_x2s")
    ats = [ar(c, 34432 + 128 * i, 256, BF16) for i in range(16)]
    b_at = [buf(c, f"ar_at{i}") for i in range(16)]
    msks = [ar(c, 36480 + 128 * i, 256, BF16) for i in range(2)]
    b_msk = [buf(c, f"ar_msk{i}") for i in range(2)]

    c.x2_done = []
    for q in range(4):
        for half in range(2):
            r0 = q * CH + half * 128
            ln_tile(c, c.x_own[r0:r0 + 128, :], xt0, b_xt0, nh, b_nh,
                    56 * (half % 2), f"qp{q}_{half}", 0, D,
                    extra_deps=([c.bc_done["u"], c.bc_done["sa"]] if q == 0 else []))
            for bt in range(4):
                for k8 in range(8):
                    k = bt * 8 + k8
                    p.op("tensor", lambda t, k=k, k8=k8: t.transpose(
                        c.ps_x[:, 128 * k8:128 * (k8 + 1)], nh[:, 128 * k:128 * (k + 1)],
                        c.cidb[:]),
                        reads=[b_nh, b_idb], writes=[b_psx], tag=f"qp{q}{half}tr{k}")
                for k8 in range(8):
                    k = bt * 8 + k8
                    p.op("vector", lambda vv, k=k, k8=k8, half=half: vv.tensor_scalar(
                        out=qpr[:, 256 * k + 128 * half:256 * k + 128 * half + 128],
                        in0=c.ps_x[:, 128 * k8:128 * (k8 + 1)],
                        scalar1=vcol(c, "qkb", k, 1), scalar2=None, op0=OP.mult),
                        reads=[b_psx, b_vecs], writes=[b_qpr], tag=f"qp{q}{half}e{k}")
        span = PAD_SPAN[q]
        first_flash = True
        for r0m in range(0, span, R_RUN):
            run = list(range(r0m, min(r0m + R_RUN, span)))
            for m in run:
                ks, bk = kts[m % 2], b_kt[m % 2]
                vs, bv = vts[m % 4], b_vt[m % 4]
                p.op("scalar", lambda s, m=m, ks=ks: s.dma_start(
                    out=ks.rearrange("p (cd t) -> p cd t", t=128),
                    in_=c.ha_fm[m].rearrange("(cd p) t -> p cd t", p=128)),
                    reads=[buf(c, "d_fm")], writes=[bk], deps=c.pdeps, tag=f"at{q}k{m}", dma=True)
                p.op("scalar", lambda s, m=m, vs=vs: s.dma_start(
                    out=vs, in_=c.ha_tok[MK * m:MK * (m + 1), :]),
                    reads=[buf(c, "d_tok")], writes=[bv], deps=c.pdeps, tag=f"at{q}v{m}", dma=True)
                mi = m % 2
                si = sum(PAD_SPAN[:q]) + m
                p.op("gpsimd", lambda g, si=si, mi=mi: g.dma_start(
                    out=msks[mi], in_=c.masks[si]),
                    writes=[b_msk[mi]], deps=c.pdeps, dma=True, tag=f"at{q}m{m}")
                if getattr(c, "dq_tick", None):
                    c.dq_tick()
                for h in range(H):
                    sb = (h % 2)
                    pso = 512 * sb
                    ati = (m % 4) * 4 + h
                    for dk in range(8):
                        kc = h * 8 + dk
                        p.op("tensor", lambda t, kc=kc, pso=pso, ks=ks: t.matmul(
                            c.ps_s[:, pso:pso + 256], ks[:, 128 * kc:128 * (kc + 1)],
                            qpr[:, 256 * kc:256 * (kc + 1)],
                            start=(kc % 8 == 0), stop=(kc % 8 == 7)),
                            reads=[bk, b_qpr], writes=[b_pss[sb]], tag=f"s{q}{m}{h}{dk}")
                    p.op("scalar", lambda s, pso=pso, ati=ati: s.activation(
                        out=ats[ati], in_=c.ps_s[:, pso:pso + 256],
                        func=AF.Sigmoid, scale=0.125),
                        reads=[b_pss[sb]], writes=[b_at[ati]], tag=f"sg{q}{m}{h}")
                    p.op("vector", lambda vv, ati=ati, mi=mi: vv.tensor_tensor(
                        out=ats[ati], in0=ats[ati], in1=msks[mi], op=OP.mult),
                        reads=[b_at[ati], b_msk[mi]], writes=[b_at[ati]], tag=f"mk{q}{m}{h}")
            for h in range(H):
                for qs in range(2):
                    for m in run:
                        ati = (m % 4) * 4 + h
                        vs, bv = vts[m % 4], b_vt[m % 4]
                        for dh in range(2):
                            p.op("tensor", lambda t, ati=ati, qs=qs, dh=dh, h=h, vs=vs,
                                 st=(m == run[0]), sp=(m == run[-1]): t.matmul(
                                c.ps_a[:, 512 * dh:512 * (dh + 1)],
                                ats[ati][:, 128 * qs:128 * (qs + 1)],
                                vs[:, 1024 * h + 512 * dh:1024 * h + 512 * (dh + 1)],
                                start=st, stop=sp),
                                reads=[b_at[ati], bv], writes=[b_psa], tag=f"av{q}{m}{h}{qs}{dh}")
                    if first_flash:
                        p.op("vector", lambda vv, qs=qs, h=h: vv.tensor_copy(
                            out=x2s[:, 4096 * qs + 1024 * h:4096 * qs + 1024 * (h + 1)],
                            in_=c.ps_a[:, 0:1024]),
                            reads=[b_psa], writes=[b_x2s], deps=c.pdeps, tag=f"fl{q}{r0m}{h}{qs}")
                    else:
                        p.op("vector", lambda vv, qs=qs, h=h: vv.tensor_tensor(
                            out=x2s[:, 4096 * qs + 1024 * h:4096 * qs + 1024 * (h + 1)],
                            in0=x2s[:, 4096 * qs + 1024 * h:4096 * qs + 1024 * (h + 1)],
                            in1=c.ps_a[:, 0:1024], op=OP.add),
                            reads=[b_psa, b_x2s], writes=[b_x2s], tag=f"fl{q}{r0m}{h}{qs}")
            first_flash = False
        for qs in range(2):
            r0 = q * CH + qs * 128
            p.op("scalar", lambda s, r0=r0: s.dma_start(
                out=xt0, in_=c.x_own[r0:r0 + 128, :]),
                writes=[b_xt0], dma=True, tag=f"ev{q}{qs}ld")
            p.op("vector", lambda vv, qs=qs: vv.tensor_tensor(
                out=x2s[:, 4096 * qs:4096 * (qs + 1)],
                in0=x2s[:, 4096 * qs:4096 * (qs + 1)],
                in1=c.bcast[:, 2 * D:3 * D], op=OP.mult),
                reads=[b_x2s, b_bc], writes=[b_x2s], tag=f"ev{q}{qs}g")
            p.op("vector", lambda vv, qs=qs: vv.tensor_tensor(
                out=x2s[:, 4096 * qs:4096 * (qs + 1)],
                in0=x2s[:, 4096 * qs:4096 * (qs + 1)],
                in1=xt0, op=OP.add),
                reads=[b_x2s, b_xt0], writes=[b_x2s], tag=f"ev{q}{qs}a")
            w = p.op("gpsimd", lambda g, r0=r0, qs=qs: g.dma_start(
                out=c.x2_dram[r0:r0 + 128, :], in_=x2s[:, 4096 * qs:4096 * (qs + 1)]),
                reads=[b_x2s], writes=[buf(c, "d_x2")], dma=True, tag=f"ev{q}{qs}w")
            c.x2_done.append(w)


def stage_lnb_ffn(c):
    p = c.p
    if getattr(c, "want_debug", False):
        c.dbg_dst_ap = c.nc.dram_tensor("dbg_dst", [128, 2048], F32, kind="ExternalOutput").ap()
        c.dbg_frg_ap = c.nc.dram_tensor("dbg_frg", [128, 512], F32, kind="ExternalOutput").ap()
        c.dbg_x2c_ap = c.nc.dram_tensor("dbg_x2c", [128, 512], F32, kind="ExternalOutput").ap()
    b_idb = buf(c, "cidb")
    b_psx = buf(c, "ps_x")
    b_vecs = buf(c, "vecs")
    b_bc = buf(c, "bcast")
    b_pst = buf(c, "ps_t")
    b_one = buf(c, "cone1")
    b_idf = buf(c, "cidf")
    b_onesr = buf(c, "cones")
    ones1 = c.cones[0:1, 128:129]

    mrow = ar(c, 0, 4096)
    b_mrow = buf(c, "ar_xt0")
    for gi, gn in enumerate(("u2", "sb", "gb")):
        for k in range(DCH):
            p.op("tensor", lambda t, gn=gn, k=k: t.matmul(
                c.ps_t[0:1, 0:128], vcol(c, gn, k, 1), c.cidf[:],
                start=True, stop=True),
                reads=[b_vecs, b_idf], writes=[b_pst], deps=c.pdeps, tag=f"r3{gn}{k}")
            p.op("vector", lambda vv, k=k: vv.tensor_copy(
                out=mrow[0:1, 128 * k:128 * (k + 1)], in_=c.ps_t[0:1, 0:128]),
                reads=[b_pst], writes=[b_mrow], deps=c.pdeps, tag=f"re3{gn}{k}")
        for s in range(8):
            p.op("tensor", lambda t, s=s: t.matmul(
                c.ps_t[:, 0:512], c.cones[0:1, 0:128], mrow[0:1, 512 * s:512 * (s + 1)],
                start=True, stop=True),
                reads=[b_mrow, b_onesr], writes=[b_pst], tag=f"b3{gn}{s}")
            p.op("vector", lambda vv, gi=gi, s=s: vv.tensor_copy(
                out=c.bcast[:, gi * D + 512 * s:gi * D + 512 * (s + 1)], in_=c.ps_t[:, 0:512]),
                reads=[b_pst], writes=[b_bc], tag=f"be3{gn}{s}")

    ht = ar(c, 10368, 32768, BF16)
    b_ht = buf(c, "ar_ht")
    xt0 = ar(c, 0, D)
    b_xt0 = buf(c, "ar_xt0")
    nh = ar(c, 8192, 4096, BF16)
    b_nh = buf(c, "ar_nh")
    pch = perm_chunk_list(D)
    for ti in range(8):
        ln_tile(c, c.x2_dram[MK * ti:MK * (ti + 1), :], xt0, b_xt0, nh, b_nh,
                56 * (ti % 2), f"lb{ti}", 0, D, extra_deps=c.x2_done)
        for bt in range(4):
            for k8 in range(8):
                kc = bt * 8 + k8
                R0, rows, j, pos = pch[kc]
                p.op("tensor", lambda t, k8=k8, R0=R0, j=j: t.transpose(
                    c.ps_x[:, 128 * k8:128 * (k8 + 1)],
                    nh[:, 8 * R0 + j:8 * R0 + j + 1017:8],
                    c.cidb[:]),
                    reads=[b_nh, b_idb], writes=[b_psx], tag=f"lb{ti}tr{kc}")
            p.op("vector", lambda vv, bt=bt, ti=ti: vv.tensor_copy(
                out=ht.rearrange("p (kc t) -> p kc t", t=1024)[
                    :, bt * 8:(bt + 1) * 8, ti * 128:(ti + 1) * 128],
                in_=c.ps_x[:].rearrange("p (k8 t) -> p k8 t", t=128)),
                reads=[b_psx], writes=[b_ht], tag=f"lb{ti}te{bt}")

    b_psa = buf(c, "ps_aA")
    b_hsh = buf(c, "hsumh")
    b_indh = buf(c, "cindh")
    for kc in range(DCH):
        Ri = kc // 8
        for th in range(2):
            p.op("tensor", lambda t, kc=kc, Ri=Ri, th=th: t.matmul(
                c.ps_a[0:32, 512 * th:512 * (th + 1)],
                c.cindh[:, 32 * Ri:32 * (Ri + 1)],
                ht.rearrange("p (kc t) -> p kc t", t=1024)[:, kc, 512 * th:512 * (th + 1)],
                start=(kc == 0), stop=(kc == DCH - 1)),
                reads=[b_ht, b_indh], writes=[b_psa], tag=f"hsh{kc}{th}")
    p.op("vector", lambda vv: vv.tensor_copy(out=c.hsumh[:], in_=c.ps_a[0:32, 0:1024]),
         reads=[b_psa], writes=[b_hsh], tag="hsh_ev")
    b_hsl = buf(c, "hsuml")
    p.op("vector", lambda vv: vv.tensor_tensor(
        out=c.hsuml[:], in0=c.ps_a[0:32, 0:1024], in1=c.hsumh[:], op=OP.subtract),
        reads=[b_psa, b_hsh], writes=[b_hsl], tag="hsl_ev")

    wg = ar(c, 26752, 4096, BF16)
    wu = ar(c, 28800, 4096, BF16)
    b_wg, b_wu = buf(c, "ar_wg"), buf(c, "ar_wu")
    sil = ar(c, 30848, 1024)
    b_sil = buf(c, "ar_sil")
    vst = ar(c, 31872, 1024, BF16)
    b_vst = buf(c, "ar_vst")
    szt = ar(c, 32384, 512, BF16)
    b_szt = buf(c, "ar_szt")
    b_pss = buf(c, "ps_sA")
    b_pstu = [buf(c, "ps_t"), buf(c, "ps_u")]
    b_hsv = buf(c, "hsumv")
    b_indv = buf(c, "cindv")
    pchI = perm_chunk_list(INTER)
    for ci, (R0, rows, j, pos) in enumerate(pchI):
        p.op("sync", lambda g, pos=pos, rows=rows: g.dma_start(
            out=wg[:, 0:32 * rows].rearrange("p (kc r) -> p kc r", kc=32),
            in_=c.wdq["g"].rearrange("(kc p) n -> p kc n", p=128)[:, :, pos:pos + rows]),
            writes=[b_wg], deps=c.pdeps, dma=True, tag=f"gw{ci}")
        p.op("sync", lambda g, pos=pos, rows=rows: g.dma_start(
            out=wu[:, 0:32 * rows].rearrange("p (kc r) -> p kc r", kc=32),
            in_=c.wdq["u"].rearrange("(kc p) n -> p kc n", p=128)[:, :, pos:pos + rows]),
            writes=[b_wu], deps=c.pdeps, dma=True, tag=f"uw{ci}")
        p.op("gpsimd", lambda g, pos=pos, rows=rows: g.dma_start(
            out=szt[0:32, 0:rows], in_=c.sz_dram["g"][0:32, pos:pos + rows]),
            writes=[b_szt], deps=c.pdeps + [c.sz_done], dma=True, tag=f"szg{ci}")
        p.op("gpsimd", lambda g, pos=pos, rows=rows: g.dma_start(
            out=szt[0:32, rows:2 * rows], in_=c.sz_dram["u"][0:32, pos:pos + rows]),
            reads=[b_szt], writes=[b_szt], dma=True, tag=f"szu{ci}")
        for th in range(2):
            for kc in range(DCH):
                p.op("tensor", lambda t, kc=kc, th=th, rows=rows: t.matmul(
                    c.ps_a[0:rows, 512 * th:512 * th + 512],
                    wg[:, kc * rows:(kc + 1) * rows],
                    ht.rearrange("p (kc t) -> p kc t", t=1024)[:, kc, 512 * th:512 * th + 512],
                    start=(kc == 0), stop=False),
                    reads=[b_wg, b_ht], writes=[b_psa], tag=f"g{ci}{th}{kc}")
                p.op("tensor", lambda t, kc=kc, th=th, rows=rows: t.matmul(
                    c.ps_s[0:rows, 512 * th:512 * th + 512],
                    wu[:, kc * rows:(kc + 1) * rows],
                    ht.rearrange("p (kc t) -> p kc t", t=1024)[:, kc, 512 * th:512 * th + 512],
                    start=(kc == 0), stop=False),
                    reads=[b_wu, b_ht], writes=[b_pss], tag=f"u{ci}{th}{kc}")
            p.op("tensor", lambda t, th=th, rows=rows: t.matmul(
                c.ps_a[0:rows, 512 * th:512 * th + 512], szt[0:32, 0:rows],
                c.hsumh[:, 512 * th:512 * th + 512], start=False, stop=False),
                reads=[b_szt, b_hsh], writes=[b_psa], tag=f"gz{ci}{th}")
            p.op("tensor", lambda t, th=th, rows=rows: t.matmul(
                c.ps_a[0:rows, 512 * th:512 * th + 512], szt[0:32, 0:rows],
                c.hsuml[:, 512 * th:512 * th + 512], start=False, stop=True),
                reads=[b_szt, b_hsl], writes=[b_psa], tag=f"gzl{ci}{th}")
            p.op("tensor", lambda t, th=th, rows=rows: t.matmul(
                c.ps_s[0:rows, 512 * th:512 * th + 512], szt[0:32, rows:2 * rows],
                c.hsumh[:, 512 * th:512 * th + 512], start=False, stop=False),
                reads=[b_szt, b_hsh], writes=[b_pss], tag=f"uz{ci}{th}")
            p.op("tensor", lambda t, th=th, rows=rows: t.matmul(
                c.ps_s[0:rows, 512 * th:512 * th + 512], szt[0:32, rows:2 * rows],
                c.hsuml[:, 512 * th:512 * th + 512], start=False, stop=True),
                reads=[b_szt, b_hsl], writes=[b_pss], tag=f"uzl{ci}{th}")
            p.op("scalar", lambda s, th=th, rows=rows: s.activation(
                out=sil[0:rows, 512 * th:512 * th + 512],
                in_=c.ps_a[0:rows, 512 * th:512 * th + 512], func=AF.Silu),
                reads=[b_psa], writes=[b_sil], tag=f"sl{ci}{th}")
            p.op("vector", lambda vv, th=th, rows=rows: vv.tensor_tensor(
                out=vst[0:rows, 512 * th:512 * th + 512],
                in0=sil[0:rows, 512 * th:512 * th + 512],
                in1=c.ps_s[0:rows, 512 * th:512 * th + 512], op=OP.mult),
                reads=[b_sil, b_pss], writes=[b_vst], tag=f"vm{ci}{th}")
            p.op("tensor", lambda t, th=th, rows=rows, Ri=R0 // 128,
                 st=(ci == 0), sp=(ci == len(pchI) - 1): t.matmul(
                (c.ps_t if th == 0 else c.ps_u)[0:86, 0:512],
                c.cindv[:, 86 * Ri:86 * (Ri + 1)][0:rows, :],
                vst[0:rows, 512 * th:512 * th + 512],
                start=st, stop=sp),
                reads=[b_vst, b_indv], writes=[b_pstu[th]], tag=f"hv{ci}{th}")
        p.op("gpsimd", lambda g, pos=pos, rows=rows: g.dma_start(
            out=c.v_dram[pos:pos + rows, :], in_=vst[0:rows, :]),
            reads=[b_vst], writes=[buf(c, "d_v")], dma=True, tag=f"vw{ci}")
    c.ffn_gu_done = []
    for th in range(2):
        ev = p.op("vector", lambda vv, th=th: vv.tensor_copy(
            out=c.hsumv[:, 512 * th:512 * th + 512],
            in_=(c.ps_t if th == 0 else c.ps_u)[0:86, 0:512]),
            reads=[b_pstu[th]], writes=[b_hsv], tag=f"hv_ev{th}")
        c.ffn_gu_done.append(ev)

    if getattr(c, "want_debug", False):
        c.dbg_gbb_ap = c.nc.dram_tensor("dbg_gbb", [128, D], BF16, kind="ExternalOutput").ap()
        p.op("gpsimd", lambda g: g.dma_start(out=c.dbg_gbb_ap, in_=c.bcast[:, 2 * D:3 * D]),
             reads=[b_bc], writes=[buf(c, "d_dbggbb")], dma=True, tag="dbg_gbb")
    dst_ = ar(c, 26752, 2048)
    b_dst = buf(c, "ar_dst")
    frg = ar(c, 28800, 512)
    b_frg = buf(c, "ar_frg")
    x2c = ar(c, 29312, 512)
    b_x2c = buf(c, "ar_x2c")
    wdt = ar(c, 29824, 1024, BF16)
    b_wdt = buf(c, "ar_wdt")
    vch = ar(c, 30848, 1024, BF16)
    b_vch = buf(c, "ar_vch")
    szd = ar(c, 31872, 1024, BF16)
    b_szd = buf(c, "ar_szd")
    psq = [(c.ps_a, 0), (c.ps_a, 512), (c.ps_s, 0), (c.ps_s, 512)]
    b_psq = [buf(c, "ps_aA"), buf(c, "ps_aB"), buf(c, "ps_sA"), buf(c, "ps_sB2")]
    b_out = buf(c, "d_out")
    c.out_done = []
    for blkn in range(8):
        for th in range(2):
            p.op("gpsimd", lambda g, blkn=blkn: g.dma_start(
                out=szd[0:86, 0:512], in_=c.sz_dram["d"][0:86, 512 * blkn:512 * (blkn + 1)]),
                writes=[b_szd], deps=c.pdeps + [c.sz_done] + c.ffn_gu_done, dma=True, tag=f"dz{blkn}{th}")
            for ci, (R0, rows, j, pos) in enumerate(pchI):
                p.op("sync", lambda g, pos=pos, rows=rows, blkn=blkn: g.dma_start(
                    out=wdt[0:rows, 0:512],
                    in_=c.wdq["d"][pos:pos + rows, 512 * blkn:512 * (blkn + 1)]),
                    writes=[b_wdt], deps=c.pdeps + c.ffn_gu_done, dma=True, tag=f"dw{blkn}{th}{ci}")
                p.op("scalar", lambda s, pos=pos, rows=rows, th=th: s.dma_start(
                    out=vch[0:rows, 0:512],
                    in_=c.v_dram[pos:pos + rows, 512 * th:512 * (th + 1)]),
                    reads=[buf(c, "d_v")], writes=[b_vch], deps=c.pdeps, dma=True,
                    tag=f"dv{blkn}{th}{ci}")
                for sub in range(4):
                    ps, off = psq[sub]
                    p.op("tensor", lambda t, sub=sub, ps=ps, off=off, rows=rows,
                         st=(ci == 0), ci=ci: t.matmul(
                        ps[:, off:off + 512], wdt[0:rows, 128 * sub:128 * (sub + 1)],
                        vch[0:rows, 0:512], start=st, stop=False),
                        reads=[b_wdt, b_vch], writes=[b_psq[sub]], tag=f"dm{blkn}{th}{ci}{sub}")
            for sub in range(4):
                ps, off = psq[sub]
                p.op("tensor", lambda t, sub=sub, ps=ps, off=off, th=th: t.matmul(
                    ps[:, off:off + 512], szd[0:86, 128 * sub:128 * (sub + 1)],
                    c.hsumv[:, 512 * th:512 * (th + 1)], start=False, stop=True),
                    reads=[b_szd, buf(c, "hsumv")], writes=[b_psq[sub]], tag=f"dzm{blkn}{th}{sub}")
                p.op("vector", lambda vv, sub=sub, ps=ps, off=off: vv.tensor_copy(
                    out=dst_[:, 512 * sub:512 * (sub + 1)], in_=ps[:, off:off + 512]),
                    reads=[b_psq[sub]], writes=[b_dst], tag=f"de{blkn}{th}{sub}")
            if blkn == 0 and th == 0 and hasattr(c, "dbg_dst_ap"):
                p.op("gpsimd", lambda g: g.dma_start(out=c.dbg_dst_ap, in_=dst_[:, 0:2048]),
                     reads=[b_dst], writes=[buf(c, "d_dbgdst")], dma=True, tag="dbg_dst")
            dstb = ar(c, 30848, 4096, BF16)
            b_dstb = buf(c, "ar_dstb")
            for sub in range(4):
                p.op("vector", lambda vv, sub=sub: vv.tensor_copy(
                    out=dstb[:, 512 * sub:512 * (sub + 1)],
                    in_=dst_[:, 512 * sub:512 * (sub + 1)]),
                    reads=[b_dst], writes=[b_dstb], tag=f"db{blkn}{th}{sub}")
            for ts_ in range(4):
                for sub in range(4):
                    p.op("tensor", lambda t, ts_=ts_, sub=sub: t.transpose(
                        c.ps_x[:, 128 * sub:128 * (sub + 1)],
                        dstb[:, 512 * sub + 128 * ts_:512 * sub + 128 * (ts_ + 1)],
                        c.cidb[:]),
                        reads=[b_dstb, buf(c, "cidb")], writes=[buf(c, "ps_x")],
                        tag=f"dt{blkn}{th}{ts_}{sub}")
                p.op("scalar", lambda s, blkn=blkn, th=th, ts_=ts_: s.dma_start(
                    out=x2c[:, 0:512],
                    in_=c.x2_dram[512 * th + 128 * ts_:512 * th + 128 * (ts_ + 1),
                                  512 * blkn:512 * (blkn + 1)]),
                    reads=[buf(c, "d_x2")], writes=[b_x2c], deps=c.ffn_gu_done, dma=True, tag=f"dx{blkn}{th}{ts_}")
                p.op("vector", lambda vv, blkn=blkn: vv.tensor_tensor(
                    out=frg[:, 0:512], in0=c.ps_x[:, 0:512],
                    in1=c.bcast[:, 2 * D + 512 * blkn:2 * D + 512 * (blkn + 1)], op=OP.mult),
                    reads=[buf(c, "ps_x"), b_bc], writes=[b_frg], tag=f"df{blkn}{th}{ts_}"),
                p.op("vector", lambda vv: vv.tensor_tensor(
                    out=frg[:, 0:512], in0=frg[:, 0:512], in1=x2c[:, 0:512], op=OP.add),
                    reads=[b_frg, b_x2c], writes=[b_frg], tag=f"dg{blkn}{th}{ts_}")

                w = p.op("gpsimd", lambda g, blkn=blkn, th=th, ts_=ts_: g.dma_start(
                    out=c.out[512 * th + 128 * ts_:512 * th + 128 * (ts_ + 1),
                              512 * blkn:512 * (blkn + 1)],
                    in_=frg[:, 0:512]),
                    reads=[b_frg], writes=[b_out], dma=True, tag=f"do{blkn}{th}{ts_}")
                c.out_done.append(w)


def stage_sz(c):
    p = c.p
    zld = ar(c, 24192, 1536, I32)
    b_zld = buf(c, "ar_zld")
    zpl = ar(c, 25728, 1536)
    b_zpl = buf(c, "ar_zpl")
    scb = ar(c, 27264, 1024)
    b_scb = buf(c, "ar_scb")
    szb = ar(c, 28288, 1024, BF16)
    b_szb = buf(c, "ar_szb")
    last = None
    for mat in "gud":
        G = {"g": 32, "u": 32, "d": 86}[mat]
        nn = INTER if mat in "gu" else D
        ncpk = nn // 8
        p.op("gpsimd", lambda g, mat=mat, G=G, ncpk=ncpk: g.dma_start(
            out=zld[0:G, 0:ncpk], in_=c.qz[mat]),
            writes=[b_zld], dma=True, tag=f"sz_{mat}_ld")
        nblocks = (nn + 1023) // 1024
        for blk in range(nblocks):
            ncols = min(1024, nn - 1024 * blk)
            p.op("gpsimd", lambda g, mat=mat, G=G, blk=blk, ncols=ncols: g.dma_start(
                out=scb[0:G, 0:ncols], in_=c.scp[mat][0:G, 1024 * blk:1024 * blk + ncols]),
                writes=[b_scb], dma=True, tag=f"sz_{mat}{blk}_sc")
            for pn in range(8):
                cw = ncols // 8
                cc0 = 128 * blk
                p.op("vector", lambda vv, G=G, pn=pn, cw=cw, cc0=cc0: vv.tensor_scalar(
                    out=zpl[0:G, 0:cw].bitcast(I32), in0=zld[0:G, cc0:cc0 + cw],
                    scalar1=4 * pn, scalar2=15, op0=OP.logical_shift_right,
                    op1=OP.bitwise_and),
                    reads=[b_zld], writes=[b_zpl], tag=f"sz_{mat}{blk}{pn}x")
                p.op("vector", lambda vv, G=G, cw=cw: vv.tensor_scalar(
                    out=zpl[0:G, 512:512 + cw], in0=zpl[0:G, 0:cw].bitcast(I32),
                    scalar1=1.0, scalar2=-1.0, op0=OP.add, op1=OP.mult),
                    reads=[b_zpl], writes=[b_zpl], tag=f"sz_{mat}{blk}{pn}n")
                if mat in "gu":
                    p.op("vector", lambda vv, G=G, pn=pn, cw=cw: vv.tensor_tensor(
                        out=szb[0:G, pn * cw:(pn + 1) * cw], in0=zpl[0:G, 512:512 + cw],
                        in1=scb[0:G, pn * cw:(pn + 1) * cw], op=OP.mult),
                        reads=[b_zpl, b_scb], writes=[b_szb], tag=f"sz_{mat}{blk}{pn}m")
                else:
                    p.op("vector", lambda vv, G=G, pn=pn, cw=cw: vv.tensor_tensor(
                        out=szb[0:G, pn:pn + 8 * (cw - 1) + 1:8],
                        in0=zpl[0:G, 512:512 + cw],
                        in1=scb[0:G, pn:pn + 8 * (cw - 1) + 1:8], op=OP.mult),
                        reads=[b_zpl, b_scb], writes=[b_szb], tag=f"sz_{mat}{blk}{pn}m")
            last = p.op("gpsimd", lambda g, mat=mat, G=G, blk=blk, ncols=ncols: g.dma_start(
                out=c.sz_dram[mat][0:G, 1024 * blk:1024 * blk + ncols],
                in_=szb[0:G, 0:ncols]),
                reads=[b_szb], writes=[buf(c, f"d_sz{mat}")], dma=True, tag=f"sz_{mat}{blk}w2")
    c.sz_done = last


def stage_debug(c):
    p = c.p
    nc = c.nc
    dbg = {}
    dbg["vecs"] = nc.dram_tensor("dbg_vecs", [128, 12 * DCH], F32, kind="ExternalOutput").ap()
    dbg["bcast"] = nc.dram_tensor("dbg_bcast", [128, 3 * D], BF16, kind="ExternalOutput").ap()
    dbg["ha"] = nc.dram_tensor("dbg_ha", [256, D], BF16, kind="ExternalOutput").ap()
    dbg["fm"] = nc.dram_tensor("dbg_fm", [D, MK], BF16, kind="ExternalOutput").ap()
    dbg["x2"] = nc.dram_tensor("dbg_x2", [TOK, D], F32, kind="ExternalOutput").ap()
    dbg["wg"] = nc.dram_tensor("dbg_wg", [256, 2048], BF16, kind="ExternalOutput").ap()
    dbg["szg"] = nc.dram_tensor("dbg_szg", [32, INTER], BF16, kind="ExternalOutput").ap()
    dbg["v"] = nc.dram_tensor("dbg_v", [256, TOK], BF16, kind="ExternalOutput").ap()
    dbg["vlast"] = nc.dram_tensor("dbg_vlast", [256, TOK], BF16, kind="ExternalOutput").ap()
    dbg["wd"] = nc.dram_tensor("dbg_wd", [256, 2048], BF16, kind="ExternalOutput").ap()
    dbg["hsv"] = nc.dram_tensor("dbg_hsv", [86, 1024], BF16, kind="ExternalOutput").ap()
    dbg["ht"] = nc.dram_tensor("dbg_ht", [128, 4096], BF16, kind="ExternalOutput").ap()
    c.dbg_outs = list(dbg.keys())
    ops = [
        ("vecs", lambda g: g.dma_start(out=dbg["vecs"], in_=c.vecs[:]), ["vecs"]),
        ("bcast", lambda g: g.dma_start(out=dbg["bcast"], in_=c.bcast[:]), ["bcast"]),
        ("ha", lambda g: g.dma_start(out=dbg["ha"], in_=c.ha_tok[0:256, :]), ["d_tok"]),
        ("fm", lambda g: g.dma_start(out=dbg["fm"], in_=c.ha_fm[3]), ["d_fm"]),
        ("x2", lambda g: g.dma_start(out=dbg["x2"], in_=c.x2_dram), ["d_x2"]),
        ("wg", lambda g: g.dma_start(out=dbg["wg"], in_=c.wdq["g"][0:256, 0:2048]), []),
        ("szg", lambda g: g.dma_start(out=dbg["szg"], in_=c.sz_dram["g"]), ["d_szg"]),
        ("v", lambda g: g.dma_start(out=dbg["v"], in_=c.v_dram[0:256, :]), ["d_v"]),
        ("vlast", lambda g: g.dma_start(out=dbg["vlast"], in_=c.v_dram[10752:11008, :]), ["d_v"]),
        ("wd", lambda g: g.dma_start(out=dbg["wd"], in_=c.wdq["d"][0:256, 0:2048]), []),
        ("hsv", lambda g: g.dma_start(out=dbg["hsv"], in_=c.hsumv[:]), ["hsumv"]),
        ("ht", lambda g: g.dma_start(out=dbg["ht"], in_=ar(c, 10368, 8192, BF16)[:, 0:4096]), ["ar_ht"]),
    ]
    for name, fn, rd in ops:
        w = p.op("gpsimd", fn, reads=[buf(c, n) for n in rd],
                 deps=c.out_done[-1:], dma=True, tag="dbg_" + name)
        p.wait("gpsimd", w)


class _SelfModule:
    def __getattr__(self, n):
        return globals()[n]


kb = K = _SelfModule()

def build_program(dq_engines=("vector", "gpsimd"), debug=False):
    nc = bass.Bass("TRN2", target_bir_lowering=False)
    st = ExitStack()
    with st:
        c = K.declare(nc, st)
        K.stage_consts(c)
        K.stage_mod(c)
        K.stage_sz(c)
        units, emit_unit = K.dequant_units(c)
        p = c.p

        b_idb = K.buf(c, "cidb")
        b_psx = K.buf(c, "ps_x")
        b_fm = K.buf(c, "ar_fm")
        fm = K.ar(c, 13952, 4096, K.BF16)
        xts = [K.ar(c, 0, K.D), K.ar(c, 4096, K.D)]
        b_xts = [K.buf(c, "ar_xt0"), K.buf(c, "ar_xt1")]
        nh = K.ar(c, 8192, 4096, K.BF16)
        b_nh = K.buf(c, "ar_nh")
        for ti in range(kb.T // kb.MK):
            xt, b_xt = xts[ti % 2], b_xts[ti % 2]
            K.ln_tile(c, c.x_full[kb.MK * ti:kb.MK * (ti + 1), :], xt, b_xt, nh, b_nh,
                      (ti % 2) * 56, f"ls{ti}", 0, kb.D)
            p.op("scalar", lambda s, ti=ti: s.dma_start(
                out=c.ha_tok[kb.MK * ti:kb.MK * (ti + 1), :], in_=nh),
                reads=[b_nh], writes=[K.buf(c, "d_tok")], dma=True, tag=f"ls{ti}_tok")
            for bt in range(4):
                for k8 in range(8):
                    k = bt * 8 + k8
                    p.op("tensor", lambda t, k=k, k8=k8: t.transpose(
                        c.ps_x[:, 128 * k8:128 * (k8 + 1)], nh[:, 128 * k:128 * (k + 1)],
                        c.cidb[:]),
                        reads=[b_nh, b_idb], writes=[b_psx], tag=f"ls{ti}_tr{k}")
                p.op("vector", lambda vv, bt=bt: vv.tensor_copy(
                    out=fm[:, 1024 * bt:1024 * (bt + 1)], in_=c.ps_x[:]),
                    reads=[b_psx], writes=[b_fm], tag=f"ls{ti}_te{bt}")
            p.op("scalar", lambda s, ti=ti: s.dma_start(
                out=c.ha_fm[ti].rearrange("(cd p) t -> p cd t", p=128),
                in_=fm.rearrange("p (cd t) -> p cd t", t=128)),
                reads=[b_fm], writes=[K.buf(c, "d_fm")], dma=True, tag=f"ls{ti}_fw")

        K.phase_barrier(c, "p2")
        ui = [0]

        def _tick():
            if ui[0] < len(units):
                emit_unit(units[ui[0]], dq_engines[ui[0] % len(dq_engines)])
                ui[0] += 1
        c.dq_tick = _tick
        K.stage_attention(c)
        c.dq_tick = None
        while ui[0] < len(units):
            emit_unit(units[ui[0]], dq_engines[ui[0] % len(dq_engines)])
            ui[0] += 1
        K.phase_barrier(c, "p3")
        c.want_debug = debug
        K.stage_lnb_ffn(c)
        for w in c.out_done:
            p.wait("gpsimd", w)
        if debug:
            K.stage_debug(c)
        stats = p.emit(nc)
    return nc, stats


_NC_CACHE = {}


def kernel(**inputs):
    from concourse.bass_utils import run_bass_kernel_spmd
    if "nc" not in _NC_CACHE:
        _NC_CACHE["nc"] = build_program()[0]
    nc = _NC_CACHE["nc"]
    in_maps = [kb.host_core_inputs(i, inputs) for i in range(kb.NCORE)]
    res = run_bass_kernel_spmd(nc, in_maps, core_ids=list(range(kb.NCORE)))
    outs = [res.results[i]["out"] for i in range(kb.NCORE)]
    return kb.out_assemble(outs).reshape(1, kb.T, kb.D)




# version 04
# speedup vs baseline: 1.8433x; 1.5919x over previous
import sys
sys.path.insert(0, "/opt/trn_rl_repo")
import numpy as np
from contextlib import ExitStack

import concourse.bass as bass
import concourse.mybir as mybir


"""Mini dependency-tracking emitter for raw Bass programs.

Records ops (closures) with buffer-level read/write sets, derives all
RAW/WAW/WAR semaphore edges (same-engine included -- the simulator's race
detector requires them), assigns semaphore increments only to ops that are
actually depended on, then emits everything inside one nc.Block().

Semaphores: one per compute engine; DMA-issuing engines use N_DMA_SLOTS
rotating semaphores per engine (the race detector allows only one
outstanding DMA per semaphore, so slot s's next DMA first waits for slot
s's previous DMA -> up to N_DMA_SLOTS DMAs in flight per queue).
"""

import os

ENGINES = ("tensor", "vector", "scalar", "gpsimd", "sync")
N_DMA_SLOTS = 8

SIM_STRICT = os.environ.get("BASS_SIM_FP64") == "1"


def _skip_wait(d, o):
    return (not SIM_STRICT) and d.sem_key == "tensor" and o.sem_key == "tensor"


class Buf:
    __slots__ = ("name", "writer", "readers")

    def __init__(self, name=""):
        self.name = name
        self.writer = None
        self.readers = []

    def __repr__(self):
        return f"Buf({self.name})"


class Op:
    __slots__ = ("engine", "emit_fn", "deps", "inc", "cum", "needs_inc", "tag",
                 "gidx", "sem_key")

    def __repr__(self):
        return f"Op({self.engine}:{self.tag}@{self.gidx})"


class Prog:
    def __init__(self):
        self.ops = {e: [] for e in ENGINES}
        self.n = 0
        self.final_waits = []
        self._dma_count = {e: 0 for e in ENGINES}
        self._last_dma_on_slot = {}

    def buf(self, name=""):
        return Buf(name)

    def op(self, engine, emit_fn, reads=(), writes=(), dma=False, deps=(), tag=""):
        assert engine in ENGINES, engine
        o = Op()
        o.engine = engine
        o.emit_fn = emit_fn
        o.inc = 16 if dma else 1
        o.tag = tag
        o.gidx = self.n
        o.needs_inc = False
        self.n += 1
        if dma:
            slot = self._dma_count[engine] % N_DMA_SLOTS
            self._dma_count[engine] += 1
            o.sem_key = (engine, slot)
        else:
            o.sem_key = engine
        dep = set(deps)
        for b in reads:
            if b.writer is not None:
                dep.add(b.writer)
        for b in writes:
            if b.writer is not None:
                dep.add(b.writer)
            dep.update(b.readers)
        if dma:
            prev = self._last_dma_on_slot.get(o.sem_key)
            if prev is not None:
                dep.add(prev)
            self._last_dma_on_slot[o.sem_key] = o
        dep.discard(o)
        o.deps = dep
        for b in reads:
            b.readers.append(o)
        for b in writes:
            b.writer = o
            b.readers = []
        self.ops[engine].append(o)
        return o

    def wait(self, engine, op):
        self.final_waits.append((engine, op))

    def emit(self, nc):
        keys = set()
        for e in ENGINES:
            for o in self.ops[e]:
                keys.add(o.sem_key)
        sems = {}
        ctxs = []
        for k in sorted(keys, key=str):
            name = f"sem_{k}" if isinstance(k, str) else f"sem_{k[0]}_{k[1]}"
            ctx = nc.semaphore(name)
            sems[k] = ctx.__enter__()
            ctxs.append(ctx)
        for e in ENGINES:
            for o in self.ops[e]:
                for d in o.deps:
                    if not _skip_wait(d, o):
                        d.needs_inc = True
        for fe, fo in self.final_waits:
            fo.needs_inc = True
        cum = {k: 0 for k in keys}
        for e in ENGINES:
            for o in self.ops[e]:
                if o.needs_inc:
                    cum[o.sem_key] += o.inc
                o.cum = cum[o.sem_key]
        stats = {e: {"ops": len(self.ops[e]),
                     "incs": sum(1 for o in self.ops[e] if o.needs_inc),
                     "waits": 0} for e in ENGINES}
        block_ctx = nc.Block()
        block = block_ctx.__enter__()
        engines_used = [e for e in ENGINES
                        if self.ops[e] or any(fe == e for fe, _ in self.final_waits)]
        for e in engines_used:
            def make_body(e):
                def body(eng):
                    waited = {k: 0 for k in keys}
                    for o in self.ops[e]:
                        need = {}
                        for d in o.deps:
                            if _skip_wait(d, o):
                                continue
                            if d.cum > need.get(d.sem_key, 0):
                                need[d.sem_key] = d.cum
                        for k in sorted(need, key=str):
                            v = need[k]
                            if v > waited[k]:
                                eng.wait_ge(sems[k], v)
                                waited[k] = v
                                stats[e]["waits"] += 1
                        inst = o.emit_fn(eng)
                        if o.needs_inc:
                            inst.then_inc(sems[o.sem_key], o.inc)
                    for fe, fo in self.final_waits:
                        if fe == e and fo.cum > waited[fo.sem_key]:
                            eng.wait_ge(sems[fo.sem_key], fo.cum)
                            waited[fo.sem_key] = fo.cum
                return body

            getattr(block, e)(make_body(e))
        block_ctx.__exit__(None, None, None)
        for ctx in reversed(ctxs):
            ctx.__exit__(None, None, None)
        return stats




T, D, H, HD, INTER = 8192, 4096, 4, 1024, 11008
NCORE = 8
CH = 256
NCH = T // CH
DCH = D // 128
MK = 128
PAD_SPAN = [16, 32, 48, 64]
NSLOT = sum(PAD_SPAN)
EPS = 1e-5
R_RUN = 4
TOK = CH * 4


def sigma_of_core(i):
    return (i, 7 - i, i, 7 - i)


def chunks_of_core(i):
    s = sigma_of_core(i)
    return [8 * q + s[q] for q in range(4)]


def slot_sched():
    return [[(q, m) for m in range(PAD_SPAN[q])] for q in range(4)]


def perm_cols(nk):
    ks = []
    nrow = nk // 8
    for R in range(0, nrow, 128):
        rows = min(128, nrow - R)
        for j in range(8):
            for r in range(rows):
                ks.append(8 * (R + r) + j)
    return np.array(ks, dtype=np.int64)


def perm_chunk_list(nk):
    out = []
    pos = 0
    nrow = nk // 8
    for R in range(0, nrow, 128):
        rows = min(128, nrow - R)
        for j in range(8):
            out.append((R, rows, j, pos))
            pos += rows
    return out


def s_rep_host(scales, nk):
    nrow = nk // 8
    idx = (8 * np.arange(nrow)) // 128
    return np.asarray(scales, dtype=np.float32)[idx]


def host_core_inputs(i, inputs):
    import ml_dtypes
    x = np.ascontiguousarray(np.asarray(inputs["L0_attn_x"], dtype=np.float32).reshape(T, D))
    chunks = chunks_of_core(i)
    x_own = np.concatenate([x[c * CH:(c + 1) * CH] for c in chunks], axis=0)
    masks = np.zeros((NSLOT, 128, CH), dtype=np.float32)
    k = 0
    for q in range(4):
        c = chunks[q]
        q_glob = c * CH + np.arange(CH)[None, :]
        for m in range(PAD_SPAN[q]):
            s_glob = m * MK + np.arange(MK)[:, None]
            masks[k] = (s_glob <= q_glob)
            k += 1
    masks = masks.astype(ml_dtypes.bfloat16)

    pd = perm_cols(INTER)
    m_ = {
        "x_full": x,
        "x_own": x_own,
        "masks": masks,
        "wmod": np.asarray(inputs["L0_wmod"], dtype=np.float32),
        "cvec": np.asarray(inputs["L0_c"], dtype=np.float32),
        "bv_q": np.asarray(inputs["L0_attn_bv_q"], dtype=np.float32),
        "bv_k": np.asarray(inputs["L0_attn_bv_k"], dtype=np.float32),
        "bv_v": np.asarray(inputs["L0_attn_bv_v"], dtype=np.float32),
        "qw_g": np.asarray(inputs["L0_ffn_gate_qweight"], dtype=np.int32),
        "qw_u": np.asarray(inputs["L0_ffn_up_qweight"], dtype=np.int32),
        "qw_d": np.asarray(inputs["L0_ffn_down_qweight"], dtype=np.int32),
        "qz_g": np.asarray(inputs["L0_ffn_gate_qzeros"], dtype=np.int32),
        "qz_u": np.asarray(inputs["L0_ffn_up_qzeros"], dtype=np.int32),
        "qz_d": np.asarray(inputs["L0_ffn_down_qzeros"], dtype=np.int32),
        "srep_g": s_rep_host(inputs["L0_ffn_gate_scales"], D)[:, pd],
        "srep_u": s_rep_host(inputs["L0_ffn_up_scales"], D)[:, pd],
        "srep_d": s_rep_host(inputs["L0_ffn_down_scales"], INTER),
        "sc_g": np.asarray(inputs["L0_ffn_gate_scales"], dtype=np.float32)[:, pd],
        "sc_u": np.asarray(inputs["L0_ffn_up_scales"], dtype=np.float32)[:, pd],
        "sc_d": np.asarray(inputs["L0_ffn_down_scales"], dtype=np.float32),
    }
    return {k2: np.ascontiguousarray(v) for k2, v in m_.items()}


def out_assemble(results, dtype=np.float32):
    full = np.empty((T, D), dtype=dtype)
    for i in range(NCORE):
        chunks = chunks_of_core(i)
        r = np.asarray(results[i]).reshape(TOK, D)
        for q, c in enumerate(chunks):
            full[c * CH:(c + 1) * CH] = r[q * CH:(q + 1) * CH]
    return full


"""Program builder for the adaLN megakernel (v1, simple-modulation design).

Arena map (f32 units per partition, arena [128, 36864]):
  P1:  XT0 0:4096 | XT1 4096:8192 | NH 8192:10240 (bf16 view) | DQ 10400:12192
       FM 12288:14336 | WMB 14336:16384 | MROW 16384:20480
  P2:  XT0 | QPR 4096:8192 | NH 8192:10240 | AMASK 10240:10368 | DQ (tail)
       KT 12288:16384 (2 slots) | VT 16384:26624 (5 slots) | X2S 26624:34816
       AT 34816:36864
  P3:  XT0 | HT 12288:28672 | WG 28672:30720 | WU 30720:32768
       SIL 32768:33792 | VST 33792:34304 | SZT 34304:34560 | VCH 34560:34816
       (down phase: DST 28672:30720, FRG 30720:31232, X2C 31232:31744,
        WDT 33792:34304)
All cross-phase reuse is protected by phase-master barrier deps.
"""

F32 = mybir.dt.float32
BF16 = mybir.dt.bfloat16
I32 = mybir.dt.int32
AF = mybir.ActivationFunctionType
OP = mybir.AluOpType
AX = mybir.AxisListType

NMODV = 6
VG = {"sa": 0, "ca": 1, "ga": 2, "sb": 3, "cb": 4, "gb": 5,
      "u": 6, "u2": 7, "qkb": 8, "vb": 9, "gavb": 10, "tmp": 11}
ARENA_F = 36864


class Ctx:
    pass


def vcol(c, name, k=0, n=None):
    base = VG[name] * DCH
    if n is None:
        n = DCH - k if k == 0 else 1
    return c.vecs[:, base + k:base + k + n]


def ar(c, off, cols, dt=F32):
    if dt == F32 or dt == I32:
        v = c.arena[:, off:off + cols]
        return v if dt == F32 else v.bitcast(I32)
    return c.arena[:, off:off + (cols + 1) // 2].bitcast(BF16)


def declare(nc, st):
    import ml_dtypes
    c = Ctx()
    c.nc = nc
    c.p = Prog()
    e = st.enter_context
    dt_ = nc.dram_tensor

    c.x_full = dt_("x_full", [T, D], F32, kind="ExternalInput").ap()
    c.x_own = dt_("x_own", [TOK, D], F32, kind="ExternalInput").ap()
    c.masks = dt_("masks", [NSLOT, MK, CH], BF16, kind="ExternalInput").ap()
    c.wmod = dt_("wmod", [D, NMODV * D], F32, kind="ExternalInput").ap()
    c.cvec = dt_("cvec", [1, D], F32, kind="ExternalInput").ap()
    c.bv = {k: dt_("bv_" + k, [H, HD], F32, kind="ExternalInput").ap() for k in "qkv"}
    c.qw = {k: dt_("qw_" + k, [{"g": D, "u": D, "d": INTER}[k] // 8,
                               {"g": INTER, "u": INTER, "d": D}[k]], I32,
                   kind="ExternalInput").ap() for k in "gud"}
    c.qz = {k: dt_("qz_" + k, [{"g": 32, "u": 32, "d": 86}[k],
                               {"g": INTER // 8, "u": INTER // 8, "d": D // 8}[k]], I32,
                   kind="ExternalInput").ap() for k in "gud"}
    c.srep = {k: dt_("srep_" + k, [{"g": D // 8, "u": D // 8, "d": INTER // 8}[k],
                                   {"g": INTER, "u": INTER, "d": D}[k]], F32,
                     kind="ExternalInput").ap() for k in "gud"}
    c.scp = {k: dt_("sc_" + k, [{"g": 32, "u": 32, "d": 86}[k],
                                {"g": INTER, "u": INTER, "d": D}[k]], F32,
                    kind="ExternalInput").ap() for k in "gud"}
    c.out = dt_("out", [TOK, D], F32, kind="ExternalOutput").ap()

    c.ha_fm = dt_("ha_fm", [T // MK, D, MK], BF16).ap()
    c.ha_tok = dt_("ha_tok", [T, D], BF16).ap()
    c.x2_dram = dt_("x2_dram", [TOK, D], F32).ap()
    c.wdq = {k: dt_("wdq_" + k, [{"g": D, "u": D, "d": INTER}[k],
                                 {"g": INTER, "u": INTER, "d": D}[k]], BF16).ap()
             for k in "gud"}
    c.v_dram = dt_("v_dram", [INTER, TOK], BF16).ap()
    c.sz_dram = {k: dt_("szd_" + k, [{"g": 32, "u": 32, "d": 86}[k],
                                     {"g": INTER, "u": INTER, "d": D}[k]], BF16).ap()
                 for k in "gud"}

    c.identf = nc.inline_tensor(np.eye(128, dtype=np.float32), "identf").ap()
    c.identb = nc.inline_tensor(np.eye(128, dtype=np.float32).astype(ml_dtypes.bfloat16), "identb").ap()
    c.ones11 = nc.inline_tensor(np.ones((1, 1), dtype=np.float32), "ones11").ap()
    c.onesf_row = nc.inline_tensor(np.ones((1, 128), dtype=np.float32), "onesf_row").ap()
    indh = np.zeros((4, 128, 32), dtype=np.float32)
    for R in range(4):
        indh[R, np.arange(128), 8 * R + np.arange(128) // 16] = 1.0
    c.indh = nc.inline_tensor(indh.astype(ml_dtypes.bfloat16).reshape(4 * 128, 32), "indh").ap()
    indv = np.zeros((11, 128, 86), dtype=np.float32)
    for R in range(11):
        rows = 128 if R < 10 else 96
        indv[R, np.arange(rows), 8 * R + np.arange(rows) // 16] = 1.0
    c.indv = nc.inline_tensor(indv.astype(ml_dtypes.bfloat16).reshape(11 * 128, 86), "indv").ap()

    c.arena = e(nc.sbuf_tensor("arena", [128, ARENA_F], F32))
    c.cidf = e(nc.sbuf_tensor("cidf", [128, 128], F32))
    c.cidb = e(nc.sbuf_tensor("cidb", [128, 128], BF16))
    c.cones = e(nc.sbuf_tensor("cones", [1, 132], F32))
    c.cindh = e(nc.sbuf_tensor("cindh", [128, 4 * 32], BF16))
    c.cindv = e(nc.sbuf_tensor("cindv", [128, 11 * 86], BF16))
    c.vecs = e(nc.sbuf_tensor("vecs", [128, 12 * DCH], F32))
    c.bcast = e(nc.sbuf_tensor("bcast", [128, 3 * D], BF16))
    c.lnst = e(nc.sbuf_tensor("lnst", [128, 128], F32))
    c.tiny = e(nc.sbuf_tensor("tiny", [128, 64], F32))
    c.ccols = e(nc.sbuf_tensor("ccols", [128, DCH], F32))
    c.bvs = e(nc.sbuf_tensor("bvs", [128, 3 * DCH], F32))
    c.hsumh = e(nc.sbuf_tensor("hsumh", [32, 1024], BF16))
    c.hsuml = e(nc.sbuf_tensor("hsuml", [32, 1024], BF16))
    c.hsumv = e(nc.sbuf_tensor("hsumv", [86, 1024], BF16))

    c.ps_a = e(nc.psum_tensor("ps_a", [128, 1024], F32))
    c.ps_s = e(nc.psum_tensor("ps_s", [128, 1024], F32))
    c.ps_t = e(nc.psum_tensor("ps_t", [128, 512], F32))
    c.ps_u = e(nc.psum_tensor("ps_u", [128, 512], F32))
    c.ps_x = e(nc.psum_tensor("ps_x", [128, 1024], BF16))
    c.ps_x2 = e(nc.psum_tensor("ps_x2", [128, 1024], BF16))

    c.B = {}
    c.pdeps = []
    return c


def buf(c, name):
    if name not in c.B:
        c.B[name] = c.p.buf(name)
    return c.B[name]


def phase_barrier(c, name):
    old = [b for n, b in c.B.items() if n.startswith("ar_")]
    mb = c.p.op("gpsimd", lambda g: g.memset(c.tiny[0:1, 63:64], 0.0),
                reads=old, writes=[buf(c, "pb_" + name)], tag="pb_" + name)
    c.pdeps = [mb]
    return mb


def stage_consts(c):
    p = c.p
    for fn, name in [
        (lambda g: g.dma_start(out=c.cidf[:], in_=c.identf), "cidf"),
        (lambda g: g.dma_start(out=c.cidb[:], in_=c.identb), "cidb"),
        (lambda g: g.dma_start(out=c.cones[0:1, 0:128], in_=c.onesf_row), "cones"),
        (lambda g: g.dma_start(out=c.cones[0:1, 128:129], in_=c.ones11), "cone1"),
        (lambda g: g.dma_start(out=c.cindh[:].rearrange("p (r g) -> p r g", r=4),
                               in_=c.indh.rearrange("(r p) g -> p r g", p=128)), "cindh"),
        (lambda g: g.dma_start(out=c.cindv[:].rearrange("p (r g) -> p r g", r=11),
                               in_=c.indv.rearrange("(r p) g -> p r g", p=128)), "cindv"),
    ]:
        p.op("gpsimd", fn, writes=[buf(c, name)], dma=True, tag="ld_" + name)


def stage_mod(c):
    p = c.p
    b_c = buf(c, "c_row")
    b_pst = buf(c, "ps_t")
    b_ccols = buf(c, "ccols")
    b_vecs = buf(c, "vecs")
    b_tiny = buf(c, "tiny")
    b_bvs = buf(c, "bvs")
    b_one = buf(c, "cone1")
    b_idf = buf(c, "cidf")
    b_onesr = buf(c, "cones")
    ones1 = c.cones[0:1, 128:129]
    xt1 = ar(c, 4096, D)

    b_xt1 = buf(c, "ar_xt1")
    p.op("sync", lambda g: g.dma_start(out=xt1[0:1, 0:D], in_=c.cvec),
         writes=[b_xt1], dma=True, tag="ld_c")
    for k in range(DCH):
        p.op("tensor", lambda t, k=k: t.matmul(
            c.ps_t[:, k:k + 1], xt1[0:1, 128 * k:128 * (k + 1)], ones1,
            start=True, stop=True),
            reads=[b_xt1, b_one], writes=[b_pst], tag=f"ccol{k}")
    p.op("scalar", lambda s: s.activation(out=c.ccols[:, 0:DCH], in_=c.ps_t[:, 0:DCH], func=AF.Silu),
         reads=[b_pst], writes=[b_ccols], tag="silu_c")

    b_wm = [buf(c, "ar_wmb0"), buf(c, "ar_wmb1")]
    b_mrow = buf(c, "ar_mrow")
    wmb2 = [ar(c, 16000, 2048), ar(c, 18048, 2048)]
    mrow = ar(c, 20096, 4096)
    c.mod_done = {}
    psacc = [(c.ps_a, 0), (c.ps_a, 512), (c.ps_s, 0), (c.ps_s, 512)]
    mv_ps = [buf(c, f"mv_ps{i}") for i in range(4)]
    for v in range(NMODV):
        name = {0: "sa", 1: "ca", 2: "ga", 3: "sb", 4: "cb", 5: "gb"}[v]
        for blk in range(2):
            col0 = v * D + blk * 2048
            for k in range(DCH):
                hb = (v * 2 * DCH + blk * DCH + k) % 2
                p.op("sync", lambda g, k=k, col0=col0, hb=hb: g.dma_start(
                    out=wmb2[hb], in_=c.wmod[128 * k:128 * (k + 1), col0:col0 + 2048]),
                    writes=[b_wm[hb]], dma=True, tag=f"wm{v}_{blk}_{k}")
                for s4 in range(4):
                    ps, off = psacc[s4]
                    p.op("tensor", lambda t, k=k, s4=s4, ps=ps, off=off, hb=hb: t.matmul(
                        ps[0:1, off:off + 512], c.ccols[:, k:k + 1],
                        wmb2[hb][:, s4 * 512:(s4 + 1) * 512],
                        start=(k == 0), stop=(k == DCH - 1)),
                        reads=[b_wm[hb], b_ccols], writes=[mv_ps[s4]], tag=f"mv{v}_{blk}_{k}_{s4}")
            for s4 in range(4):
                ps, off = psacc[s4]
                p.op("vector", lambda vv, s4=s4, ps=ps, off=off, blk=blk: vv.tensor_copy(
                    out=mrow[0:1, blk * 2048 + s4 * 512:blk * 2048 + (s4 + 1) * 512],
                    in_=ps[0:1, off:off + 512]),
                    reads=[mv_ps[s4]], writes=[b_mrow], tag=f"mvev{v}_{blk}_{s4}")
        for half16 in range(2):
            for i in range(16):
                k = half16 * 16 + i
                p.op("tensor", lambda t, k=k, i=i: t.matmul(
                    c.ps_t[:, 32 + i:33 + i], mrow[0:1, 128 * k:128 * (k + 1)], ones1,
                    start=True, stop=True),
                    reads=[b_mrow, b_one], writes=[b_pst], tag=f"mvc{v}_{k}")
            last = p.op("vector", lambda vv, name=name, half16=half16: vv.tensor_copy(
                out=c.vecs[:, VG[name] * DCH + half16 * 16:VG[name] * DCH + half16 * 16 + 16],
                in_=c.ps_t[:, 32:48]),
                reads=[b_pst], writes=[b_vecs], tag=f"mvout{v}_{half16}")
        c.mod_done[name] = last

    for bi, kk in enumerate("qkv"):
        for h in range(H):
            p.op("gpsimd", lambda g, kk=kk, h=h: g.dma_start(
                out=xt1[0:1, 0:HD], in_=c.bv[kk][h:h + 1, :]),
                writes=[b_xt1], dma=True, tag=f"ldbv{kk}{h}")
            p.op("vector", lambda vv, bi=bi, h=h: vv.reduce_sum(
                out=c.tiny[0:1, 8 + bi * 4 + h:9 + bi * 4 + h], in_=xt1[0:1, 0:HD],
                axis=AX.X, apply_absolute_value=True),
                reads=[b_xt1], writes=[b_tiny], tag=f"alpha{kk}{h}")
            p.op("vector", lambda vv: vv.tensor_scalar(
                out=xt1[0:1, 0:HD], in0=xt1[0:1, 0:HD],
                scalar1=0.0, scalar2=2.0, op0=OP.is_ge, op1=OP.mult),
                reads=[b_xt1], writes=[b_xt1], tag=f"sge{kk}{h}")
            p.op("vector", lambda vv, bi=bi, h=h: vv.tensor_scalar(
                out=xt1[0:1, 0:HD], in0=xt1[0:1, 0:HD],
                scalar1=-1.0, scalar2=c.tiny[0:1, 8 + bi * 4 + h:9 + bi * 4 + h],
                op0=OP.add, op1=OP.mult),
                reads=[b_xt1, b_tiny], writes=[b_xt1], tag=f"sgn{kk}{h}")
            p.op("vector", lambda vv: vv.tensor_scalar(
                out=xt1[0:1, 0:HD], in0=xt1[0:1, 0:HD],
                scalar1=1.0 / HD, scalar2=None, op0=OP.mult),
                reads=[b_xt1], writes=[b_xt1], tag=f"bsc{kk}{h}")
            for sl in range(8):
                p.op("tensor", lambda t, sl=sl: t.matmul(
                    c.ps_t[:, 48 + sl:49 + sl],
                    xt1[0:1, 128 * sl:128 * (sl + 1)], ones1,
                    start=True, stop=True),
                    reads=[b_xt1, b_one], writes=[b_pst], tag=f"bvc{kk}{h}{sl}")
            p.op("vector", lambda vv, bi=bi, h=h: vv.tensor_copy(
                out=c.bvs[:, bi * DCH + h * 8:bi * DCH + h * 8 + 8],
                in_=c.ps_t[:, 48:56]),
                reads=[b_pst], writes=[b_bvs], tag=f"bvce{kk}{h}")

    def vts(out, in0, s1, s2, o0, o1, deps, tag):
        if s2 is None:
            return p.op("vector", lambda vv: vv.tensor_scalar(
                out=out, in0=in0, scalar1=s1, scalar2=None, op0=o0),
                reads=[b_vecs, b_bvs], writes=[b_vecs], deps=deps, tag=tag)
        return p.op("vector", lambda vv: vv.tensor_scalar(
            out=out, in0=in0, scalar1=s1, scalar2=s2, op0=o0, op1=o1),
            reads=[b_vecs, b_bvs], writes=[b_vecs], deps=deps, tag=tag)

    def vtt(out, in0, in1, op, deps, tag):
        return p.op("vector", lambda vv: vv.tensor_tensor(out=out, in0=in0, in1=in1, op=op),
                    reads=[b_vecs, b_bvs], writes=[b_vecs], deps=deps, tag=tag)

    u_op = vts(vcol(c, "u"), vcol(c, "ca"), 1.0, None, OP.add, OP.bypass, [c.mod_done["ca"]], "mk_u")
    u2_op = vts(vcol(c, "u2"), vcol(c, "cb"), 1.0, None, OP.add, OP.bypass, [c.mod_done["cb"]], "mk_u2")
    for gn in ("ga", "gb"):
        dep = [c.mod_done[gn]]
        vts(vcol(c, "tmp"), vcol(c, gn), -1.0, None, OP.mult, None, dep, f"cl1a{gn}")
        vtt(vcol(c, "tmp"), vcol(c, "tmp"), vcol(c, gn), OP.max, [], f"cl1b{gn}")
        vts(vcol(c, "tmp"), vcol(c, "tmp"), 0.25, None, OP.max, None, [], f"cl1c{gn}")
        vts(vcol(c, gn), vcol(c, gn), 0.0, 2.0, OP.is_ge, OP.mult, dep, f"cl2{gn}")
        vts(vcol(c, gn), vcol(c, gn), -1.0, None, OP.add, OP.bypass, [], f"cl3{gn}")
        vtt(vcol(c, gn), vcol(c, gn), vcol(c, "tmp"), OP.mult, [], f"cl4{gn}")
    vtt(vcol(c, "qkb"), c.bvs[:, 0:DCH], c.bvs[:, DCH:2 * DCH], OP.mult, [], "mk_qkb")
    p.op("vector", lambda vv: vv.tensor_copy(out=vcol(c, "vb"), in_=c.bvs[:, 2 * DCH:3 * DCH]),
         reads=[b_bvs], writes=[b_vecs], tag="mk_vb")
    gavb_op = vtt(vcol(c, "gavb"), vcol(c, "ga"), vcol(c, "vb"), OP.mult, [], "mk_gavb")

    b_bc = buf(c, "bcast")
    c.bc_done = {}
    for gi, (gn, dep) in enumerate((("u", [u_op]), ("sa", [c.mod_done["sa"]]),
                                    ("gavb", [gavb_op]))):
        for k in range(DCH):
            p.op("tensor", lambda t, gn=gn, k=k: t.matmul(
                c.ps_t[0:1, 0:128], vcol(c, gn, k, 1), c.cidf[:],
                start=True, stop=True),
                reads=[b_vecs, b_idf], writes=[b_pst], deps=dep, tag=f"r{gn}{k}")
            p.op("vector", lambda vv, k=k: vv.tensor_copy(
                out=mrow[0:1, 128 * k:128 * (k + 1)], in_=c.ps_t[0:1, 0:128]),
                reads=[b_pst], writes=[b_mrow], tag=f"re{gn}{k}")
        lastb = None
        for s in range(8):
            p.op("tensor", lambda t, s=s: t.matmul(
                c.ps_t[:, 0:512], c.cones[0:1, 0:128], mrow[0:1, 512 * s:512 * (s + 1)],
                start=True, stop=True),
                reads=[b_mrow, b_onesr], writes=[b_pst], tag=f"b{gn}{s}")
            lastb = p.op("vector", lambda vv, gi=gi, s=s: vv.tensor_copy(
                out=c.bcast[:, gi * D + 512 * s:gi * D + 512 * (s + 1)], in_=c.ps_t[:, 0:512]),
                reads=[b_pst], writes=[b_bc], tag=f"be{gn}{s}")
        c.bc_done[gn] = lastb
    return c


def ln_tile(c, xsrc, xt, b_xt, nh, b_nh, stcol, tag, uoff, soff, extra_deps=()):
    p = c.p
    b_st = buf(c, "lnst")
    b_bc = buf(c, "bcast")
    ld = p.op("scalar", lambda s: s.dma_start(out=xt, in_=xsrc),
              writes=[b_xt], deps=list(extra_deps) + c.pdeps, dma=True, tag=f"{tag}_ld")
    for i in range(8):
        p.op("vector", lambda vv, i=i: vv.bn_stats(
            out=c.lnst[:, stcol + 6 * i:stcol + 6 * (i + 1)],
            in_=xt[:, 512 * i:512 * (i + 1)]),
            reads=[b_xt], writes=[b_st], tag=f"{tag}_bn{i}")
    p.op("vector", lambda vv: vv.bn_aggr(
        out=c.lnst[:, stcol + 48:stcol + 50], in_=c.lnst[:, stcol:stcol + 48]),
        reads=[b_st], writes=[b_st], tag=f"{tag}_ag")
    p.op("vector", lambda vv: vv.tensor_scalar(
        out=c.lnst[:, stcol + 53:stcol + 54], in0=c.lnst[:, stcol + 49:stcol + 50],
        scalar1=EPS, scalar2=None, op0=OP.add),
        reads=[b_st], writes=[b_st], tag=f"{tag}_ep")
    p.op("scalar", lambda s: s.activation(
        out=c.lnst[:, stcol + 50:stcol + 51], in_=c.lnst[:, stcol + 53:stcol + 54],
        func=AF.Sqrt),
        reads=[b_st], writes=[b_st], tag=f"{tag}_sq")
    p.op("vector", lambda vv: vv.reciprocal(
        out=c.lnst[:, stcol + 51:stcol + 52], in_=c.lnst[:, stcol + 50:stcol + 51]),
        reads=[b_st], writes=[b_st], tag=f"{tag}_rc")
    p.op("vector", lambda vv: vv.tensor_scalar(
        out=c.lnst[:, stcol + 52:stcol + 53], in0=c.lnst[:, stcol + 48:stcol + 49],
        scalar1=c.lnst[:, stcol + 51:stcol + 52], scalar2=-1.0, op0=OP.mult, op1=OP.mult),
        reads=[b_st], writes=[b_st], tag=f"{tag}_nb")
    p.op("vector", lambda vv: vv.tensor_scalar(
        out=nh, in0=xt, scalar1=c.lnst[:, stcol + 51:stcol + 52],
        scalar2=c.lnst[:, stcol + 52:stcol + 53], op0=OP.mult, op1=OP.add),
        reads=[b_xt, b_st], writes=[b_nh], tag=f"{tag}_nm")
    p.op("vector", lambda vv: vv.tensor_tensor(
        out=nh, in0=nh, in1=c.bcast[:, uoff:uoff + D], op=OP.mult),
        reads=[b_nh, b_bc], writes=[b_nh], tag=f"{tag}_mu")
    p.op("vector", lambda vv: vv.tensor_tensor(
        out=nh, in0=nh, in1=c.bcast[:, soff:soff + D], op=OP.add),
        reads=[b_nh, b_bc], writes=[b_nh], tag=f"{tag}_ms")
    return ld


def stage_lnspan(c):
    p = c.p
    b_idb = buf(c, "cidb")
    b_psx = buf(c, "ps_x")
    b_fm = buf(c, "ar_fm")
    fm = ar(c, 13952, 4096, BF16)
    xts = [ar(c, 0, D), ar(c, 4096, D)]
    b_xts = [buf(c, "ar_xt0"), buf(c, "ar_xt1")]
    nh = ar(c, 8192, 4096, BF16)
    b_nh = buf(c, "ar_nh")
    c.ha_done = []
    for ti in range(T // MK):
        xt, b_xt = xts[ti % 2], b_xts[ti % 2]
        dep = [c.bc_done["u"], c.bc_done["sa"]] if ti == 0 else []
        ln_tile(c, c.x_full[MK * ti:MK * (ti + 1), :], xt, b_xt, nh, b_nh,
                (ti % 2) * 56, f"ls{ti}", 0, D, extra_deps=dep)
        tok = p.op("scalar", lambda s, ti=ti: s.dma_start(
            out=c.ha_tok[MK * ti:MK * (ti + 1), :], in_=nh),
            reads=[b_nh], writes=[buf(c, "d_tok")], dma=True, tag=f"ls{ti}_tok")
        for bt in range(4):
            for k8 in range(8):
                k = bt * 8 + k8
                p.op("tensor", lambda t, k=k, k8=k8: t.transpose(
                    c.ps_x[:, 128 * k8:128 * (k8 + 1)], nh[:, 128 * k:128 * (k + 1)],
                    c.cidb[:]),
                    reads=[b_nh, b_idb], writes=[b_psx], tag=f"ls{ti}_tr{k}")
            p.op("vector", lambda vv, bt=bt: vv.tensor_copy(
                out=fm[:, 1024 * bt:1024 * (bt + 1)], in_=c.ps_x[:]),
                reads=[b_psx], writes=[b_fm], tag=f"ls{ti}_te{bt}")
        fmw = p.op("scalar", lambda s, ti=ti: s.dma_start(
            out=c.ha_fm[ti].rearrange("(cd p) t -> p cd t", p=128),
            in_=fm.rearrange("p (cd t) -> p cd t", t=128)),
            reads=[b_fm], writes=[buf(c, "d_fm")], dma=True, tag=f"ls{ti}_fw")
        c.ha_done.append((tok, fmw))


def dequant_units(c):
    p = c.p
    sr_f = ar(c, 10368, 1024)
    pk_i = ar(c, 11392, 1024, I32)
    nb_i = ar(c, 12416, 1024, I32)
    ou_b = ar(c, 13440, 1024, BF16)
    b_sr, b_pk, b_ni, b_ou = (buf(c, "ar_dqsr"), buf(c, "ar_dqpk"),
                              buf(c, "ar_dqni"), buf(c, "ar_dqou"))
    units = []
    for mat in "gud":
        nk = D if mat in "gu" else INTER
        nn = INTER if mat in "gu" else D
        nrow = nk // 8
        rtiles = [(R, min(128, nrow - R)) for R in range(0, nrow, 128)]
        for Ri, (R0, rows) in enumerate(rtiles):
            for blk in range((nn + 1023) // 1024):
                ncols = min(1024, nn - 1024 * blk)
                units.append((mat, Ri, R0, rows, blk, ncols))

    def emit_unit(u, eng):
        mat, Ri, R0, rows, blk, ncols = u
        tg = f"dq{mat}{Ri}_{blk}"
        p.op("sync", lambda g: g.dma_start(
            out=pk_i[0:rows, 0:ncols], in_=c.qw[mat][R0:R0 + rows, 1024 * blk:1024 * blk + ncols]),
            writes=[b_pk], deps=c.pdeps, dma=True, tag=tg + "pk")
        p.op("sync", lambda g: g.dma_start(
            out=sr_f[0:rows, 0:ncols], in_=c.srep[mat][R0:R0 + rows, 1024 * blk:1024 * blk + ncols]),
            writes=[b_sr], deps=c.pdeps, dma=True, tag=tg + "sr")
        nh2 = (ncols + 511) // 512
        for j in range(8):
            for hf in range(nh2):
                hc = min(512, ncols - 512 * hf)
                if mat in "gu":
                    rw = ncols // 8
                    r0h = 64 * hf
                    rcnt = hc // 8
                    p.op("vector", lambda E, j=j, hf=hf, hc=hc, rw=rw, r0h=r0h,
                         rcnt=rcnt, ncols=ncols: E.tensor_scalar(
                        out=nb_i[0:rows, 0:ncols].rearrange(
                            "p (j2 r2) -> p r2 j2", j2=8)[:, r0h:r0h + rcnt, :],
                        in0=pk_i[0:rows, 512 * hf:512 * hf + hc].rearrange(
                            "p (r2 j2) -> p r2 j2", j2=8),
                        scalar1=4 * j, scalar2=15, op0=OP.logical_shift_right,
                        op1=OP.bitwise_and),
                        reads=[b_pk], writes=[b_ni], tag=f"{tg}_{j}{hf}x")
                else:
                    p.op("vector", lambda E, j=j, hf=hf, hc=hc: E.tensor_scalar(
                        out=nb_i[0:rows, 512 * hf:512 * hf + hc],
                        in0=pk_i[0:rows, 512 * hf:512 * hf + hc],
                        scalar1=4 * j, scalar2=15, op0=OP.logical_shift_right,
                        op1=OP.bitwise_and),
                        reads=[b_pk], writes=[b_ni], tag=f"{tg}_{j}{hf}x")
            p.op(eng, lambda E, ncols=ncols: E.tensor_tensor(
                out=ou_b[0:rows, 0:ncols], in0=nb_i[0:rows, 0:ncols],
                in1=sr_f[0:rows, 0:ncols], op=OP.mult),
                reads=[b_ni, b_sr], writes=[b_ou], tag=f"{tg}_{j}m")
            p.op("sync", lambda g, j=j: g.dma_start(
                out=c.wdq[mat][1024 * Ri + rows * j:1024 * Ri + rows * (j + 1),
                               1024 * blk:1024 * blk + ncols],
                in_=ou_b[0:rows, 0:ncols]),
                reads=[b_ou], dma=True, tag=f"{tg}_{j}w")
    return units, emit_unit


def stage_attention(c):
    p = c.p
    b_idb = buf(c, "cidb")
    b_psxs = [buf(c, "ps_xA"), buf(c, "ps_xB")]
    b_psa = buf(c, "ps_aA")
    b_pss = [buf(c, "ps_sA"), buf(c, "ps_sB")]
    b_vecs = buf(c, "vecs")
    b_bc = buf(c, "bcast")
    xt0 = ar(c, 0, D)
    b_xt0 = buf(c, "ar_xt0")
    nh = ar(c, 8192, 4096, BF16)
    b_nh = buf(c, "ar_nh")
    qpr = ar(c, 4096, 8192, BF16)
    b_qpr = buf(c, "ar_qpr")
    kts = [ar(c, 13952 + 2048 * i, 4096, BF16) for i in range(2)]
    b_kt = [buf(c, f"ar_kt{i}") for i in range(2)]
    vts = [ar(c, 18048 + 2048 * i, 4096, BF16) for i in range(4)]
    b_vt = [buf(c, f"ar_vt{i}") for i in range(4)]
    x2s = ar(c, 26240, 8192)
    b_x2s = buf(c, "ar_x2s")
    ats = [ar(c, 34432 + 128 * i, 256, BF16) for i in range(16)]
    b_at = [buf(c, f"ar_at{i}") for i in range(16)]
    msks = [ar(c, 36480 + 128 * i, 256, BF16) for i in range(2)]
    b_msk = [buf(c, f"ar_msk{i}") for i in range(2)]

    c.x2_done = []
    for q in range(4):
        for half in range(2):
            r0 = q * CH + half * 128
            ln_tile(c, c.x_own[r0:r0 + 128, :], xt0, b_xt0, nh, b_nh,
                    56 * (half % 2), f"qp{q}_{half}", 0, D,
                    extra_deps=([c.bc_done["u"], c.bc_done["sa"]] if q == 0 else []))
            for bt in range(4):
                psx = c.ps_x if bt % 2 == 0 else c.ps_x2
                b_psx = b_psxs[bt % 2]
                for k8 in range(8):
                    k = bt * 8 + k8
                    p.op("tensor", lambda t, k=k, k8=k8, psx=psx: t.transpose(
                        psx[:, 128 * k8:128 * (k8 + 1)], nh[:, 128 * k:128 * (k + 1)],
                        c.cidb[:]),
                        reads=[b_nh, b_idb], writes=[b_psx], tag=f"qp{q}{half}tr{k}")
                for k8 in range(8):
                    k = bt * 8 + k8
                    p.op("vector", lambda vv, k=k, k8=k8, half=half, psx=psx,
                         b_psx=b_psx: vv.tensor_scalar(
                        out=qpr[:, 256 * k + 128 * half:256 * k + 128 * half + 128],
                        in0=psx[:, 128 * k8:128 * (k8 + 1)],
                        scalar1=vcol(c, "qkb", k, 1), scalar2=None, op0=OP.mult),
                        reads=[b_psx, b_vecs], writes=[b_qpr], tag=f"qp{q}{half}e{k}")
        span = PAD_SPAN[q]
        first_flash = True
        for r0m in range(0, span, R_RUN):
            run = list(range(r0m, min(r0m + R_RUN, span)))
            for m in run:
                ks, bk = kts[m % 2], b_kt[m % 2]
                vs, bv = vts[m % 4], b_vt[m % 4]
                p.op("scalar", lambda s, m=m, ks=ks: s.dma_start(
                    out=ks.rearrange("p (cd t) -> p cd t", t=128),
                    in_=c.ha_fm[m].rearrange("(cd p) t -> p cd t", p=128)),
                    reads=[buf(c, "d_fm")], writes=[bk], deps=c.pdeps, tag=f"at{q}k{m}", dma=True)
                p.op("scalar", lambda s, m=m, vs=vs: s.dma_start(
                    out=vs, in_=c.ha_tok[MK * m:MK * (m + 1), :]),
                    reads=[buf(c, "d_tok")], writes=[bv], deps=c.pdeps, tag=f"at{q}v{m}", dma=True)
                mi = m % 2
                si = sum(PAD_SPAN[:q]) + m
                p.op("gpsimd", lambda g, si=si, mi=mi: g.dma_start(
                    out=msks[mi], in_=c.masks[si]),
                    writes=[b_msk[mi]], deps=c.pdeps, dma=True, tag=f"at{q}m{m}")
                if getattr(c, "dq_tick", None):
                    c.dq_tick()
                for h in range(H):
                    sb = (h % 2)
                    pso = 512 * sb
                    ati = (m % 4) * 4 + h
                    for dk in range(8):
                        kc = h * 8 + dk
                        p.op("tensor", lambda t, kc=kc, pso=pso, ks=ks: t.matmul(
                            c.ps_s[:, pso:pso + 256], ks[:, 128 * kc:128 * (kc + 1)],
                            qpr[:, 256 * kc:256 * (kc + 1)],
                            start=(kc % 8 == 0), stop=(kc % 8 == 7)),
                            reads=[bk, b_qpr], writes=[b_pss[sb]], tag=f"s{q}{m}{h}{dk}")
                    p.op("scalar", lambda s, pso=pso, ati=ati: s.activation(
                        out=ats[ati], in_=c.ps_s[:, pso:pso + 256],
                        func=AF.Sigmoid, scale=0.125),
                        reads=[b_pss[sb]], writes=[b_at[ati]], tag=f"sg{q}{m}{h}")
                    p.op("gpsimd", lambda vv, ati=ati, mi=mi: vv.tensor_tensor(
                        out=ats[ati], in0=ats[ati], in1=msks[mi], op=OP.mult),
                        reads=[b_at[ati], b_msk[mi]], writes=[b_at[ati]], tag=f"mk{q}{m}{h}")
            for h in range(H):
                for qs in range(2):
                    for m in run:
                        ati = (m % 4) * 4 + h
                        vs, bv = vts[m % 4], b_vt[m % 4]
                        for dh in range(2):
                            p.op("tensor", lambda t, ati=ati, qs=qs, dh=dh, h=h, vs=vs,
                                 st=(m == run[0]), sp=(m == run[-1]): t.matmul(
                                c.ps_a[:, 512 * dh:512 * (dh + 1)],
                                ats[ati][:, 128 * qs:128 * (qs + 1)],
                                vs[:, 1024 * h + 512 * dh:1024 * h + 512 * (dh + 1)],
                                start=st, stop=sp),
                                reads=[b_at[ati], bv], writes=[b_psa], tag=f"av{q}{m}{h}{qs}{dh}")
                    if first_flash:
                        p.op("vector", lambda vv, qs=qs, h=h: vv.tensor_copy(
                            out=x2s[:, 4096 * qs + 1024 * h:4096 * qs + 1024 * (h + 1)],
                            in_=c.ps_a[:, 0:1024]),
                            reads=[b_psa], writes=[b_x2s], deps=c.pdeps, tag=f"fl{q}{r0m}{h}{qs}")
                    else:
                        p.op("vector", lambda vv, qs=qs, h=h: vv.tensor_tensor(
                            out=x2s[:, 4096 * qs + 1024 * h:4096 * qs + 1024 * (h + 1)],
                            in0=x2s[:, 4096 * qs + 1024 * h:4096 * qs + 1024 * (h + 1)],
                            in1=c.ps_a[:, 0:1024], op=OP.add),
                            reads=[b_psa, b_x2s], writes=[b_x2s], tag=f"fl{q}{r0m}{h}{qs}")
            first_flash = False
        for qs in range(2):
            r0 = q * CH + qs * 128
            p.op("scalar", lambda s, r0=r0: s.dma_start(
                out=xt0, in_=c.x_own[r0:r0 + 128, :]),
                writes=[b_xt0], dma=True, tag=f"ev{q}{qs}ld")
            p.op("vector", lambda vv, qs=qs: vv.tensor_tensor(
                out=x2s[:, 4096 * qs:4096 * (qs + 1)],
                in0=x2s[:, 4096 * qs:4096 * (qs + 1)],
                in1=c.bcast[:, 2 * D:3 * D], op=OP.mult),
                reads=[b_x2s, b_bc], writes=[b_x2s], tag=f"ev{q}{qs}g")
            p.op("vector", lambda vv, qs=qs: vv.tensor_tensor(
                out=x2s[:, 4096 * qs:4096 * (qs + 1)],
                in0=x2s[:, 4096 * qs:4096 * (qs + 1)],
                in1=xt0, op=OP.add),
                reads=[b_x2s, b_xt0], writes=[b_x2s], tag=f"ev{q}{qs}a")
            w = p.op("gpsimd", lambda g, r0=r0, qs=qs: g.dma_start(
                out=c.x2_dram[r0:r0 + 128, :], in_=x2s[:, 4096 * qs:4096 * (qs + 1)]),
                reads=[b_x2s], writes=[buf(c, "d_x2")], dma=True, tag=f"ev{q}{qs}w")
            c.x2_done.append(w)


def stage_lnb_ffn(c):
    p = c.p
    if getattr(c, "want_debug", False):
        c.dbg_dst_ap = c.nc.dram_tensor("dbg_dst", [128, 2048], F32, kind="ExternalOutput").ap()
        c.dbg_frg_ap = c.nc.dram_tensor("dbg_frg", [128, 512], F32, kind="ExternalOutput").ap()
        c.dbg_x2c_ap = c.nc.dram_tensor("dbg_x2c", [128, 512], F32, kind="ExternalOutput").ap()
    b_idb = buf(c, "cidb")
    b_psxs = [buf(c, "ps_xA"), buf(c, "ps_xB")]
    b_vecs = buf(c, "vecs")
    b_bc = buf(c, "bcast")
    b_pst = buf(c, "ps_t")
    b_one = buf(c, "cone1")
    b_idf = buf(c, "cidf")
    b_onesr = buf(c, "cones")
    ones1 = c.cones[0:1, 128:129]

    mrow = ar(c, 0, 4096)
    b_mrow = buf(c, "ar_xt0")
    for gi, gn in enumerate(("u2", "sb", "gb")):
        for k in range(DCH):
            p.op("tensor", lambda t, gn=gn, k=k: t.matmul(
                c.ps_t[0:1, 0:128], vcol(c, gn, k, 1), c.cidf[:],
                start=True, stop=True),
                reads=[b_vecs, b_idf], writes=[b_pst], deps=c.pdeps, tag=f"r3{gn}{k}")
            p.op("vector", lambda vv, k=k: vv.tensor_copy(
                out=mrow[0:1, 128 * k:128 * (k + 1)], in_=c.ps_t[0:1, 0:128]),
                reads=[b_pst], writes=[b_mrow], deps=c.pdeps, tag=f"re3{gn}{k}")
        for s in range(8):
            p.op("tensor", lambda t, s=s: t.matmul(
                c.ps_t[:, 0:512], c.cones[0:1, 0:128], mrow[0:1, 512 * s:512 * (s + 1)],
                start=True, stop=True),
                reads=[b_mrow, b_onesr], writes=[b_pst], tag=f"b3{gn}{s}")
            p.op("vector", lambda vv, gi=gi, s=s: vv.tensor_copy(
                out=c.bcast[:, gi * D + 512 * s:gi * D + 512 * (s + 1)], in_=c.ps_t[:, 0:512]),
                reads=[b_pst], writes=[b_bc], tag=f"be3{gn}{s}")

    ht = ar(c, 10368, 32768, BF16)
    b_ht = buf(c, "ar_ht")
    xt0 = ar(c, 0, D)
    b_xt0 = buf(c, "ar_xt0")
    nh = ar(c, 8192, 4096, BF16)
    b_nh = buf(c, "ar_nh")
    pch = perm_chunk_list(D)
    for ti in range(8):
        ln_tile(c, c.x2_dram[MK * ti:MK * (ti + 1), :], xt0, b_xt0, nh, b_nh,
                56 * (ti % 2), f"lb{ti}", 0, D, extra_deps=c.x2_done)
        for bt in range(4):
            psx = c.ps_x if bt % 2 == 0 else c.ps_x2
            b_psx2 = b_psxs[bt % 2]
            for k8 in range(8):
                kc = bt * 8 + k8
                R0, rows, j, pos = pch[kc]
                p.op("tensor", lambda t, k8=k8, R0=R0, j=j, psx=psx: t.transpose(
                    psx[:, 128 * k8:128 * (k8 + 1)],
                    nh[:, 8 * R0 + j:8 * R0 + j + 1017:8],
                    c.cidb[:]),
                    reads=[b_nh, b_idb], writes=[b_psx2], tag=f"lb{ti}tr{kc}")
            p.op("vector", lambda vv, bt=bt, ti=ti, psx=psx, b_psx2=b_psx2: vv.tensor_copy(
                out=ht.rearrange("p (kc t) -> p kc t", t=1024)[
                    :, bt * 8:(bt + 1) * 8, ti * 128:(ti + 1) * 128],
                in_=psx[:].rearrange("p (k8 t) -> p k8 t", t=128)),
                reads=[b_psx2], writes=[b_ht], tag=f"lb{ti}te{bt}")

    b_psa = buf(c, "ps_aA")
    b_hsh = buf(c, "hsumh")
    b_indh = buf(c, "cindh")
    for kc in range(DCH):
        Ri = kc // 8
        for th in range(2):
            p.op("tensor", lambda t, kc=kc, Ri=Ri, th=th: t.matmul(
                c.ps_a[0:32, 512 * th:512 * (th + 1)],
                c.cindh[:, 32 * Ri:32 * (Ri + 1)],
                ht.rearrange("p (kc t) -> p kc t", t=1024)[:, kc, 512 * th:512 * (th + 1)],
                start=(kc == 0), stop=(kc == DCH - 1)),
                reads=[b_ht, b_indh], writes=[b_psa], tag=f"hsh{kc}{th}")
    p.op("vector", lambda vv: vv.tensor_copy(out=c.hsumh[:], in_=c.ps_a[0:32, 0:1024]),
         reads=[b_psa], writes=[b_hsh], tag="hsh_ev")
    b_hsl = buf(c, "hsuml")
    p.op("vector", lambda vv: vv.tensor_tensor(
        out=c.hsuml[:], in0=c.ps_a[0:32, 0:1024], in1=c.hsumh[:], op=OP.subtract),
        reads=[b_psa, b_hsh], writes=[b_hsl], tag="hsl_ev")

    wgs = [ar(c, 26752, 4096, BF16), ar(c, 32640, 4096, BF16)]
    wus = [ar(c, 28800, 4096, BF16), ar(c, 34688, 4096, BF16)]
    b_wgs = [buf(c, "ar_wgA"), buf(c, "ar_wgB")]
    b_wus = [buf(c, "ar_wuA"), buf(c, "ar_wuB")]
    sil = ar(c, 30848, 1024)
    b_sil = buf(c, "ar_sil")
    vst = ar(c, 31872, 1024, BF16)
    b_vst = buf(c, "ar_vst")
    szt = ar(c, 32384, 512, BF16)
    b_szt = buf(c, "ar_szt")
    b_pss = buf(c, "ps_sA")
    b_pstu = [buf(c, "ps_t"), buf(c, "ps_u")]
    b_hsv = buf(c, "hsumv")
    b_indv = buf(c, "cindv")
    pchI = perm_chunk_list(INTER)
    for ci, (R0, rows, j, pos) in enumerate(pchI):
        s2 = ci % 2
        wg, b_wg = wgs[s2], b_wgs[s2]
        wu, b_wu = wus[s2], b_wus[s2]
        p.op("sync", lambda g, pos=pos, rows=rows, wg=wg: g.dma_start(
            out=wg[:, 0:32 * rows].rearrange("p (kc r) -> p kc r", kc=32),
            in_=c.wdq["g"].rearrange("(kc p) n -> p kc n", p=128)[:, :, pos:pos + rows]),
            writes=[b_wg], deps=c.pdeps, dma=True, tag=f"gw{ci}")
        p.op("sync", lambda g, pos=pos, rows=rows, wu=wu: g.dma_start(
            out=wu[:, 0:32 * rows].rearrange("p (kc r) -> p kc r", kc=32),
            in_=c.wdq["u"].rearrange("(kc p) n -> p kc n", p=128)[:, :, pos:pos + rows]),
            writes=[b_wu], deps=c.pdeps, dma=True, tag=f"uw{ci}")
        p.op("gpsimd", lambda g, pos=pos, rows=rows: g.dma_start(
            out=szt[0:32, 0:rows], in_=c.sz_dram["g"][0:32, pos:pos + rows]),
            writes=[b_szt], deps=c.pdeps + [c.sz_done], dma=True, tag=f"szg{ci}")
        p.op("gpsimd", lambda g, pos=pos, rows=rows: g.dma_start(
            out=szt[0:32, rows:2 * rows], in_=c.sz_dram["u"][0:32, pos:pos + rows]),
            reads=[b_szt], writes=[b_szt], dma=True, tag=f"szu{ci}")
        for th in range(2):
            for kc in range(DCH):
                p.op("tensor", lambda t, kc=kc, th=th, rows=rows, wg=wg: t.matmul(
                    c.ps_a[0:rows, 512 * th:512 * th + 512],
                    wg[:, kc * rows:(kc + 1) * rows],
                    ht.rearrange("p (kc t) -> p kc t", t=1024)[:, kc, 512 * th:512 * th + 512],
                    start=(kc == 0), stop=False),
                    reads=[b_wg, b_ht], writes=[b_psa], tag=f"g{ci}{th}{kc}")
                p.op("tensor", lambda t, kc=kc, th=th, rows=rows, wu=wu: t.matmul(
                    c.ps_s[0:rows, 512 * th:512 * th + 512],
                    wu[:, kc * rows:(kc + 1) * rows],
                    ht.rearrange("p (kc t) -> p kc t", t=1024)[:, kc, 512 * th:512 * th + 512],
                    start=(kc == 0), stop=False),
                    reads=[b_wu, b_ht], writes=[b_pss], tag=f"u{ci}{th}{kc}")
            p.op("tensor", lambda t, th=th, rows=rows: t.matmul(
                c.ps_a[0:rows, 512 * th:512 * th + 512], szt[0:32, 0:rows],
                c.hsumh[:, 512 * th:512 * th + 512], start=False, stop=False),
                reads=[b_szt, b_hsh], writes=[b_psa], tag=f"gz{ci}{th}")
            p.op("tensor", lambda t, th=th, rows=rows: t.matmul(
                c.ps_a[0:rows, 512 * th:512 * th + 512], szt[0:32, 0:rows],
                c.hsuml[:, 512 * th:512 * th + 512], start=False, stop=True),
                reads=[b_szt, b_hsl], writes=[b_psa], tag=f"gzl{ci}{th}")
            p.op("tensor", lambda t, th=th, rows=rows: t.matmul(
                c.ps_s[0:rows, 512 * th:512 * th + 512], szt[0:32, rows:2 * rows],
                c.hsumh[:, 512 * th:512 * th + 512], start=False, stop=False),
                reads=[b_szt, b_hsh], writes=[b_pss], tag=f"uz{ci}{th}")
            p.op("tensor", lambda t, th=th, rows=rows: t.matmul(
                c.ps_s[0:rows, 512 * th:512 * th + 512], szt[0:32, rows:2 * rows],
                c.hsuml[:, 512 * th:512 * th + 512], start=False, stop=True),
                reads=[b_szt, b_hsl], writes=[b_pss], tag=f"uzl{ci}{th}")
            p.op("scalar", lambda s, th=th, rows=rows: s.activation(
                out=sil[0:rows, 512 * th:512 * th + 512],
                in_=c.ps_a[0:rows, 512 * th:512 * th + 512], func=AF.Silu),
                reads=[b_psa], writes=[b_sil], tag=f"sl{ci}{th}")
            p.op("vector", lambda vv, th=th, rows=rows: vv.tensor_tensor(
                out=vst[0:rows, 512 * th:512 * th + 512],
                in0=sil[0:rows, 512 * th:512 * th + 512],
                in1=c.ps_s[0:rows, 512 * th:512 * th + 512], op=OP.mult),
                reads=[b_sil, b_pss], writes=[b_vst], tag=f"vm{ci}{th}")
            p.op("tensor", lambda t, th=th, rows=rows, Ri=R0 // 128,
                 st=(ci == 0), sp=(ci == len(pchI) - 1): t.matmul(
                (c.ps_t if th == 0 else c.ps_u)[0:86, 0:512],
                c.cindv[:, 86 * Ri:86 * (Ri + 1)][0:rows, :],
                vst[0:rows, 512 * th:512 * th + 512],
                start=st, stop=sp),
                reads=[b_vst, b_indv], writes=[b_pstu[th]], tag=f"hv{ci}{th}")
        p.op("gpsimd", lambda g, pos=pos, rows=rows: g.dma_start(
            out=c.v_dram[pos:pos + rows, :], in_=vst[0:rows, :]),
            reads=[b_vst], writes=[buf(c, "d_v")], dma=True, tag=f"vw{ci}")
    c.ffn_gu_done = []
    for th in range(2):
        ev = p.op("vector", lambda vv, th=th: vv.tensor_copy(
            out=c.hsumv[:, 512 * th:512 * th + 512],
            in_=(c.ps_t if th == 0 else c.ps_u)[0:86, 0:512]),
            reads=[b_pstu[th]], writes=[b_hsv], tag=f"hv_ev{th}")
        c.ffn_gu_done.append(ev)

    if getattr(c, "want_debug", False):
        c.dbg_gbb_ap = c.nc.dram_tensor("dbg_gbb", [128, D], BF16, kind="ExternalOutput").ap()
        p.op("gpsimd", lambda g: g.dma_start(out=c.dbg_gbb_ap, in_=c.bcast[:, 2 * D:3 * D]),
             reads=[b_bc], writes=[buf(c, "d_dbggbb")], dma=True, tag="dbg_gbb")
    wdts = [ar(c, 10368 + 256 * s, 512, BF16) for s in range(4)]
    b_wdts = [buf(c, f"ar_wdt{s}") for s in range(4)]
    vchs = [ar(c, 11392 + 256 * s, 512, BF16) for s in range(4)]
    b_vchs = [buf(c, f"ar_vch{s}") for s in range(4)]
    szd = ar(c, 12416, 512, BF16)
    b_szd = buf(c, "ar_szd")
    dstb = ar(c, 12672, 2048, BF16)
    b_dstb = buf(c, "ar_dstb")
    frgs = [ar(c, 13696 + 512 * s, 512) for s in range(2)]
    b_frgs = [buf(c, f"ar_frg{s}") for s in range(2)]
    x2cs = [ar(c, 14720 + 512 * s, 512) for s in range(2)]
    b_x2cs = [buf(c, f"ar_x2c{s}") for s in range(2)]
    dst_ = ar(c, 15744, 2048)
    b_dst = buf(c, "ar_dst")
    psq = [(c.ps_a, 0), (c.ps_a, 512), (c.ps_s, 0), (c.ps_s, 512)]
    b_psq = [buf(c, "ps_aA"), buf(c, "ps_aB"), buf(c, "ps_sA"), buf(c, "ps_sB2")]
    b_psxH = [buf(c, "ps_xA"), buf(c, "ps_xB")]
    b_out = buf(c, "d_out")
    c.out_done = []
    for blkn in range(8):
        for th in range(2):
            p.op("gpsimd", lambda g, blkn=blkn: g.dma_start(
                out=szd[0:86, 0:512], in_=c.sz_dram["d"][0:86, 512 * blkn:512 * (blkn + 1)]),
                writes=[b_szd], deps=c.pdeps + [c.sz_done] + c.ffn_gu_done, dma=True, tag=f"dz{blkn}{th}")
            for ci, (R0, rows, j, pos) in enumerate(pchI):
                s4 = ci % 4
                wdt, b_wdt = wdts[s4], b_wdts[s4]
                vch, b_vch = vchs[s4], b_vchs[s4]
                p.op("sync", lambda g, pos=pos, rows=rows, blkn=blkn, wdt=wdt: g.dma_start(
                    out=wdt[0:rows, 0:512],
                    in_=c.wdq["d"][pos:pos + rows, 512 * blkn:512 * (blkn + 1)]),
                    writes=[b_wdt], deps=c.pdeps + c.ffn_gu_done, dma=True, tag=f"dw{blkn}{th}{ci}")
                p.op("scalar", lambda s, pos=pos, rows=rows, th=th, vch=vch: s.dma_start(
                    out=vch[0:rows, 0:512],
                    in_=c.v_dram[pos:pos + rows, 512 * th:512 * (th + 1)]),
                    reads=[buf(c, "d_v")], writes=[b_vch],
                    deps=c.pdeps + c.ffn_gu_done, dma=True, tag=f"dv{blkn}{th}{ci}")
                for sub in range(4):
                    ps, off = psq[sub]
                    p.op("tensor", lambda t, sub=sub, ps=ps, off=off, rows=rows,
                         st=(ci == 0), wdt=wdt, vch=vch: t.matmul(
                        ps[:, off:off + 512], wdt[0:rows, 128 * sub:128 * (sub + 1)],
                        vch[0:rows, 0:512], start=st, stop=False),
                        reads=[b_wdt, b_vch], writes=[b_psq[sub]], tag=f"dm{blkn}{th}{ci}{sub}")
            for sub in range(4):
                ps, off = psq[sub]
                p.op("tensor", lambda t, sub=sub, ps=ps, off=off, th=th: t.matmul(
                    ps[:, off:off + 512], szd[0:86, 128 * sub:128 * (sub + 1)],
                    c.hsumv[:, 512 * th:512 * (th + 1)], start=False, stop=True),
                    reads=[b_szd, buf(c, "hsumv")], writes=[b_psq[sub]], tag=f"dzm{blkn}{th}{sub}")
                p.op("vector", lambda vv, sub=sub, ps=ps, off=off: vv.tensor_copy(
                    out=dst_[:, 512 * sub:512 * (sub + 1)], in_=ps[:, off:off + 512]),
                    reads=[b_psq[sub]], writes=[b_dst], tag=f"de{blkn}{th}{sub}")
            if blkn == 0 and th == 0 and hasattr(c, "dbg_dst_ap"):
                p.op("gpsimd", lambda g: g.dma_start(out=c.dbg_dst_ap, in_=dst_[:, 0:2048]),
                     reads=[b_dst], writes=[buf(c, "d_dbgdst")], dma=True, tag="dbg_dst")
            for sub in range(4):
                p.op("vector", lambda vv, sub=sub: vv.tensor_copy(
                    out=dstb[:, 512 * sub:512 * (sub + 1)],
                    in_=dst_[:, 512 * sub:512 * (sub + 1)]),
                    reads=[b_dst], writes=[b_dstb], tag=f"db{blkn}{th}{sub}")
            for ts_ in range(4):
                hx = ts_ % 2
                psx = c.ps_x if hx == 0 else c.ps_x2
                frg, b_frg = frgs[hx], b_frgs[hx]
                x2c, b_x2c = x2cs[hx], b_x2cs[hx]
                for sub in range(4):
                    p.op("tensor", lambda t, ts_=ts_, sub=sub, psx=psx: t.transpose(
                        psx[:, 128 * sub:128 * (sub + 1)],
                        dstb[:, 512 * sub + 128 * ts_:512 * sub + 128 * (ts_ + 1)],
                        c.cidb[:]),
                        reads=[b_dstb, buf(c, "cidb")], writes=[b_psxH[hx]],
                        tag=f"dt{blkn}{th}{ts_}{sub}")
                p.op("scalar", lambda s, blkn=blkn, th=th, ts_=ts_, x2c=x2c: s.dma_start(
                    out=x2c[:, 0:512],
                    in_=c.x2_dram[512 * th + 128 * ts_:512 * th + 128 * (ts_ + 1),
                                  512 * blkn:512 * (blkn + 1)]),
                    reads=[buf(c, "d_x2")], writes=[b_x2c], deps=c.ffn_gu_done, dma=True, tag=f"dx{blkn}{th}{ts_}")
                p.op("vector", lambda vv, blkn=blkn, hx=hx, frg=frg, psx=psx: vv.tensor_tensor(
                    out=frg[:, 0:512], in0=psx[:, 0:512],
                    in1=c.bcast[:, 2 * D + 512 * blkn:2 * D + 512 * (blkn + 1)], op=OP.mult),
                    reads=[b_psxH[hx], b_bc], writes=[b_frg], tag=f"df{blkn}{th}{ts_}"),
                p.op("vector", lambda vv, frg=frg, x2c=x2c: vv.tensor_tensor(
                    out=frg[:, 0:512], in0=frg[:, 0:512], in1=x2c[:, 0:512], op=OP.add),
                    reads=[b_frg, b_x2c], writes=[b_frg], tag=f"dg{blkn}{th}{ts_}")

                w = p.op("gpsimd", lambda g, blkn=blkn, th=th, ts_=ts_, frg=frg: g.dma_start(
                    out=c.out[512 * th + 128 * ts_:512 * th + 128 * (ts_ + 1),
                              512 * blkn:512 * (blkn + 1)],
                    in_=frg[:, 0:512]),
                    reads=[b_frg], writes=[b_out], dma=True, tag=f"do{blkn}{th}{ts_}")
                c.out_done.append(w)


def stage_sz(c):
    p = c.p
    zld = ar(c, 24192, 1536, I32)
    b_zld = buf(c, "ar_zld")
    zpl = ar(c, 25728, 1536)
    b_zpl = buf(c, "ar_zpl")
    scb = ar(c, 27264, 1024)
    b_scb = buf(c, "ar_scb")
    szb = ar(c, 28288, 1024, BF16)
    b_szb = buf(c, "ar_szb")
    last = None
    for mat in "gud":
        G = {"g": 32, "u": 32, "d": 86}[mat]
        nn = INTER if mat in "gu" else D
        ncpk = nn // 8
        p.op("gpsimd", lambda g, mat=mat, G=G, ncpk=ncpk: g.dma_start(
            out=zld[0:G, 0:ncpk], in_=c.qz[mat]),
            writes=[b_zld], dma=True, tag=f"sz_{mat}_ld")
        nblocks = (nn + 1023) // 1024
        for blk in range(nblocks):
            ncols = min(1024, nn - 1024 * blk)
            p.op("gpsimd", lambda g, mat=mat, G=G, blk=blk, ncols=ncols: g.dma_start(
                out=scb[0:G, 0:ncols], in_=c.scp[mat][0:G, 1024 * blk:1024 * blk + ncols]),
                writes=[b_scb], dma=True, tag=f"sz_{mat}{blk}_sc")
            for pn in range(8):
                cw = ncols // 8
                cc0 = 128 * blk
                p.op("vector", lambda vv, G=G, pn=pn, cw=cw, cc0=cc0: vv.tensor_scalar(
                    out=zpl[0:G, 0:cw].bitcast(I32), in0=zld[0:G, cc0:cc0 + cw],
                    scalar1=4 * pn, scalar2=15, op0=OP.logical_shift_right,
                    op1=OP.bitwise_and),
                    reads=[b_zld], writes=[b_zpl], tag=f"sz_{mat}{blk}{pn}x")
                p.op("vector", lambda vv, G=G, cw=cw: vv.tensor_scalar(
                    out=zpl[0:G, 512:512 + cw], in0=zpl[0:G, 0:cw].bitcast(I32),
                    scalar1=1.0, scalar2=-1.0, op0=OP.add, op1=OP.mult),
                    reads=[b_zpl], writes=[b_zpl], tag=f"sz_{mat}{blk}{pn}n")
                if mat in "gu":
                    p.op("vector", lambda vv, G=G, pn=pn, cw=cw: vv.tensor_tensor(
                        out=szb[0:G, pn * cw:(pn + 1) * cw], in0=zpl[0:G, 512:512 + cw],
                        in1=scb[0:G, pn * cw:(pn + 1) * cw], op=OP.mult),
                        reads=[b_zpl, b_scb], writes=[b_szb], tag=f"sz_{mat}{blk}{pn}m")
                else:
                    p.op("vector", lambda vv, G=G, pn=pn, cw=cw: vv.tensor_tensor(
                        out=szb[0:G, pn:pn + 8 * (cw - 1) + 1:8],
                        in0=zpl[0:G, 512:512 + cw],
                        in1=scb[0:G, pn:pn + 8 * (cw - 1) + 1:8], op=OP.mult),
                        reads=[b_zpl, b_scb], writes=[b_szb], tag=f"sz_{mat}{blk}{pn}m")
            last = p.op("gpsimd", lambda g, mat=mat, G=G, blk=blk, ncols=ncols: g.dma_start(
                out=c.sz_dram[mat][0:G, 1024 * blk:1024 * blk + ncols],
                in_=szb[0:G, 0:ncols]),
                reads=[b_szb], writes=[buf(c, f"d_sz{mat}")], dma=True, tag=f"sz_{mat}{blk}w2")
    c.sz_done = last


def stage_debug(c):
    p = c.p
    nc = c.nc
    dbg = {}
    dbg["vecs"] = nc.dram_tensor("dbg_vecs", [128, 12 * DCH], F32, kind="ExternalOutput").ap()
    dbg["bcast"] = nc.dram_tensor("dbg_bcast", [128, 3 * D], BF16, kind="ExternalOutput").ap()
    dbg["ha"] = nc.dram_tensor("dbg_ha", [256, D], BF16, kind="ExternalOutput").ap()
    dbg["fm"] = nc.dram_tensor("dbg_fm", [D, MK], BF16, kind="ExternalOutput").ap()
    dbg["x2"] = nc.dram_tensor("dbg_x2", [TOK, D], F32, kind="ExternalOutput").ap()
    dbg["wg"] = nc.dram_tensor("dbg_wg", [256, 2048], BF16, kind="ExternalOutput").ap()
    dbg["szg"] = nc.dram_tensor("dbg_szg", [32, INTER], BF16, kind="ExternalOutput").ap()
    dbg["v"] = nc.dram_tensor("dbg_v", [256, TOK], BF16, kind="ExternalOutput").ap()
    dbg["vlast"] = nc.dram_tensor("dbg_vlast", [256, TOK], BF16, kind="ExternalOutput").ap()
    dbg["wd"] = nc.dram_tensor("dbg_wd", [256, 2048], BF16, kind="ExternalOutput").ap()
    dbg["hsv"] = nc.dram_tensor("dbg_hsv", [86, 1024], BF16, kind="ExternalOutput").ap()
    dbg["ht"] = nc.dram_tensor("dbg_ht", [128, 4096], BF16, kind="ExternalOutput").ap()
    c.dbg_outs = list(dbg.keys())
    ops = [
        ("vecs", lambda g: g.dma_start(out=dbg["vecs"], in_=c.vecs[:]), ["vecs"]),
        ("bcast", lambda g: g.dma_start(out=dbg["bcast"], in_=c.bcast[:]), ["bcast"]),
        ("ha", lambda g: g.dma_start(out=dbg["ha"], in_=c.ha_tok[0:256, :]), ["d_tok"]),
        ("fm", lambda g: g.dma_start(out=dbg["fm"], in_=c.ha_fm[3]), ["d_fm"]),
        ("x2", lambda g: g.dma_start(out=dbg["x2"], in_=c.x2_dram), ["d_x2"]),
        ("wg", lambda g: g.dma_start(out=dbg["wg"], in_=c.wdq["g"][0:256, 0:2048]), []),
        ("szg", lambda g: g.dma_start(out=dbg["szg"], in_=c.sz_dram["g"]), ["d_szg"]),
        ("v", lambda g: g.dma_start(out=dbg["v"], in_=c.v_dram[0:256, :]), ["d_v"]),
        ("vlast", lambda g: g.dma_start(out=dbg["vlast"], in_=c.v_dram[10752:11008, :]), ["d_v"]),
        ("wd", lambda g: g.dma_start(out=dbg["wd"], in_=c.wdq["d"][0:256, 0:2048]), []),
        ("hsv", lambda g: g.dma_start(out=dbg["hsv"], in_=c.hsumv[:]), ["hsumv"]),
        ("ht", lambda g: g.dma_start(out=dbg["ht"], in_=ar(c, 10368, 8192, BF16)[:, 0:4096]), ["ar_ht"]),
    ]
    for name, fn, rd in ops:
        w = p.op("gpsimd", fn, reads=[buf(c, n) for n in rd],
                 deps=c.out_done[-1:], dma=True, tag="dbg_" + name)
        p.wait("gpsimd", w)


class _SelfModule:
    def __getattr__(self, n):
        return globals()[n]


kb = K = _SelfModule()

def build_program(dq_engines=("vector",), debug=False):
    nc = bass.Bass("TRN2", target_bir_lowering=False)
    st = ExitStack()
    with st:
        c = K.declare(nc, st)
        K.stage_consts(c)
        K.stage_mod(c)
        K.stage_sz(c)
        units, emit_unit = K.dequant_units(c)
        p = c.p

        b_idb = K.buf(c, "cidb")
        b_psxs = [K.buf(c, "ps_xA"), K.buf(c, "ps_xB")]
        b_fms = [K.buf(c, "ar_fm"), K.buf(c, "ar_fm2")]
        fms = [K.ar(c, 29312, 4096, K.BF16), K.ar(c, 31360, 4096, K.BF16)]
        xts = [K.ar(c, 0, K.D), K.ar(c, 4096, K.D)]
        b_xts = [K.buf(c, "ar_xt0"), K.buf(c, "ar_xt1")]
        nhs = [K.ar(c, 8192, 4096, K.BF16), K.ar(c, 10368, 4096, K.BF16)]
        b_nhs = [K.buf(c, "ar_nh"), K.buf(c, "ar_nh2")]
        for ti in range(kb.T // kb.MK):
            xt, b_xt = xts[ti % 2], b_xts[ti % 2]
            nh, b_nh = nhs[ti % 2], b_nhs[ti % 2]
            fm, b_fm = fms[ti % 2], b_fms[ti % 2]
            K.ln_tile(c, c.x_full[kb.MK * ti:kb.MK * (ti + 1), :], xt, b_xt, nh, b_nh,
                      (ti % 2) * 56, f"ls{ti}", 0, kb.D)
            p.op("scalar", lambda s, ti=ti, nh=nh: s.dma_start(
                out=c.ha_tok[kb.MK * ti:kb.MK * (ti + 1), :], in_=nh),
                reads=[b_nh], writes=[K.buf(c, "d_tok")], dma=True, tag=f"ls{ti}_tok")
            for bt in range(4):
                psx = c.ps_x if bt % 2 == 0 else c.ps_x2
                b_psx = b_psxs[bt % 2]
                for k8 in range(8):
                    k = bt * 8 + k8
                    p.op("tensor", lambda t, k=k, k8=k8, psx=psx, nh=nh: t.transpose(
                        psx[:, 128 * k8:128 * (k8 + 1)], nh[:, 128 * k:128 * (k + 1)],
                        c.cidb[:]),
                        reads=[b_nh, b_idb], writes=[b_psx], tag=f"ls{ti}_tr{k}")
                p.op("vector", lambda vv, bt=bt, psx=psx, fm=fm, b_psx=b_psx, b_fm=b_fm: vv.tensor_copy(
                    out=fm[:, 1024 * bt:1024 * (bt + 1)], in_=psx[:]),
                    reads=[b_psx], writes=[b_fm], tag=f"ls{ti}_te{bt}")
            p.op("scalar", lambda s, ti=ti, fm=fm: s.dma_start(
                out=c.ha_fm[ti].rearrange("(cd p) t -> p cd t", p=128),
                in_=fm.rearrange("p (cd t) -> p cd t", t=128)),
                reads=[b_fm], writes=[K.buf(c, "d_fm")], dma=True, tag=f"ls{ti}_fw")

        K.phase_barrier(c, "p2")
        ui = [0]

        def _tick():
            if ui[0] < len(units):
                emit_unit(units[ui[0]], dq_engines[ui[0] % len(dq_engines)])
                ui[0] += 1
        c.dq_tick = _tick
        K.stage_attention(c)
        c.dq_tick = None
        while ui[0] < len(units):
            emit_unit(units[ui[0]], dq_engines[ui[0] % len(dq_engines)])
            ui[0] += 1
        K.phase_barrier(c, "p3")
        c.want_debug = debug
        K.stage_lnb_ffn(c)
        for w in c.out_done:
            p.wait("gpsimd", w)
        if debug:
            K.stage_debug(c)
        stats = p.emit(nc)
    return nc, stats


_NC_CACHE = {}


def kernel(**inputs):
    from concourse.bass_utils import run_bass_kernel_spmd
    if "nc" not in _NC_CACHE:
        _NC_CACHE["nc"] = build_program()[0]
    nc = _NC_CACHE["nc"]
    in_maps = [kb.host_core_inputs(i, inputs) for i in range(kb.NCORE)]
    res = run_bass_kernel_spmd(nc, in_maps, core_ids=list(range(kb.NCORE)))
    outs = [res.results[i]["out"] for i in range(kb.NCORE)]
    return kb.out_assemble(outs).reshape(1, kb.T, kb.D)




# version 05
# speedup vs baseline: 1.9447x; 1.0550x over previous
import sys
sys.path.insert(0, "/opt/trn_rl_repo")
import numpy as np
from contextlib import ExitStack

import concourse.bass as bass
import concourse.mybir as mybir


"""Mini dependency-tracking emitter for raw Bass programs.

Records ops (closures) with buffer-level read/write sets, derives all
RAW/WAW/WAR semaphore edges (same-engine included -- the simulator's race
detector requires them), assigns semaphore increments only to ops that are
actually depended on, then emits everything inside one nc.Block().

Semaphores: one per compute engine; DMA-issuing engines use N_DMA_SLOTS
rotating semaphores per engine (the race detector allows only one
outstanding DMA per semaphore, so slot s's next DMA first waits for slot
s's previous DMA -> up to N_DMA_SLOTS DMAs in flight per queue).
"""

import os

ENGINES = ("tensor", "vector", "scalar", "gpsimd", "sync")
N_DMA_SLOTS = 8

SIM_STRICT = os.environ.get("BASS_SIM_FP64") == "1"


def _skip_wait(d, o):
    return (not SIM_STRICT) and d.sem_key == "tensor" and o.sem_key == "tensor"


class Buf:
    __slots__ = ("name", "writer", "readers")

    def __init__(self, name=""):
        self.name = name
        self.writer = None
        self.readers = []

    def __repr__(self):
        return f"Buf({self.name})"


class Op:
    __slots__ = ("engine", "emit_fn", "deps", "inc", "cum", "needs_inc", "tag",
                 "gidx", "sem_key")

    def __repr__(self):
        return f"Op({self.engine}:{self.tag}@{self.gidx})"


class Prog:
    def __init__(self):
        self.ops = {e: [] for e in ENGINES}
        self.n = 0
        self.final_waits = []
        self._dma_count = {e: 0 for e in ENGINES}
        self._last_dma_on_slot = {}

    def buf(self, name=""):
        return Buf(name)

    def op(self, engine, emit_fn, reads=(), writes=(), dma=False, deps=(), tag=""):
        assert engine in ENGINES, engine
        o = Op()
        o.engine = engine
        o.emit_fn = emit_fn
        o.inc = 16 if dma else 1
        o.tag = tag
        o.gidx = self.n
        o.needs_inc = False
        self.n += 1
        if dma:
            slot = self._dma_count[engine] % N_DMA_SLOTS
            self._dma_count[engine] += 1
            o.sem_key = (engine, slot)
        else:
            o.sem_key = engine
        dep = set(deps)
        for b in reads:
            if b.writer is not None:
                dep.add(b.writer)
        for b in writes:
            if b.writer is not None:
                dep.add(b.writer)
            dep.update(b.readers)
        if dma:
            prev = self._last_dma_on_slot.get(o.sem_key)
            if prev is not None:
                dep.add(prev)
            self._last_dma_on_slot[o.sem_key] = o
        dep.discard(o)
        o.deps = dep
        for b in reads:
            b.readers.append(o)
        for b in writes:
            b.writer = o
            b.readers = []
        self.ops[engine].append(o)
        return o

    def wait(self, engine, op):
        self.final_waits.append((engine, op))

    def emit(self, nc):
        keys = set()
        for e in ENGINES:
            for o in self.ops[e]:
                keys.add(o.sem_key)
        sems = {}
        ctxs = []
        for k in sorted(keys, key=str):
            name = f"sem_{k}" if isinstance(k, str) else f"sem_{k[0]}_{k[1]}"
            ctx = nc.semaphore(name)
            sems[k] = ctx.__enter__()
            ctxs.append(ctx)
        for e in ENGINES:
            for o in self.ops[e]:
                for d in o.deps:
                    if not _skip_wait(d, o):
                        d.needs_inc = True
        for fe, fo in self.final_waits:
            fo.needs_inc = True
        cum = {k: 0 for k in keys}
        for e in ENGINES:
            for o in self.ops[e]:
                if o.needs_inc:
                    cum[o.sem_key] += o.inc
                o.cum = cum[o.sem_key]
        stats = {e: {"ops": len(self.ops[e]),
                     "incs": sum(1 for o in self.ops[e] if o.needs_inc),
                     "waits": 0} for e in ENGINES}
        block_ctx = nc.Block()
        block = block_ctx.__enter__()
        engines_used = [e for e in ENGINES
                        if self.ops[e] or any(fe == e for fe, _ in self.final_waits)]
        for e in engines_used:
            def make_body(e):
                def body(eng):
                    waited = {k: 0 for k in keys}
                    for o in self.ops[e]:
                        need = {}
                        for d in o.deps:
                            if _skip_wait(d, o):
                                continue
                            if d.cum > need.get(d.sem_key, 0):
                                need[d.sem_key] = d.cum
                        for k in sorted(need, key=str):
                            v = need[k]
                            if v > waited[k]:
                                eng.wait_ge(sems[k], v)
                                waited[k] = v
                                stats[e]["waits"] += 1
                        inst = o.emit_fn(eng)
                        if o.needs_inc:
                            inst.then_inc(sems[o.sem_key], o.inc)
                    for fe, fo in self.final_waits:
                        if fe == e and fo.cum > waited[fo.sem_key]:
                            eng.wait_ge(sems[fo.sem_key], fo.cum)
                            waited[fo.sem_key] = fo.cum
                return body

            getattr(block, e)(make_body(e))
        block_ctx.__exit__(None, None, None)
        for ctx in reversed(ctxs):
            ctx.__exit__(None, None, None)
        return stats




T, D, H, HD, INTER = 8192, 4096, 4, 1024, 11008
NCORE = 8
CH = 256
NCH = T // CH
DCH = D // 128
MK = 128
PAD_SPAN = [16, 32, 48, 64]
NSLOT = sum(PAD_SPAN)
EPS = 1e-5
R_RUN = 4
TOK = CH * 4


def sigma_of_core(i):
    return (i, 7 - i, i, 7 - i)


def chunks_of_core(i):
    s = sigma_of_core(i)
    return [8 * q + s[q] for q in range(4)]


def slot_sched():
    return [[(q, m) for m in range(PAD_SPAN[q])] for q in range(4)]


def perm_cols(nk):
    ks = []
    nrow = nk // 8
    for R in range(0, nrow, 128):
        rows = min(128, nrow - R)
        for j in range(8):
            for r in range(rows):
                ks.append(8 * (R + r) + j)
    return np.array(ks, dtype=np.int64)


def perm_chunk_list(nk):
    out = []
    pos = 0
    nrow = nk // 8
    for R in range(0, nrow, 128):
        rows = min(128, nrow - R)
        for j in range(8):
            out.append((R, rows, j, pos))
            pos += rows
    return out


def s_rep_host(scales, nk):
    nrow = nk // 8
    idx = (8 * np.arange(nrow)) // 128
    return np.asarray(scales, dtype=np.float32)[idx]


def host_core_inputs(i, inputs):
    import ml_dtypes
    x = np.ascontiguousarray(np.asarray(inputs["L0_attn_x"], dtype=np.float32).reshape(T, D))
    chunks = chunks_of_core(i)
    x_own = np.concatenate([x[c * CH:(c + 1) * CH] for c in chunks], axis=0)
    masks = np.zeros((NSLOT, 128, CH), dtype=np.float32)
    k = 0
    for q in range(4):
        c = chunks[q]
        q_glob = c * CH + np.arange(CH)[None, :]
        for m in range(PAD_SPAN[q]):
            s_glob = m * MK + np.arange(MK)[:, None]
            masks[k] = (s_glob <= q_glob)
            k += 1
    masks = masks.astype(ml_dtypes.bfloat16)

    pd = perm_cols(INTER)
    m_ = {
        "x_full": x,
        "x_own": x_own,
        "masks": masks,
        "wmod": np.asarray(inputs["L0_wmod"], dtype=np.float32),
        "cvec": np.asarray(inputs["L0_c"], dtype=np.float32),
        "bv_q": np.asarray(inputs["L0_attn_bv_q"], dtype=np.float32),
        "bv_k": np.asarray(inputs["L0_attn_bv_k"], dtype=np.float32),
        "bv_v": np.asarray(inputs["L0_attn_bv_v"], dtype=np.float32),
        "qw_g": np.asarray(inputs["L0_ffn_gate_qweight"], dtype=np.int32),
        "qw_u": np.asarray(inputs["L0_ffn_up_qweight"], dtype=np.int32),
        "qw_d": np.asarray(inputs["L0_ffn_down_qweight"], dtype=np.int32),
        "qz_g": np.asarray(inputs["L0_ffn_gate_qzeros"], dtype=np.int32),
        "qz_u": np.asarray(inputs["L0_ffn_up_qzeros"], dtype=np.int32),
        "qz_d": np.asarray(inputs["L0_ffn_down_qzeros"], dtype=np.int32),
        "srep_g": s_rep_host(inputs["L0_ffn_gate_scales"], D)[:, pd],
        "srep_u": s_rep_host(inputs["L0_ffn_up_scales"], D)[:, pd],
        "srep_d": s_rep_host(inputs["L0_ffn_down_scales"], INTER),
        "sc_g": np.asarray(inputs["L0_ffn_gate_scales"], dtype=np.float32)[:, pd],
        "sc_u": np.asarray(inputs["L0_ffn_up_scales"], dtype=np.float32)[:, pd],
        "sc_d": np.asarray(inputs["L0_ffn_down_scales"], dtype=np.float32),
    }
    return {k2: np.ascontiguousarray(v) for k2, v in m_.items()}


def out_assemble(results, dtype=np.float32):
    full = np.empty((T, D), dtype=dtype)
    for i in range(NCORE):
        chunks = chunks_of_core(i)
        r = np.asarray(results[i]).reshape(TOK, D)
        for q, c in enumerate(chunks):
            full[c * CH:(c + 1) * CH] = r[q * CH:(q + 1) * CH]
    return full


"""Program builder for the adaLN megakernel (v1, simple-modulation design).

Arena map (f32 units per partition, arena [128, 36864]):
  P1:  XT0 0:4096 | XT1 4096:8192 | NH 8192:10240 (bf16 view) | DQ 10400:12192
       FM 12288:14336 | WMB 14336:16384 | MROW 16384:20480
  P2:  XT0 | QPR 4096:8192 | NH 8192:10240 | AMASK 10240:10368 | DQ (tail)
       KT 12288:16384 (2 slots) | VT 16384:26624 (5 slots) | X2S 26624:34816
       AT 34816:36864
  P3:  XT0 | HT 12288:28672 | WG 28672:30720 | WU 30720:32768
       SIL 32768:33792 | VST 33792:34304 | SZT 34304:34560 | VCH 34560:34816
       (down phase: DST 28672:30720, FRG 30720:31232, X2C 31232:31744,
        WDT 33792:34304)
All cross-phase reuse is protected by phase-master barrier deps.
"""

F32 = mybir.dt.float32
BF16 = mybir.dt.bfloat16
I32 = mybir.dt.int32
AF = mybir.ActivationFunctionType
OP = mybir.AluOpType
AX = mybir.AxisListType

NMODV = 6
VG = {"sa": 0, "ca": 1, "ga": 2, "sb": 3, "cb": 4, "gb": 5,
      "u": 6, "u2": 7, "qkb": 8, "vb": 9, "gavb": 10, "tmp": 11}
ARENA_F = 36864


class Ctx:
    pass


def vcol(c, name, k=0, n=None):
    base = VG[name] * DCH
    if n is None:
        n = DCH - k if k == 0 else 1
    return c.vecs[:, base + k:base + k + n]


def ar(c, off, cols, dt=F32):
    if dt == F32 or dt == I32:
        v = c.arena[:, off:off + cols]
        return v if dt == F32 else v.bitcast(I32)
    return c.arena[:, off:off + (cols + 1) // 2].bitcast(BF16)


def declare(nc, st):
    import ml_dtypes
    c = Ctx()
    c.nc = nc
    c.p = Prog()
    e = st.enter_context
    dt_ = nc.dram_tensor

    c.x_full = dt_("x_full", [T, D], F32, kind="ExternalInput").ap()
    c.x_own = dt_("x_own", [TOK, D], F32, kind="ExternalInput").ap()
    c.masks = dt_("masks", [NSLOT, MK, CH], BF16, kind="ExternalInput").ap()
    c.wmod = dt_("wmod", [D, NMODV * D], F32, kind="ExternalInput").ap()
    c.cvec = dt_("cvec", [1, D], F32, kind="ExternalInput").ap()
    c.bv = {k: dt_("bv_" + k, [H, HD], F32, kind="ExternalInput").ap() for k in "qkv"}
    c.qw = {k: dt_("qw_" + k, [{"g": D, "u": D, "d": INTER}[k] // 8,
                               {"g": INTER, "u": INTER, "d": D}[k]], I32,
                   kind="ExternalInput").ap() for k in "gud"}
    c.qz = {k: dt_("qz_" + k, [{"g": 32, "u": 32, "d": 86}[k],
                               {"g": INTER // 8, "u": INTER // 8, "d": D // 8}[k]], I32,
                   kind="ExternalInput").ap() for k in "gud"}
    c.srep = {k: dt_("srep_" + k, [{"g": D // 8, "u": D // 8, "d": INTER // 8}[k],
                                   {"g": INTER, "u": INTER, "d": D}[k]], F32,
                     kind="ExternalInput").ap() for k in "gud"}
    c.scp = {k: dt_("sc_" + k, [{"g": 32, "u": 32, "d": 86}[k],
                                {"g": INTER, "u": INTER, "d": D}[k]], F32,
                    kind="ExternalInput").ap() for k in "gud"}
    c.out = dt_("out", [TOK, D], F32, kind="ExternalOutput").ap()

    c.ha_fm = dt_("ha_fm", [T // MK, D, MK], BF16).ap()
    c.ha_tok = dt_("ha_tok", [T, D], BF16).ap()
    c.x2_dram = dt_("x2_dram", [TOK, D], F32).ap()
    c.wdq = {k: dt_("wdq_" + k, [{"g": D, "u": D, "d": INTER}[k],
                                 {"g": INTER, "u": INTER, "d": D}[k]], BF16).ap()
             for k in "gud"}
    c.v_dram = dt_("v_dram", [INTER, TOK], BF16).ap()
    c.sz_dram = {k: dt_("szd_" + k, [{"g": 32, "u": 32, "d": 86}[k],
                                     {"g": INTER, "u": INTER, "d": D}[k]], BF16).ap()
                 for k in "gud"}

    c.identf = nc.inline_tensor(np.eye(128, dtype=np.float32), "identf").ap()
    c.identb = nc.inline_tensor(np.eye(128, dtype=np.float32).astype(ml_dtypes.bfloat16), "identb").ap()
    c.ones11 = nc.inline_tensor(np.ones((1, 1), dtype=np.float32), "ones11").ap()
    c.onesf_row = nc.inline_tensor(np.ones((1, 128), dtype=np.float32), "onesf_row").ap()
    indh = np.zeros((4, 128, 32), dtype=np.float32)
    for R in range(4):
        indh[R, np.arange(128), 8 * R + np.arange(128) // 16] = 1.0
    c.indh = nc.inline_tensor(indh.astype(ml_dtypes.bfloat16).reshape(4 * 128, 32), "indh").ap()
    indv = np.zeros((11, 128, 86), dtype=np.float32)
    for R in range(11):
        rows = 128 if R < 10 else 96
        indv[R, np.arange(rows), 8 * R + np.arange(rows) // 16] = 1.0
    c.indv = nc.inline_tensor(indv.astype(ml_dtypes.bfloat16).reshape(11 * 128, 86), "indv").ap()

    c.arena = e(nc.sbuf_tensor("arena", [128, ARENA_F], F32))
    c.cidf = e(nc.sbuf_tensor("cidf", [128, 128], F32))
    c.cidb = e(nc.sbuf_tensor("cidb", [128, 128], BF16))
    c.cones = e(nc.sbuf_tensor("cones", [1, 132], F32))
    c.cindh = e(nc.sbuf_tensor("cindh", [128, 4 * 32], BF16))
    c.cindv = e(nc.sbuf_tensor("cindv", [128, 11 * 86], BF16))
    c.vecs = e(nc.sbuf_tensor("vecs", [128, 12 * DCH], F32))
    c.bcast = e(nc.sbuf_tensor("bcast", [128, 3 * D], BF16))
    c.lnst = e(nc.sbuf_tensor("lnst", [128, 128], F32))
    c.tiny = e(nc.sbuf_tensor("tiny", [128, 64], F32))
    c.ccols = e(nc.sbuf_tensor("ccols", [128, DCH], F32))
    c.bvs = e(nc.sbuf_tensor("bvs", [128, 3 * DCH], F32))
    c.hsumh = e(nc.sbuf_tensor("hsumh", [32, 1024], BF16))
    c.hsuml = e(nc.sbuf_tensor("hsuml", [32, 1024], BF16))
    c.hsumv = e(nc.sbuf_tensor("hsumv", [86, 1024], BF16))

    c.ps_a = e(nc.psum_tensor("ps_a", [128, 1024], F32))
    c.ps_s = e(nc.psum_tensor("ps_s", [128, 1024], F32))
    c.ps_t = e(nc.psum_tensor("ps_t", [128, 512], F32))
    c.ps_u = e(nc.psum_tensor("ps_u", [128, 512], F32))
    c.ps_x = e(nc.psum_tensor("ps_x", [128, 1024], BF16))
    c.ps_x2 = e(nc.psum_tensor("ps_x2", [128, 1024], BF16))

    c.B = {}
    c.pdeps = []
    return c


def buf(c, name):
    if name not in c.B:
        c.B[name] = c.p.buf(name)
    return c.B[name]


def phase_barrier(c, name):
    old = [b for n, b in c.B.items() if n.startswith("ar_")]
    mb = c.p.op("gpsimd", lambda g: g.memset(c.tiny[0:1, 63:64], 0.0),
                reads=old, writes=[buf(c, "pb_" + name)], tag="pb_" + name)
    c.pdeps = [mb]
    return mb


def stage_consts(c):
    p = c.p
    for fn, name in [
        (lambda g: g.dma_start(out=c.cidf[:], in_=c.identf), "cidf"),
        (lambda g: g.dma_start(out=c.cidb[:], in_=c.identb), "cidb"),
        (lambda g: g.dma_start(out=c.cones[0:1, 0:128], in_=c.onesf_row), "cones"),
        (lambda g: g.dma_start(out=c.cones[0:1, 128:129], in_=c.ones11), "cone1"),
        (lambda g: g.dma_start(out=c.cindh[:].rearrange("p (r g) -> p r g", r=4),
                               in_=c.indh.rearrange("(r p) g -> p r g", p=128)), "cindh"),
        (lambda g: g.dma_start(out=c.cindv[:].rearrange("p (r g) -> p r g", r=11),
                               in_=c.indv.rearrange("(r p) g -> p r g", p=128)), "cindv"),
    ]:
        p.op("gpsimd", fn, writes=[buf(c, name)], dma=True, tag="ld_" + name)


def stage_mod(c):
    p = c.p
    b_c = buf(c, "c_row")
    b_pst = buf(c, "ps_t")
    b_ccols = buf(c, "ccols")
    b_vecs = buf(c, "vecs")
    b_tiny = buf(c, "tiny")
    b_bvs = buf(c, "bvs")
    b_one = buf(c, "cone1")
    b_idf = buf(c, "cidf")
    b_onesr = buf(c, "cones")
    ones1 = c.cones[0:1, 128:129]
    xt1 = ar(c, 4096, D)

    b_xt1 = buf(c, "ar_xt1")
    p.op("sync", lambda g: g.dma_start(out=xt1[0:1, 0:D], in_=c.cvec),
         writes=[b_xt1], dma=True, tag="ld_c")
    for k in range(DCH):
        p.op("tensor", lambda t, k=k: t.matmul(
            c.ps_t[:, k:k + 1], xt1[0:1, 128 * k:128 * (k + 1)], ones1,
            start=True, stop=True),
            reads=[b_xt1, b_one], writes=[b_pst], tag=f"ccol{k}")
    p.op("scalar", lambda s: s.activation(out=c.ccols[:, 0:DCH], in_=c.ps_t[:, 0:DCH], func=AF.Silu),
         reads=[b_pst], writes=[b_ccols], tag="silu_c")

    b_wm = [buf(c, "ar_wmb0"), buf(c, "ar_wmb1")]
    b_mrow = buf(c, "ar_mrow")
    wmb2 = [ar(c, 16000, 2048), ar(c, 18048, 2048)]
    mrow = ar(c, 20096, 4096)
    c.mod_done = {}
    psacc = [(c.ps_a, 0), (c.ps_a, 512), (c.ps_s, 0), (c.ps_s, 512)]
    mv_ps = [buf(c, f"mv_ps{i}") for i in range(4)]
    for v in range(NMODV):
        name = {0: "sa", 1: "ca", 2: "ga", 3: "sb", 4: "cb", 5: "gb"}[v]
        for blk in range(2):
            col0 = v * D + blk * 2048
            for k in range(DCH):
                hb = (v * 2 * DCH + blk * DCH + k) % 2
                p.op("sync", lambda g, k=k, col0=col0, hb=hb: g.dma_start(
                    out=wmb2[hb], in_=c.wmod[128 * k:128 * (k + 1), col0:col0 + 2048]),
                    writes=[b_wm[hb]], dma=True, tag=f"wm{v}_{blk}_{k}")
                for s4 in range(4):
                    ps, off = psacc[s4]
                    p.op("tensor", lambda t, k=k, s4=s4, ps=ps, off=off, hb=hb: t.matmul(
                        ps[0:1, off:off + 512], c.ccols[:, k:k + 1],
                        wmb2[hb][:, s4 * 512:(s4 + 1) * 512],
                        start=(k == 0), stop=(k == DCH - 1)),
                        reads=[b_wm[hb], b_ccols], writes=[mv_ps[s4]], tag=f"mv{v}_{blk}_{k}_{s4}")
            for s4 in range(4):
                ps, off = psacc[s4]
                p.op("vector", lambda vv, s4=s4, ps=ps, off=off, blk=blk: vv.tensor_copy(
                    out=mrow[0:1, blk * 2048 + s4 * 512:blk * 2048 + (s4 + 1) * 512],
                    in_=ps[0:1, off:off + 512]),
                    reads=[mv_ps[s4]], writes=[b_mrow], tag=f"mvev{v}_{blk}_{s4}")
        for half16 in range(2):
            for i in range(16):
                k = half16 * 16 + i
                p.op("tensor", lambda t, k=k, i=i: t.matmul(
                    c.ps_t[:, 32 + i:33 + i], mrow[0:1, 128 * k:128 * (k + 1)], ones1,
                    start=True, stop=True),
                    reads=[b_mrow, b_one], writes=[b_pst], tag=f"mvc{v}_{k}")
            last = p.op("vector", lambda vv, name=name, half16=half16: vv.tensor_copy(
                out=c.vecs[:, VG[name] * DCH + half16 * 16:VG[name] * DCH + half16 * 16 + 16],
                in_=c.ps_t[:, 32:48]),
                reads=[b_pst], writes=[b_vecs], tag=f"mvout{v}_{half16}")
        c.mod_done[name] = last

    for bi, kk in enumerate("qkv"):
        for h in range(H):
            p.op("gpsimd", lambda g, kk=kk, h=h: g.dma_start(
                out=xt1[0:1, 0:HD], in_=c.bv[kk][h:h + 1, :]),
                writes=[b_xt1], dma=True, tag=f"ldbv{kk}{h}")
            p.op("vector", lambda vv, bi=bi, h=h: vv.reduce_sum(
                out=c.tiny[0:1, 8 + bi * 4 + h:9 + bi * 4 + h], in_=xt1[0:1, 0:HD],
                axis=AX.X, apply_absolute_value=True),
                reads=[b_xt1], writes=[b_tiny], tag=f"alpha{kk}{h}")
            p.op("vector", lambda vv: vv.tensor_scalar(
                out=xt1[0:1, 0:HD], in0=xt1[0:1, 0:HD],
                scalar1=0.0, scalar2=2.0, op0=OP.is_ge, op1=OP.mult),
                reads=[b_xt1], writes=[b_xt1], tag=f"sge{kk}{h}")
            p.op("vector", lambda vv, bi=bi, h=h: vv.tensor_scalar(
                out=xt1[0:1, 0:HD], in0=xt1[0:1, 0:HD],
                scalar1=-1.0, scalar2=c.tiny[0:1, 8 + bi * 4 + h:9 + bi * 4 + h],
                op0=OP.add, op1=OP.mult),
                reads=[b_xt1, b_tiny], writes=[b_xt1], tag=f"sgn{kk}{h}")
            p.op("vector", lambda vv: vv.tensor_scalar(
                out=xt1[0:1, 0:HD], in0=xt1[0:1, 0:HD],
                scalar1=1.0 / HD, scalar2=None, op0=OP.mult),
                reads=[b_xt1], writes=[b_xt1], tag=f"bsc{kk}{h}")
            for sl in range(8):
                p.op("tensor", lambda t, sl=sl: t.matmul(
                    c.ps_t[:, 48 + sl:49 + sl],
                    xt1[0:1, 128 * sl:128 * (sl + 1)], ones1,
                    start=True, stop=True),
                    reads=[b_xt1, b_one], writes=[b_pst], tag=f"bvc{kk}{h}{sl}")
            p.op("vector", lambda vv, bi=bi, h=h: vv.tensor_copy(
                out=c.bvs[:, bi * DCH + h * 8:bi * DCH + h * 8 + 8],
                in_=c.ps_t[:, 48:56]),
                reads=[b_pst], writes=[b_bvs], tag=f"bvce{kk}{h}")

    def vts(out, in0, s1, s2, o0, o1, deps, tag):
        if s2 is None:
            return p.op("vector", lambda vv: vv.tensor_scalar(
                out=out, in0=in0, scalar1=s1, scalar2=None, op0=o0),
                reads=[b_vecs, b_bvs], writes=[b_vecs], deps=deps, tag=tag)
        return p.op("vector", lambda vv: vv.tensor_scalar(
            out=out, in0=in0, scalar1=s1, scalar2=s2, op0=o0, op1=o1),
            reads=[b_vecs, b_bvs], writes=[b_vecs], deps=deps, tag=tag)

    def vtt(out, in0, in1, op, deps, tag):
        return p.op("vector", lambda vv: vv.tensor_tensor(out=out, in0=in0, in1=in1, op=op),
                    reads=[b_vecs, b_bvs], writes=[b_vecs], deps=deps, tag=tag)

    u_op = vts(vcol(c, "u"), vcol(c, "ca"), 1.0, None, OP.add, OP.bypass, [c.mod_done["ca"]], "mk_u")
    u2_op = vts(vcol(c, "u2"), vcol(c, "cb"), 1.0, None, OP.add, OP.bypass, [c.mod_done["cb"]], "mk_u2")
    for gn in ("ga", "gb"):
        dep = [c.mod_done[gn]]
        vts(vcol(c, "tmp"), vcol(c, gn), -1.0, None, OP.mult, None, dep, f"cl1a{gn}")
        vtt(vcol(c, "tmp"), vcol(c, "tmp"), vcol(c, gn), OP.max, [], f"cl1b{gn}")
        vts(vcol(c, "tmp"), vcol(c, "tmp"), 0.25, None, OP.max, None, [], f"cl1c{gn}")
        vts(vcol(c, gn), vcol(c, gn), 0.0, 2.0, OP.is_ge, OP.mult, dep, f"cl2{gn}")
        vts(vcol(c, gn), vcol(c, gn), -1.0, None, OP.add, OP.bypass, [], f"cl3{gn}")
        vtt(vcol(c, gn), vcol(c, gn), vcol(c, "tmp"), OP.mult, [], f"cl4{gn}")
    vtt(vcol(c, "qkb"), c.bvs[:, 0:DCH], c.bvs[:, DCH:2 * DCH], OP.mult, [], "mk_qkb")
    p.op("vector", lambda vv: vv.tensor_copy(out=vcol(c, "vb"), in_=c.bvs[:, 2 * DCH:3 * DCH]),
         reads=[b_bvs], writes=[b_vecs], tag="mk_vb")
    gavb_op = vtt(vcol(c, "gavb"), vcol(c, "ga"), vcol(c, "vb"), OP.mult, [], "mk_gavb")

    b_bc = buf(c, "bcast")
    c.bc_done = {}
    for gi, (gn, dep) in enumerate((("u", [u_op]), ("sa", [c.mod_done["sa"]]),
                                    ("gavb", [gavb_op]))):
        for k in range(DCH):
            p.op("tensor", lambda t, gn=gn, k=k: t.matmul(
                c.ps_t[0:1, 0:128], vcol(c, gn, k, 1), c.cidf[:],
                start=True, stop=True),
                reads=[b_vecs, b_idf], writes=[b_pst], deps=dep, tag=f"r{gn}{k}")
            p.op("vector", lambda vv, k=k: vv.tensor_copy(
                out=mrow[0:1, 128 * k:128 * (k + 1)], in_=c.ps_t[0:1, 0:128]),
                reads=[b_pst], writes=[b_mrow], tag=f"re{gn}{k}")
        lastb = None
        for s in range(8):
            p.op("tensor", lambda t, s=s: t.matmul(
                c.ps_t[:, 0:512], c.cones[0:1, 0:128], mrow[0:1, 512 * s:512 * (s + 1)],
                start=True, stop=True),
                reads=[b_mrow, b_onesr], writes=[b_pst], tag=f"b{gn}{s}")
            lastb = p.op("vector", lambda vv, gi=gi, s=s: vv.tensor_copy(
                out=c.bcast[:, gi * D + 512 * s:gi * D + 512 * (s + 1)], in_=c.ps_t[:, 0:512]),
                reads=[b_pst], writes=[b_bc], tag=f"be{gn}{s}")
        c.bc_done[gn] = lastb
    return c


def ln_tile(c, xsrc, xt, b_xt, nh, b_nh, stcol, tag, uoff, soff, extra_deps=()):
    p = c.p
    b_st = buf(c, "lnst")
    b_bc = buf(c, "bcast")
    ld = p.op("scalar", lambda s: s.dma_start(out=xt, in_=xsrc),
              writes=[b_xt], deps=list(extra_deps) + c.pdeps, dma=True, tag=f"{tag}_ld")
    for i in range(8):
        p.op("vector", lambda vv, i=i: vv.bn_stats(
            out=c.lnst[:, stcol + 6 * i:stcol + 6 * (i + 1)],
            in_=xt[:, 512 * i:512 * (i + 1)]),
            reads=[b_xt], writes=[b_st], tag=f"{tag}_bn{i}")
    p.op("vector", lambda vv: vv.bn_aggr(
        out=c.lnst[:, stcol + 48:stcol + 50], in_=c.lnst[:, stcol:stcol + 48]),
        reads=[b_st], writes=[b_st], tag=f"{tag}_ag")
    p.op("vector", lambda vv: vv.tensor_scalar(
        out=c.lnst[:, stcol + 53:stcol + 54], in0=c.lnst[:, stcol + 49:stcol + 50],
        scalar1=EPS, scalar2=None, op0=OP.add),
        reads=[b_st], writes=[b_st], tag=f"{tag}_ep")
    p.op("scalar", lambda s: s.activation(
        out=c.lnst[:, stcol + 50:stcol + 51], in_=c.lnst[:, stcol + 53:stcol + 54],
        func=AF.Sqrt),
        reads=[b_st], writes=[b_st], tag=f"{tag}_sq")
    p.op("vector", lambda vv: vv.reciprocal(
        out=c.lnst[:, stcol + 51:stcol + 52], in_=c.lnst[:, stcol + 50:stcol + 51]),
        reads=[b_st], writes=[b_st], tag=f"{tag}_rc")
    p.op("vector", lambda vv: vv.tensor_scalar(
        out=c.lnst[:, stcol + 52:stcol + 53], in0=c.lnst[:, stcol + 48:stcol + 49],
        scalar1=c.lnst[:, stcol + 51:stcol + 52], scalar2=-1.0, op0=OP.mult, op1=OP.mult),
        reads=[b_st], writes=[b_st], tag=f"{tag}_nb")
    p.op("vector", lambda vv: vv.tensor_scalar(
        out=nh, in0=xt, scalar1=c.lnst[:, stcol + 51:stcol + 52],
        scalar2=c.lnst[:, stcol + 52:stcol + 53], op0=OP.mult, op1=OP.add),
        reads=[b_xt, b_st], writes=[b_nh], tag=f"{tag}_nm")
    p.op("vector", lambda vv: vv.tensor_tensor(
        out=nh, in0=nh, in1=c.bcast[:, uoff:uoff + D], op=OP.mult),
        reads=[b_nh, b_bc], writes=[b_nh], tag=f"{tag}_mu")
    p.op("vector", lambda vv: vv.tensor_tensor(
        out=nh, in0=nh, in1=c.bcast[:, soff:soff + D], op=OP.add),
        reads=[b_nh, b_bc], writes=[b_nh], tag=f"{tag}_ms")
    return ld


def stage_lnspan(c):
    p = c.p
    b_idb = buf(c, "cidb")
    b_psx = buf(c, "ps_x")
    b_fm = buf(c, "ar_fm")
    fm = ar(c, 13952, 4096, BF16)
    xts = [ar(c, 0, D), ar(c, 4096, D)]
    b_xts = [buf(c, "ar_xt0"), buf(c, "ar_xt1")]
    nh = ar(c, 8192, 4096, BF16)
    b_nh = buf(c, "ar_nh")
    c.ha_done = []
    for ti in range(T // MK):
        xt, b_xt = xts[ti % 2], b_xts[ti % 2]
        dep = [c.bc_done["u"], c.bc_done["sa"]] if ti == 0 else []
        ln_tile(c, c.x_full[MK * ti:MK * (ti + 1), :], xt, b_xt, nh, b_nh,
                (ti % 2) * 56, f"ls{ti}", 0, D, extra_deps=dep)
        tok = p.op("scalar", lambda s, ti=ti: s.dma_start(
            out=c.ha_tok[MK * ti:MK * (ti + 1), :], in_=nh),
            reads=[b_nh], writes=[buf(c, "d_tok")], dma=True, tag=f"ls{ti}_tok")
        for bt in range(4):
            for k8 in range(8):
                k = bt * 8 + k8
                p.op("tensor", lambda t, k=k, k8=k8: t.transpose(
                    c.ps_x[:, 128 * k8:128 * (k8 + 1)], nh[:, 128 * k:128 * (k + 1)],
                    c.cidb[:]),
                    reads=[b_nh, b_idb], writes=[b_psx], tag=f"ls{ti}_tr{k}")
            p.op("vector", lambda vv, bt=bt: vv.tensor_copy(
                out=fm[:, 1024 * bt:1024 * (bt + 1)], in_=c.ps_x[:]),
                reads=[b_psx], writes=[b_fm], tag=f"ls{ti}_te{bt}")
        fmw = p.op("scalar", lambda s, ti=ti: s.dma_start(
            out=c.ha_fm[ti].rearrange("(cd p) t -> p cd t", p=128),
            in_=fm.rearrange("p (cd t) -> p cd t", t=128)),
            reads=[b_fm], writes=[buf(c, "d_fm")], dma=True, tag=f"ls{ti}_fw")
        c.ha_done.append((tok, fmw))


def dequant_units(c):
    p = c.p
    pk_i = ar(c, 10368, 1024, I32)
    srs = [ar(c, 11392 + 512 * s, 512) for s in range(2)]
    nbs = [ar(c, 12416 + 512 * s, 512, I32) for s in range(2)]
    ous = [ar(c, 13440 + 256 * s, 512, BF16) for s in range(2)]
    b_pk = buf(c, "ar_dqpk")
    b_srs = [buf(c, f"ar_dqsr{s}") for s in range(2)]
    b_nbs = [buf(c, f"ar_dqni{s}") for s in range(2)]
    b_ous = [buf(c, f"ar_dqou{s}") for s in range(2)]
    units = []
    for mat in "gud":
        nk = D if mat in "gu" else INTER
        nn = INTER if mat in "gu" else D
        nrow = nk // 8
        rtiles = [(R, min(128, nrow - R)) for R in range(0, nrow, 128)]
        for Ri, (R0, rows) in enumerate(rtiles):
            for blk in range((nn + 1023) // 1024):
                ncols = min(1024, nn - 1024 * blk)
                for h in range(2):
                    units.append((mat, Ri, R0, rows, blk, ncols, h))
    ucount = [0]

    def emit_unit(u, eng):
        mat, Ri, R0, rows, blk, ncols, h = u
        tg = f"dq{mat}{Ri}_{blk}{h}"
        rw = ncols // 8
        HS = 4 * rw
        ss = ucount[0] % 2
        ucount[0] += 1
        sr_f, b_sr = srs[ss], b_srs[ss]
        if h == 0:
            p.op("sync", lambda g: g.dma_start(
                out=pk_i[0:rows, 0:ncols],
                in_=c.qw[mat][R0:R0 + rows, 1024 * blk:1024 * blk + ncols]),
                writes=[b_pk], deps=c.pdeps, dma=True, tag=tg + "pk")
        p.op("sync", lambda g, HS=HS, h=h, sr_f=sr_f: g.dma_start(
            out=sr_f[0:rows, 0:HS],
            in_=c.srep[mat][R0:R0 + rows, 1024 * blk + HS * h:1024 * blk + HS * (h + 1)]),
            writes=[b_sr], deps=c.pdeps, dma=True, tag=tg + "sr")
        for j in range(8):
            js = j % 2
            nb_i, b_ni = nbs[js], b_nbs[js]
            ou_b, b_ou = ous[js], b_ous[js]
            if mat in "gu":
                p.op("vector", lambda E, j=j, h=h, rw=rw, HS=HS, ncols=ncols,
                     nb_i=nb_i: E.tensor_scalar(
                    out=nb_i[0:rows, 0:HS].rearrange(
                        "p (j2 r2) -> p r2 j2", j2=4),
                    in0=pk_i[0:rows, 0:ncols].rearrange(
                        "p (r2 j2) -> p r2 j2", j2=8)[:, :, 4 * h:4 * h + 4],
                    scalar1=4 * j, scalar2=15, op0=OP.logical_shift_right,
                    op1=OP.bitwise_and),
                    reads=[b_pk], writes=[b_ni], tag=f"{tg}_{j}x")
            else:
                p.op("vector", lambda E, j=j, h=h, HS=HS, nb_i=nb_i: E.tensor_scalar(
                    out=nb_i[0:rows, 0:HS],
                    in0=pk_i[0:rows, HS * h:HS * h + HS],
                    scalar1=4 * j, scalar2=15, op0=OP.logical_shift_right,
                    op1=OP.bitwise_and),
                    reads=[b_pk], writes=[b_ni], tag=f"{tg}_{j}x")
            p.op("vector", lambda E, HS=HS, nb_i=nb_i, ou_b=ou_b,
                 sr_f=sr_f: E.tensor_tensor(
                out=ou_b[0:rows, 0:HS], in0=nb_i[0:rows, 0:HS],
                in1=sr_f[0:rows, 0:HS], op=OP.mult),
                reads=[b_ni, b_sr], writes=[b_ou], tag=f"{tg}_{j}m")
            p.op("sync", lambda g, j=j, h=h, HS=HS, ou_b=ou_b: g.dma_start(
                out=c.wdq[mat][1024 * Ri + rows * j:1024 * Ri + rows * (j + 1),
                               1024 * blk + HS * h:1024 * blk + HS * (h + 1)],
                in_=ou_b[0:rows, 0:HS]),
                reads=[b_ou], dma=True, tag=f"{tg}_{j}w")
    return units, emit_unit


def stage_attention(c):
    p = c.p
    b_idb = buf(c, "cidb")
    b_psxs = [buf(c, "ps_xA"), buf(c, "ps_xB")]
    b_psa = buf(c, "ps_aA")
    b_pss = [buf(c, "ps_sA"), buf(c, "ps_sB")]
    b_vecs = buf(c, "vecs")
    b_bc = buf(c, "bcast")
    xt0 = ar(c, 0, D)
    b_xt0 = buf(c, "ar_xt0")
    nh = ar(c, 8192, 4096, BF16)
    b_nh = buf(c, "ar_nh")
    qpr = ar(c, 4096, 8192, BF16)
    b_qpr = buf(c, "ar_qpr")
    kts = [ar(c, 13952 + 2048 * i, 4096, BF16) for i in range(2)]
    b_kt = [buf(c, f"ar_kt{i}") for i in range(2)]
    vts = [ar(c, 18048 + 2048 * i, 4096, BF16) for i in range(4)]
    b_vt = [buf(c, f"ar_vt{i}") for i in range(4)]
    x2s = ar(c, 26240, 8192)
    b_x2s = buf(c, "ar_x2s")
    ats = [ar(c, 34432 + 128 * i, 256, BF16) for i in range(16)]
    b_at = [buf(c, f"ar_at{i}") for i in range(16)]
    msks = [ar(c, 36480 + 128 * i, 256, BF16) for i in range(2)]
    b_msk = [buf(c, f"ar_msk{i}") for i in range(2)]

    c.x2_done = []
    for q in range(4):
        for half in range(2):
            r0 = q * CH + half * 128
            ln_tile(c, c.x_own[r0:r0 + 128, :], xt0, b_xt0, nh, b_nh,
                    56 * (half % 2), f"qp{q}_{half}", 0, D,
                    extra_deps=([c.bc_done["u"], c.bc_done["sa"]] if q == 0 else []))
            for bt in range(4):
                psx = c.ps_x if bt % 2 == 0 else c.ps_x2
                b_psx = b_psxs[bt % 2]
                for k8 in range(8):
                    k = bt * 8 + k8
                    p.op("tensor", lambda t, k=k, k8=k8, psx=psx: t.transpose(
                        psx[:, 128 * k8:128 * (k8 + 1)], nh[:, 128 * k:128 * (k + 1)],
                        c.cidb[:]),
                        reads=[b_nh, b_idb], writes=[b_psx], tag=f"qp{q}{half}tr{k}")
                for k8 in range(8):
                    k = bt * 8 + k8
                    p.op("vector", lambda vv, k=k, k8=k8, half=half, psx=psx,
                         b_psx=b_psx: vv.tensor_scalar(
                        out=qpr[:, 256 * k + 128 * half:256 * k + 128 * half + 128],
                        in0=psx[:, 128 * k8:128 * (k8 + 1)],
                        scalar1=vcol(c, "qkb", k, 1), scalar2=None, op0=OP.mult),
                        reads=[b_psx, b_vecs], writes=[b_qpr], tag=f"qp{q}{half}e{k}")
        span = PAD_SPAN[q]
        first_flash = True
        for r0m in range(0, span, R_RUN):
            run = list(range(r0m, min(r0m + R_RUN, span)))
            for m in run:
                ks, bk = kts[m % 2], b_kt[m % 2]
                vs, bv = vts[m % 4], b_vt[m % 4]
                p.op("scalar", lambda s, m=m, ks=ks: s.dma_start(
                    out=ks.rearrange("p (cd t) -> p cd t", t=128),
                    in_=c.ha_fm[m].rearrange("(cd p) t -> p cd t", p=128)),
                    reads=[buf(c, "d_fm")], writes=[bk], deps=c.pdeps, tag=f"at{q}k{m}", dma=True)
                p.op("scalar", lambda s, m=m, vs=vs: s.dma_start(
                    out=vs, in_=c.ha_tok[MK * m:MK * (m + 1), :]),
                    reads=[buf(c, "d_tok")], writes=[bv], deps=c.pdeps, tag=f"at{q}v{m}", dma=True)
                mi = m % 2
                si = sum(PAD_SPAN[:q]) + m
                p.op("gpsimd", lambda g, si=si, mi=mi: g.dma_start(
                    out=msks[mi], in_=c.masks[si]),
                    writes=[b_msk[mi]], deps=c.pdeps, dma=True, tag=f"at{q}m{m}")
                if getattr(c, "dq_tick", None):
                    c.dq_tick()
                for h in range(H):
                    sb = (h % 2)
                    pso = 512 * sb
                    ati = (m % 4) * 4 + h
                    for dk in range(8):
                        kc = h * 8 + dk
                        p.op("tensor", lambda t, kc=kc, pso=pso, ks=ks: t.matmul(
                            c.ps_s[:, pso:pso + 256], ks[:, 128 * kc:128 * (kc + 1)],
                            qpr[:, 256 * kc:256 * (kc + 1)],
                            start=(kc % 8 == 0), stop=(kc % 8 == 7)),
                            reads=[bk, b_qpr], writes=[b_pss[sb]], tag=f"s{q}{m}{h}{dk}")
                    p.op("scalar", lambda s, pso=pso, ati=ati: s.activation(
                        out=ats[ati], in_=c.ps_s[:, pso:pso + 256],
                        func=AF.Sigmoid, scale=0.125),
                        reads=[b_pss[sb]], writes=[b_at[ati]], tag=f"sg{q}{m}{h}")
                    p.op("gpsimd", lambda vv, ati=ati, mi=mi: vv.tensor_tensor(
                        out=ats[ati], in0=ats[ati], in1=msks[mi], op=OP.mult),
                        reads=[b_at[ati], b_msk[mi]], writes=[b_at[ati]], tag=f"mk{q}{m}{h}")
            for h in range(H):
                for qs in range(2):
                    for m in run:
                        ati = (m % 4) * 4 + h
                        vs, bv = vts[m % 4], b_vt[m % 4]
                        for dh in range(2):
                            p.op("tensor", lambda t, ati=ati, qs=qs, dh=dh, h=h, vs=vs,
                                 st=(m == run[0]), sp=(m == run[-1]): t.matmul(
                                c.ps_a[:, 512 * dh:512 * (dh + 1)],
                                ats[ati][:, 128 * qs:128 * (qs + 1)],
                                vs[:, 1024 * h + 512 * dh:1024 * h + 512 * (dh + 1)],
                                start=st, stop=sp),
                                reads=[b_at[ati], bv], writes=[b_psa], tag=f"av{q}{m}{h}{qs}{dh}")
                    if first_flash:
                        p.op("vector", lambda vv, qs=qs, h=h: vv.tensor_copy(
                            out=x2s[:, 4096 * qs + 1024 * h:4096 * qs + 1024 * (h + 1)],
                            in_=c.ps_a[:, 0:1024]),
                            reads=[b_psa], writes=[b_x2s], deps=c.pdeps, tag=f"fl{q}{r0m}{h}{qs}")
                    else:
                        p.op("vector", lambda vv, qs=qs, h=h: vv.tensor_tensor(
                            out=x2s[:, 4096 * qs + 1024 * h:4096 * qs + 1024 * (h + 1)],
                            in0=x2s[:, 4096 * qs + 1024 * h:4096 * qs + 1024 * (h + 1)],
                            in1=c.ps_a[:, 0:1024], op=OP.add),
                            reads=[b_psa, b_x2s], writes=[b_x2s], tag=f"fl{q}{r0m}{h}{qs}")
            first_flash = False
        for qs in range(2):
            r0 = q * CH + qs * 128
            p.op("scalar", lambda s, r0=r0: s.dma_start(
                out=xt0, in_=c.x_own[r0:r0 + 128, :]),
                writes=[b_xt0], dma=True, tag=f"ev{q}{qs}ld")
            p.op("vector", lambda vv, qs=qs: vv.tensor_tensor(
                out=x2s[:, 4096 * qs:4096 * (qs + 1)],
                in0=x2s[:, 4096 * qs:4096 * (qs + 1)],
                in1=c.bcast[:, 2 * D:3 * D], op=OP.mult),
                reads=[b_x2s, b_bc], writes=[b_x2s], tag=f"ev{q}{qs}g")
            p.op("vector", lambda vv, qs=qs: vv.tensor_tensor(
                out=x2s[:, 4096 * qs:4096 * (qs + 1)],
                in0=x2s[:, 4096 * qs:4096 * (qs + 1)],
                in1=xt0, op=OP.add),
                reads=[b_x2s, b_xt0], writes=[b_x2s], tag=f"ev{q}{qs}a")
            w = p.op("gpsimd", lambda g, r0=r0, qs=qs: g.dma_start(
                out=c.x2_dram[r0:r0 + 128, :], in_=x2s[:, 4096 * qs:4096 * (qs + 1)]),
                reads=[b_x2s], writes=[buf(c, "d_x2")], dma=True, tag=f"ev{q}{qs}w")
            c.x2_done.append(w)


def stage_lnb_ffn(c):
    p = c.p
    if getattr(c, "want_debug", False):
        c.dbg_dst_ap = c.nc.dram_tensor("dbg_dst", [128, 2048], F32, kind="ExternalOutput").ap()
        c.dbg_frg_ap = c.nc.dram_tensor("dbg_frg", [128, 512], F32, kind="ExternalOutput").ap()
        c.dbg_x2c_ap = c.nc.dram_tensor("dbg_x2c", [128, 512], F32, kind="ExternalOutput").ap()
    b_idb = buf(c, "cidb")
    b_psxs = [buf(c, "ps_xA"), buf(c, "ps_xB")]
    b_vecs = buf(c, "vecs")
    b_bc = buf(c, "bcast")
    b_pst = buf(c, "ps_t")
    b_one = buf(c, "cone1")
    b_idf = buf(c, "cidf")
    b_onesr = buf(c, "cones")
    ones1 = c.cones[0:1, 128:129]

    mrow = ar(c, 0, 4096)
    b_mrow = buf(c, "ar_xt0")
    for gi, gn in enumerate(("u2", "sb", "gb")):
        for k in range(DCH):
            p.op("tensor", lambda t, gn=gn, k=k: t.matmul(
                c.ps_t[0:1, 0:128], vcol(c, gn, k, 1), c.cidf[:],
                start=True, stop=True),
                reads=[b_vecs, b_idf], writes=[b_pst], deps=c.pdeps, tag=f"r3{gn}{k}")
            p.op("vector", lambda vv, k=k: vv.tensor_copy(
                out=mrow[0:1, 128 * k:128 * (k + 1)], in_=c.ps_t[0:1, 0:128]),
                reads=[b_pst], writes=[b_mrow], deps=c.pdeps, tag=f"re3{gn}{k}")
        for s in range(8):
            p.op("tensor", lambda t, s=s: t.matmul(
                c.ps_t[:, 0:512], c.cones[0:1, 0:128], mrow[0:1, 512 * s:512 * (s + 1)],
                start=True, stop=True),
                reads=[b_mrow, b_onesr], writes=[b_pst], tag=f"b3{gn}{s}")
            p.op("vector", lambda vv, gi=gi, s=s: vv.tensor_copy(
                out=c.bcast[:, gi * D + 512 * s:gi * D + 512 * (s + 1)], in_=c.ps_t[:, 0:512]),
                reads=[b_pst], writes=[b_bc], tag=f"be3{gn}{s}")

    ht = ar(c, 10368, 32768, BF16)
    b_ht = buf(c, "ar_ht")
    xt0 = ar(c, 0, D)
    b_xt0 = buf(c, "ar_xt0")
    nh = ar(c, 8192, 4096, BF16)
    b_nh = buf(c, "ar_nh")
    pch = perm_chunk_list(D)
    for ti in range(8):
        ln_tile(c, c.x2_dram[MK * ti:MK * (ti + 1), :], xt0, b_xt0, nh, b_nh,
                56 * (ti % 2), f"lb{ti}", 0, D, extra_deps=c.x2_done)
        for bt in range(4):
            psx = c.ps_x if bt % 2 == 0 else c.ps_x2
            b_psx2 = b_psxs[bt % 2]
            for k8 in range(8):
                kc = bt * 8 + k8
                R0, rows, j, pos = pch[kc]
                p.op("tensor", lambda t, k8=k8, R0=R0, j=j, psx=psx: t.transpose(
                    psx[:, 128 * k8:128 * (k8 + 1)],
                    nh[:, 8 * R0 + j:8 * R0 + j + 1017:8],
                    c.cidb[:]),
                    reads=[b_nh, b_idb], writes=[b_psx2], tag=f"lb{ti}tr{kc}")
            p.op("vector", lambda vv, bt=bt, ti=ti, psx=psx, b_psx2=b_psx2: vv.tensor_copy(
                out=ht.rearrange("p (kc t) -> p kc t", t=1024)[
                    :, bt * 8:(bt + 1) * 8, ti * 128:(ti + 1) * 128],
                in_=psx[:].rearrange("p (k8 t) -> p k8 t", t=128)),
                reads=[b_psx2], writes=[b_ht], tag=f"lb{ti}te{bt}")

    b_psa = buf(c, "ps_aA")
    b_hsh = buf(c, "hsumh")
    b_indh = buf(c, "cindh")
    for kc in range(DCH):
        Ri = kc // 8
        for th in range(2):
            p.op("tensor", lambda t, kc=kc, Ri=Ri, th=th: t.matmul(
                c.ps_a[0:32, 512 * th:512 * (th + 1)],
                c.cindh[:, 32 * Ri:32 * (Ri + 1)],
                ht.rearrange("p (kc t) -> p kc t", t=1024)[:, kc, 512 * th:512 * (th + 1)],
                start=(kc == 0), stop=(kc == DCH - 1)),
                reads=[b_ht, b_indh], writes=[b_psa], tag=f"hsh{kc}{th}")
    p.op("vector", lambda vv: vv.tensor_copy(out=c.hsumh[:], in_=c.ps_a[0:32, 0:1024]),
         reads=[b_psa], writes=[b_hsh], tag="hsh_ev")
    b_hsl = buf(c, "hsuml")
    p.op("vector", lambda vv: vv.tensor_tensor(
        out=c.hsuml[:], in0=c.ps_a[0:32, 0:1024], in1=c.hsumh[:], op=OP.subtract),
        reads=[b_psa, b_hsh], writes=[b_hsl], tag="hsl_ev")

    wgs = [ar(c, 26752, 4096, BF16), ar(c, 32640, 4096, BF16)]
    wus = [ar(c, 28800, 4096, BF16), ar(c, 34688, 4096, BF16)]
    b_wgs = [buf(c, "ar_wgA"), buf(c, "ar_wgB")]
    b_wus = [buf(c, "ar_wuA"), buf(c, "ar_wuB")]
    sil = ar(c, 30848, 1024)
    b_sil = buf(c, "ar_sil")
    vst = ar(c, 31872, 1024, BF16)
    b_vst = buf(c, "ar_vst")
    szt = ar(c, 32384, 512, BF16)
    b_szt = buf(c, "ar_szt")
    b_pss = buf(c, "ps_sA")
    b_pstu = [buf(c, "ps_t"), buf(c, "ps_u")]
    b_hsv = buf(c, "hsumv")
    b_indv = buf(c, "cindv")
    pchI = perm_chunk_list(INTER)
    for ci, (R0, rows, j, pos) in enumerate(pchI):
        s2 = ci % 2
        wg, b_wg = wgs[s2], b_wgs[s2]
        wu, b_wu = wus[s2], b_wus[s2]
        p.op("sync", lambda g, pos=pos, rows=rows, wg=wg: g.dma_start(
            out=wg[:, 0:32 * rows].rearrange("p (kc r) -> p kc r", kc=32),
            in_=c.wdq["g"].rearrange("(kc p) n -> p kc n", p=128)[:, :, pos:pos + rows]),
            writes=[b_wg], deps=c.pdeps, dma=True, tag=f"gw{ci}")
        p.op("sync", lambda g, pos=pos, rows=rows, wu=wu: g.dma_start(
            out=wu[:, 0:32 * rows].rearrange("p (kc r) -> p kc r", kc=32),
            in_=c.wdq["u"].rearrange("(kc p) n -> p kc n", p=128)[:, :, pos:pos + rows]),
            writes=[b_wu], deps=c.pdeps, dma=True, tag=f"uw{ci}")
        p.op("gpsimd", lambda g, pos=pos, rows=rows: g.dma_start(
            out=szt[0:32, 0:rows], in_=c.sz_dram["g"][0:32, pos:pos + rows]),
            writes=[b_szt], deps=c.pdeps + [c.sz_done], dma=True, tag=f"szg{ci}")
        p.op("gpsimd", lambda g, pos=pos, rows=rows: g.dma_start(
            out=szt[0:32, rows:2 * rows], in_=c.sz_dram["u"][0:32, pos:pos + rows]),
            reads=[b_szt], writes=[b_szt], dma=True, tag=f"szu{ci}")
        for th in range(2):
            for kc in range(DCH):
                p.op("tensor", lambda t, kc=kc, th=th, rows=rows, wg=wg: t.matmul(
                    c.ps_a[0:rows, 512 * th:512 * th + 512],
                    wg[:, kc * rows:(kc + 1) * rows],
                    ht.rearrange("p (kc t) -> p kc t", t=1024)[:, kc, 512 * th:512 * th + 512],
                    start=(kc == 0), stop=False),
                    reads=[b_wg, b_ht], writes=[b_psa], tag=f"g{ci}{th}{kc}")
                p.op("tensor", lambda t, kc=kc, th=th, rows=rows, wu=wu: t.matmul(
                    c.ps_s[0:rows, 512 * th:512 * th + 512],
                    wu[:, kc * rows:(kc + 1) * rows],
                    ht.rearrange("p (kc t) -> p kc t", t=1024)[:, kc, 512 * th:512 * th + 512],
                    start=(kc == 0), stop=False),
                    reads=[b_wu, b_ht], writes=[b_pss], tag=f"u{ci}{th}{kc}")
            p.op("tensor", lambda t, th=th, rows=rows: t.matmul(
                c.ps_a[0:rows, 512 * th:512 * th + 512], szt[0:32, 0:rows],
                c.hsumh[:, 512 * th:512 * th + 512], start=False, stop=False),
                reads=[b_szt, b_hsh], writes=[b_psa], tag=f"gz{ci}{th}")
            p.op("tensor", lambda t, th=th, rows=rows: t.matmul(
                c.ps_a[0:rows, 512 * th:512 * th + 512], szt[0:32, 0:rows],
                c.hsuml[:, 512 * th:512 * th + 512], start=False, stop=True),
                reads=[b_szt, b_hsl], writes=[b_psa], tag=f"gzl{ci}{th}")
            p.op("tensor", lambda t, th=th, rows=rows: t.matmul(
                c.ps_s[0:rows, 512 * th:512 * th + 512], szt[0:32, rows:2 * rows],
                c.hsumh[:, 512 * th:512 * th + 512], start=False, stop=False),
                reads=[b_szt, b_hsh], writes=[b_pss], tag=f"uz{ci}{th}")
            p.op("tensor", lambda t, th=th, rows=rows: t.matmul(
                c.ps_s[0:rows, 512 * th:512 * th + 512], szt[0:32, rows:2 * rows],
                c.hsuml[:, 512 * th:512 * th + 512], start=False, stop=True),
                reads=[b_szt, b_hsl], writes=[b_pss], tag=f"uzl{ci}{th}")
            p.op("scalar", lambda s, th=th, rows=rows: s.activation(
                out=sil[0:rows, 512 * th:512 * th + 512],
                in_=c.ps_a[0:rows, 512 * th:512 * th + 512], func=AF.Silu),
                reads=[b_psa], writes=[b_sil], tag=f"sl{ci}{th}")
            p.op("vector", lambda vv, th=th, rows=rows: vv.tensor_tensor(
                out=vst[0:rows, 512 * th:512 * th + 512],
                in0=sil[0:rows, 512 * th:512 * th + 512],
                in1=c.ps_s[0:rows, 512 * th:512 * th + 512], op=OP.mult),
                reads=[b_sil, b_pss], writes=[b_vst], tag=f"vm{ci}{th}")
            p.op("tensor", lambda t, th=th, rows=rows, Ri=R0 // 128,
                 st=(ci == 0), sp=(ci == len(pchI) - 1): t.matmul(
                (c.ps_t if th == 0 else c.ps_u)[0:86, 0:512],
                c.cindv[:, 86 * Ri:86 * (Ri + 1)][0:rows, :],
                vst[0:rows, 512 * th:512 * th + 512],
                start=st, stop=sp),
                reads=[b_vst, b_indv], writes=[b_pstu[th]], tag=f"hv{ci}{th}")
        p.op("gpsimd", lambda g, pos=pos, rows=rows: g.dma_start(
            out=c.v_dram[pos:pos + rows, :], in_=vst[0:rows, :]),
            reads=[b_vst], writes=[buf(c, "d_v")], dma=True, tag=f"vw{ci}")
    c.ffn_gu_done = []
    for th in range(2):
        ev = p.op("vector", lambda vv, th=th: vv.tensor_copy(
            out=c.hsumv[:, 512 * th:512 * th + 512],
            in_=(c.ps_t if th == 0 else c.ps_u)[0:86, 0:512]),
            reads=[b_pstu[th]], writes=[b_hsv], tag=f"hv_ev{th}")
        c.ffn_gu_done.append(ev)

    if getattr(c, "want_debug", False):
        c.dbg_gbb_ap = c.nc.dram_tensor("dbg_gbb", [128, D], BF16, kind="ExternalOutput").ap()
        p.op("gpsimd", lambda g: g.dma_start(out=c.dbg_gbb_ap, in_=c.bcast[:, 2 * D:3 * D]),
             reads=[b_bc], writes=[buf(c, "d_dbggbb")], dma=True, tag="dbg_gbb")
    wdts = [ar(c, 10368 + 256 * s, 512, BF16) for s in range(4)]
    b_wdts = [buf(c, f"ar_wdt{s}") for s in range(4)]
    vchs = [ar(c, 11392 + 256 * s, 512, BF16) for s in range(4)]
    b_vchs = [buf(c, f"ar_vch{s}") for s in range(4)]
    szd = ar(c, 12416, 512, BF16)
    b_szd = buf(c, "ar_szd")
    dstb = ar(c, 12672, 2048, BF16)
    b_dstb = buf(c, "ar_dstb")
    frgs = [ar(c, 13696 + 512 * s, 512) for s in range(2)]
    b_frgs = [buf(c, f"ar_frg{s}") for s in range(2)]
    x2cs = [ar(c, 14720 + 512 * s, 512) for s in range(2)]
    b_x2cs = [buf(c, f"ar_x2c{s}") for s in range(2)]
    dst_ = ar(c, 15744, 2048)
    b_dst = buf(c, "ar_dst")
    psq = [(c.ps_a, 0), (c.ps_a, 512), (c.ps_s, 0), (c.ps_s, 512)]
    b_psq = [buf(c, "ps_aA"), buf(c, "ps_aB"), buf(c, "ps_sA"), buf(c, "ps_sB2")]
    b_psxH = [buf(c, "ps_xA"), buf(c, "ps_xB")]
    b_out = buf(c, "d_out")
    c.out_done = []
    for blkn in range(8):
        for th in range(2):
            p.op("gpsimd", lambda g, blkn=blkn: g.dma_start(
                out=szd[0:86, 0:512], in_=c.sz_dram["d"][0:86, 512 * blkn:512 * (blkn + 1)]),
                writes=[b_szd], deps=c.pdeps + [c.sz_done] + c.ffn_gu_done, dma=True, tag=f"dz{blkn}{th}")
            for ci, (R0, rows, j, pos) in enumerate(pchI):
                s4 = ci % 4
                wdt, b_wdt = wdts[s4], b_wdts[s4]
                vch, b_vch = vchs[s4], b_vchs[s4]
                p.op("sync", lambda g, pos=pos, rows=rows, blkn=blkn, wdt=wdt: g.dma_start(
                    out=wdt[0:rows, 0:512],
                    in_=c.wdq["d"][pos:pos + rows, 512 * blkn:512 * (blkn + 1)]),
                    writes=[b_wdt], deps=c.pdeps + c.ffn_gu_done, dma=True, tag=f"dw{blkn}{th}{ci}")
                p.op("scalar", lambda s, pos=pos, rows=rows, th=th, vch=vch: s.dma_start(
                    out=vch[0:rows, 0:512],
                    in_=c.v_dram[pos:pos + rows, 512 * th:512 * (th + 1)]),
                    reads=[buf(c, "d_v")], writes=[b_vch],
                    deps=c.pdeps + c.ffn_gu_done, dma=True, tag=f"dv{blkn}{th}{ci}")
                for sub in range(4):
                    ps, off = psq[sub]
                    p.op("tensor", lambda t, sub=sub, ps=ps, off=off, rows=rows,
                         st=(ci == 0), wdt=wdt, vch=vch: t.matmul(
                        ps[:, off:off + 512], wdt[0:rows, 128 * sub:128 * (sub + 1)],
                        vch[0:rows, 0:512], start=st, stop=False),
                        reads=[b_wdt, b_vch], writes=[b_psq[sub]], tag=f"dm{blkn}{th}{ci}{sub}")
            for sub in range(4):
                ps, off = psq[sub]
                p.op("tensor", lambda t, sub=sub, ps=ps, off=off, th=th: t.matmul(
                    ps[:, off:off + 512], szd[0:86, 128 * sub:128 * (sub + 1)],
                    c.hsumv[:, 512 * th:512 * (th + 1)], start=False, stop=True),
                    reads=[b_szd, buf(c, "hsumv")], writes=[b_psq[sub]], tag=f"dzm{blkn}{th}{sub}")
                p.op("vector", lambda vv, sub=sub, ps=ps, off=off: vv.tensor_copy(
                    out=dst_[:, 512 * sub:512 * (sub + 1)], in_=ps[:, off:off + 512]),
                    reads=[b_psq[sub]], writes=[b_dst], tag=f"de{blkn}{th}{sub}")
            if blkn == 0 and th == 0 and hasattr(c, "dbg_dst_ap"):
                p.op("gpsimd", lambda g: g.dma_start(out=c.dbg_dst_ap, in_=dst_[:, 0:2048]),
                     reads=[b_dst], writes=[buf(c, "d_dbgdst")], dma=True, tag="dbg_dst")
            for sub in range(4):
                p.op("vector", lambda vv, sub=sub: vv.tensor_copy(
                    out=dstb[:, 512 * sub:512 * (sub + 1)],
                    in_=dst_[:, 512 * sub:512 * (sub + 1)]),
                    reads=[b_dst], writes=[b_dstb], tag=f"db{blkn}{th}{sub}")
            for ts_ in range(4):
                hx = ts_ % 2
                psx = c.ps_x if hx == 0 else c.ps_x2
                frg, b_frg = frgs[hx], b_frgs[hx]
                x2c, b_x2c = x2cs[hx], b_x2cs[hx]
                for sub in range(4):
                    p.op("tensor", lambda t, ts_=ts_, sub=sub, psx=psx: t.transpose(
                        psx[:, 128 * sub:128 * (sub + 1)],
                        dstb[:, 512 * sub + 128 * ts_:512 * sub + 128 * (ts_ + 1)],
                        c.cidb[:]),
                        reads=[b_dstb, buf(c, "cidb")], writes=[b_psxH[hx]],
                        tag=f"dt{blkn}{th}{ts_}{sub}")
                p.op("scalar", lambda s, blkn=blkn, th=th, ts_=ts_, x2c=x2c: s.dma_start(
                    out=x2c[:, 0:512],
                    in_=c.x2_dram[512 * th + 128 * ts_:512 * th + 128 * (ts_ + 1),
                                  512 * blkn:512 * (blkn + 1)]),
                    reads=[buf(c, "d_x2")], writes=[b_x2c], deps=c.ffn_gu_done, dma=True, tag=f"dx{blkn}{th}{ts_}")
                p.op("vector", lambda vv, blkn=blkn, hx=hx, frg=frg, psx=psx: vv.tensor_tensor(
                    out=frg[:, 0:512], in0=psx[:, 0:512],
                    in1=c.bcast[:, 2 * D + 512 * blkn:2 * D + 512 * (blkn + 1)], op=OP.mult),
                    reads=[b_psxH[hx], b_bc], writes=[b_frg], tag=f"df{blkn}{th}{ts_}"),
                p.op("vector", lambda vv, frg=frg, x2c=x2c: vv.tensor_tensor(
                    out=frg[:, 0:512], in0=frg[:, 0:512], in1=x2c[:, 0:512], op=OP.add),
                    reads=[b_frg, b_x2c], writes=[b_frg], tag=f"dg{blkn}{th}{ts_}")

                w = p.op("gpsimd", lambda g, blkn=blkn, th=th, ts_=ts_, frg=frg: g.dma_start(
                    out=c.out[512 * th + 128 * ts_:512 * th + 128 * (ts_ + 1),
                              512 * blkn:512 * (blkn + 1)],
                    in_=frg[:, 0:512]),
                    reads=[b_frg], writes=[b_out], dma=True, tag=f"do{blkn}{th}{ts_}")
                c.out_done.append(w)


def stage_sz(c):
    p = c.p
    zld = ar(c, 24192, 1536, I32)
    b_zld = buf(c, "ar_zld")
    zpl = ar(c, 25728, 1536)
    b_zpl = buf(c, "ar_zpl")
    scb = ar(c, 27264, 1024)
    b_scb = buf(c, "ar_scb")
    szb = ar(c, 28288, 1024, BF16)
    b_szb = buf(c, "ar_szb")
    last = None
    for mat in "gud":
        G = {"g": 32, "u": 32, "d": 86}[mat]
        nn = INTER if mat in "gu" else D
        ncpk = nn // 8
        p.op("gpsimd", lambda g, mat=mat, G=G, ncpk=ncpk: g.dma_start(
            out=zld[0:G, 0:ncpk], in_=c.qz[mat]),
            writes=[b_zld], dma=True, tag=f"sz_{mat}_ld")
        nblocks = (nn + 1023) // 1024
        for blk in range(nblocks):
            ncols = min(1024, nn - 1024 * blk)
            p.op("gpsimd", lambda g, mat=mat, G=G, blk=blk, ncols=ncols: g.dma_start(
                out=scb[0:G, 0:ncols], in_=c.scp[mat][0:G, 1024 * blk:1024 * blk + ncols]),
                writes=[b_scb], dma=True, tag=f"sz_{mat}{blk}_sc")
            for pn in range(8):
                cw = ncols // 8
                cc0 = 128 * blk
                p.op("vector", lambda vv, G=G, pn=pn, cw=cw, cc0=cc0: vv.tensor_scalar(
                    out=zpl[0:G, 0:cw].bitcast(I32), in0=zld[0:G, cc0:cc0 + cw],
                    scalar1=4 * pn, scalar2=15, op0=OP.logical_shift_right,
                    op1=OP.bitwise_and),
                    reads=[b_zld], writes=[b_zpl], tag=f"sz_{mat}{blk}{pn}x")
                p.op("vector", lambda vv, G=G, cw=cw: vv.tensor_scalar(
                    out=zpl[0:G, 512:512 + cw], in0=zpl[0:G, 0:cw].bitcast(I32),
                    scalar1=1.0, scalar2=-1.0, op0=OP.add, op1=OP.mult),
                    reads=[b_zpl], writes=[b_zpl], tag=f"sz_{mat}{blk}{pn}n")
                if mat in "gu":
                    p.op("vector", lambda vv, G=G, pn=pn, cw=cw: vv.tensor_tensor(
                        out=szb[0:G, pn * cw:(pn + 1) * cw], in0=zpl[0:G, 512:512 + cw],
                        in1=scb[0:G, pn * cw:(pn + 1) * cw], op=OP.mult),
                        reads=[b_zpl, b_scb], writes=[b_szb], tag=f"sz_{mat}{blk}{pn}m")
                else:
                    p.op("vector", lambda vv, G=G, pn=pn, cw=cw: vv.tensor_tensor(
                        out=szb[0:G, pn:pn + 8 * (cw - 1) + 1:8],
                        in0=zpl[0:G, 512:512 + cw],
                        in1=scb[0:G, pn:pn + 8 * (cw - 1) + 1:8], op=OP.mult),
                        reads=[b_zpl, b_scb], writes=[b_szb], tag=f"sz_{mat}{blk}{pn}m")
            last = p.op("gpsimd", lambda g, mat=mat, G=G, blk=blk, ncols=ncols: g.dma_start(
                out=c.sz_dram[mat][0:G, 1024 * blk:1024 * blk + ncols],
                in_=szb[0:G, 0:ncols]),
                reads=[b_szb], writes=[buf(c, f"d_sz{mat}")], dma=True, tag=f"sz_{mat}{blk}w2")
    c.sz_done = last


def stage_debug(c):
    p = c.p
    nc = c.nc
    dbg = {}
    dbg["vecs"] = nc.dram_tensor("dbg_vecs", [128, 12 * DCH], F32, kind="ExternalOutput").ap()
    dbg["bcast"] = nc.dram_tensor("dbg_bcast", [128, 3 * D], BF16, kind="ExternalOutput").ap()
    dbg["ha"] = nc.dram_tensor("dbg_ha", [256, D], BF16, kind="ExternalOutput").ap()
    dbg["fm"] = nc.dram_tensor("dbg_fm", [D, MK], BF16, kind="ExternalOutput").ap()
    dbg["x2"] = nc.dram_tensor("dbg_x2", [TOK, D], F32, kind="ExternalOutput").ap()
    dbg["wg"] = nc.dram_tensor("dbg_wg", [256, 2048], BF16, kind="ExternalOutput").ap()
    dbg["szg"] = nc.dram_tensor("dbg_szg", [32, INTER], BF16, kind="ExternalOutput").ap()
    dbg["v"] = nc.dram_tensor("dbg_v", [256, TOK], BF16, kind="ExternalOutput").ap()
    dbg["vlast"] = nc.dram_tensor("dbg_vlast", [256, TOK], BF16, kind="ExternalOutput").ap()
    dbg["wd"] = nc.dram_tensor("dbg_wd", [256, 2048], BF16, kind="ExternalOutput").ap()
    dbg["hsv"] = nc.dram_tensor("dbg_hsv", [86, 1024], BF16, kind="ExternalOutput").ap()
    dbg["ht"] = nc.dram_tensor("dbg_ht", [128, 4096], BF16, kind="ExternalOutput").ap()
    c.dbg_outs = list(dbg.keys())
    ops = [
        ("vecs", lambda g: g.dma_start(out=dbg["vecs"], in_=c.vecs[:]), ["vecs"]),
        ("bcast", lambda g: g.dma_start(out=dbg["bcast"], in_=c.bcast[:]), ["bcast"]),
        ("ha", lambda g: g.dma_start(out=dbg["ha"], in_=c.ha_tok[0:256, :]), ["d_tok"]),
        ("fm", lambda g: g.dma_start(out=dbg["fm"], in_=c.ha_fm[3]), ["d_fm"]),
        ("x2", lambda g: g.dma_start(out=dbg["x2"], in_=c.x2_dram), ["d_x2"]),
        ("wg", lambda g: g.dma_start(out=dbg["wg"], in_=c.wdq["g"][0:256, 0:2048]), []),
        ("szg", lambda g: g.dma_start(out=dbg["szg"], in_=c.sz_dram["g"]), ["d_szg"]),
        ("v", lambda g: g.dma_start(out=dbg["v"], in_=c.v_dram[0:256, :]), ["d_v"]),
        ("vlast", lambda g: g.dma_start(out=dbg["vlast"], in_=c.v_dram[10752:11008, :]), ["d_v"]),
        ("wd", lambda g: g.dma_start(out=dbg["wd"], in_=c.wdq["d"][0:256, 0:2048]), []),
        ("hsv", lambda g: g.dma_start(out=dbg["hsv"], in_=c.hsumv[:]), ["hsumv"]),
        ("ht", lambda g: g.dma_start(out=dbg["ht"], in_=ar(c, 10368, 8192, BF16)[:, 0:4096]), ["ar_ht"]),
    ]
    for name, fn, rd in ops:
        w = p.op("gpsimd", fn, reads=[buf(c, n) for n in rd],
                 deps=c.out_done[-1:], dma=True, tag="dbg_" + name)
        p.wait("gpsimd", w)


class _SelfModule:
    def __getattr__(self, n):
        return globals()[n]


kb = K = _SelfModule()

def build_program(dq_engines=("vector",), debug=False):
    nc = bass.Bass("TRN2", target_bir_lowering=False)
    st = ExitStack()
    with st:
        c = K.declare(nc, st)
        K.stage_consts(c)
        K.stage_mod(c)
        K.stage_sz(c)
        units, emit_unit = K.dequant_units(c)
        p = c.p

        b_idb = K.buf(c, "cidb")
        b_psxs = [K.buf(c, "ps_xA"), K.buf(c, "ps_xB")]
        b_fms = [K.buf(c, "ar_fm"), K.buf(c, "ar_fm2")]
        fms = [K.ar(c, 29312, 4096, K.BF16), K.ar(c, 31360, 4096, K.BF16)]
        xts = [K.ar(c, 0, K.D), K.ar(c, 4096, K.D)]
        b_xts = [K.buf(c, "ar_xt0"), K.buf(c, "ar_xt1")]
        nhs = [K.ar(c, 8192, 4096, K.BF16), K.ar(c, 10368, 4096, K.BF16)]
        b_nhs = [K.buf(c, "ar_nh"), K.buf(c, "ar_nh2")]
        for ti in range(kb.T // kb.MK):
            xt, b_xt = xts[ti % 2], b_xts[ti % 2]
            nh, b_nh = nhs[ti % 2], b_nhs[ti % 2]
            fm, b_fm = fms[ti % 2], b_fms[ti % 2]
            K.ln_tile(c, c.x_full[kb.MK * ti:kb.MK * (ti + 1), :], xt, b_xt, nh, b_nh,
                      (ti % 2) * 56, f"ls{ti}", 0, kb.D)
            p.op("scalar", lambda s, ti=ti, nh=nh: s.dma_start(
                out=c.ha_tok[kb.MK * ti:kb.MK * (ti + 1), :], in_=nh),
                reads=[b_nh], writes=[K.buf(c, "d_tok")], dma=True, tag=f"ls{ti}_tok")
            for bt in range(4):
                psx = c.ps_x if bt % 2 == 0 else c.ps_x2
                b_psx = b_psxs[bt % 2]
                for k8 in range(8):
                    k = bt * 8 + k8
                    p.op("tensor", lambda t, k=k, k8=k8, psx=psx, nh=nh: t.transpose(
                        psx[:, 128 * k8:128 * (k8 + 1)], nh[:, 128 * k:128 * (k + 1)],
                        c.cidb[:]),
                        reads=[b_nh, b_idb], writes=[b_psx], tag=f"ls{ti}_tr{k}")
                p.op("vector", lambda vv, bt=bt, psx=psx, fm=fm, b_psx=b_psx, b_fm=b_fm: vv.tensor_copy(
                    out=fm[:, 1024 * bt:1024 * (bt + 1)], in_=psx[:]),
                    reads=[b_psx], writes=[b_fm], tag=f"ls{ti}_te{bt}")
            p.op("scalar", lambda s, ti=ti, fm=fm: s.dma_start(
                out=c.ha_fm[ti].rearrange("(cd p) t -> p cd t", p=128),
                in_=fm.rearrange("p (cd t) -> p cd t", t=128)),
                reads=[b_fm], writes=[K.buf(c, "d_fm")], dma=True, tag=f"ls{ti}_fw")

        K.phase_barrier(c, "p2")
        ui = [0]

        def _tick():
            for _ in range(2):
                if ui[0] < len(units):
                    emit_unit(units[ui[0]], dq_engines[ui[0] % len(dq_engines)])
                    ui[0] += 1
        c.dq_tick = _tick
        K.stage_attention(c)
        c.dq_tick = None
        while ui[0] < len(units):
            emit_unit(units[ui[0]], dq_engines[ui[0] % len(dq_engines)])
            ui[0] += 1
        K.phase_barrier(c, "p3")
        c.want_debug = debug
        K.stage_lnb_ffn(c)
        for w in c.out_done:
            p.wait("gpsimd", w)
        if debug:
            K.stage_debug(c)
        stats = p.emit(nc)
    return nc, stats


_NC_CACHE = {}


def kernel(**inputs):
    from concourse.bass_utils import run_bass_kernel_spmd
    if "nc" not in _NC_CACHE:
        _NC_CACHE["nc"] = build_program()[0]
    nc = _NC_CACHE["nc"]
    in_maps = [kb.host_core_inputs(i, inputs) for i in range(kb.NCORE)]
    res = run_bass_kernel_spmd(nc, in_maps, core_ids=list(range(kb.NCORE)))
    outs = [res.results[i]["out"] for i in range(kb.NCORE)]
    return kb.out_assemble(outs).reshape(1, kb.T, kb.D)


